# version 02
# baseline (speedup 1.0000x reference)
import math
import numpy as np
import ml_dtypes
import concourse.bass as bass
import concourse.mybir as mybir
from concourse.bass_utils import run_bass_kernel_spmd

F32 = mybir.dt.float32
BF16 = mybir.dt.bfloat16
I32 = mybir.dt.int32
AF = mybir.ActivationFunctionType
ALU = mybir.AluOpType
AXX = mybir.AxisListType.X
BF = ml_dtypes.bfloat16


class Cfg:
    def __init__(self, DM=4096, S=2048, H=32, HKV=8, NCL=64):
        self.B = 4
        self.DM = DM
        self.S = S
        self.H = H
        self.HKV = HKV
        self.G = H // HKV
        self.HD = 128
        self.NCL = NCL
        self.F = DM // 128
        self.QF = H * self.HD
        self.KF = HKV * self.HD
        self.NT = S // 256
        self.TPC = self.NT * 128
        self.ST = S // 128
        start = 2 ** (-(2 ** (-(math.log2(H) - 3))))
        self.slopes = [start * start ** i for i in range(H)]
        self.EPS_LN = 1e-5
        self.EPS_RMS = 1e-6
        self.TEMP = 0.1
        self.THRESH = 0.01
        self.NEGBIG = 3.0e32


ENGS = ("pe", "dve", "act", "pool", "sp")


class Buf:
    __slots__ = ("w", "rs", "ws")

    def __init__(self):
        self.w = None
        self.rs = []
        self.ws = {}


class Prog:
    def __init__(self, nc):
        self.nc = nc
        self.ops = {e: [] for e in ENGS}
        self.cnt = {e: 0 for e in ENGS}
        self.esem = {e: nc.alloc_semaphore(f"es_{e}") for e in ENGS}
        self.pe_sem_name = self.esem["pe"].name
        self.wm = {e: {} for e in ENGS}
        self.dsem = {}

    def _waits(self, e, deps):
        best = {}
        for d in deps:
            if d is None:
                continue
            sem, val = d
            k = sem.name
            if e == "pe" and k == self.pe_sem_name:
                continue
            if k not in best or val > best[k][1]:
                best[k] = (sem, val)
        waits = []
        for k, (sem, val) in sorted(best.items()):
            if self.wm[e].get(k, 0) >= val:
                continue
            waits.append((sem, val))
            self.wm[e][k] = val
        return waits

    def _deps(self, r, w, wa, deps):
        out = list(deps)
        for b in r:
            if b.w is not None:
                out.append(b.w)
            out.extend(b.ws.values())
        for b in w:
            if b.w is not None:
                out.append(b.w)
            out.extend(b.ws.values())
            out.extend(b.rs)
        for b in wa:
            pass
        return out

    def _book(self, tok, r, w, wa):
        for b in r:
            b.rs.append(tok)
        for b in w:
            b.w = tok
            b.rs = []
        for b in wa:
            b.ws[tok[0].name] = tok
        return tok

    def do(self, e, fn, r=(), w=(), wa=(), deps=()):
        waits = self._waits(e, self._deps(r, w, wa, deps))
        self.cnt[e] += 1
        tok = (self.esem[e], self.cnt[e])
        self.ops[e].append((waits, fn, self.esem[e], 1))
        return self._book(tok, r, w, wa)

    def dma(self, q, fn, stream, r=(), w=(), wa=(), deps=(), inc=16):
        waits = self._waits(q, self._deps(r, w, wa, deps))
        if stream not in self.dsem:
            self.dsem[stream] = [self.nc.alloc_semaphore("ds_" + stream), 0]
        ent = self.dsem[stream]
        ent[1] += inc
        tok = (ent[0], ent[1])
        self.ops[q].append((waits, fn, ent[0], inc))
        return self._book(tok, r, w, wa)

    def cc(self, fn, stream, r=(), w=(), deps=()):
        return self.dma("pool", fn, stream, r=r, w=w, deps=deps, inc=1)

    def wait_only(self, e, deps):
        waits = self._waits(e, [d for d in deps if d is not None])
        if waits:
            self.ops[e].append((waits, None, None, 0))

    def emit(self):
        nc = self.nc

        def mk(e):
            def body(eng):
                for waits, fn, sem, inc in self.ops[e]:
                    for (s, v) in waits:
                        eng.wait_ge(s, v)
                    if fn is not None:
                        inst = fn(eng)
                        if inst is not None and sem is not None:
                            inst.then_inc(sem, inc)
            return body

        with nc.Block() as block:
            block.sync(mk("sp"))
            block.gpsimd(mk("pool"))
            block.tensor(mk("pe"))
            block.vector(mk("dve"))
            block.scalar(mk("act"))


def build_program(cfg: Cfg, stop_phase=99):
    nc = bass.Bass("TRN2", target_bir_lowering=False)
    DM, S, H, HKV, G, HD = cfg.DM, cfg.S, cfg.H, cfg.HKV, cfg.G, cfg.HD
    F, QF, KF, NT, NCL, ST = cfg.F, cfg.QF, cfg.KF, cfg.NT, cfg.NCL, cfg.ST
    TPC = cfg.TPC
    HC = QF // 128
    KC = KF // 128
    QCG, QW = (QF // 512, 512) if QF >= 512 else (1, QF)
    KCG, KW = (KF // 512, 512) if KF >= 512 else (1, KF)
    DCG, DW = (DM // 512, 512) if DM >= 512 else (1, DM)
    NSL = 8 if F % 8 == 0 else F
    NTK = 2 * NT
    BIGA = HKV * S
    BIGC = max(2 * ST * KF, F * TPC)
    PCNT = min(NT, 8)

    d_x = nc.dram_tensor("xt", [TPC, DM], F32, kind="ExternalInput").ap()
    d_c2 = nc.dram_tensor("c2in", [128, NT * (NTK + 1)], F32, kind="ExternalInput").ap()
    d_wq = nc.dram_tensor("wqT", [DM, QF], F32, kind="ExternalInput").ap()
    d_wk = nc.dram_tensor("wkT", [DM, KF], F32, kind="ExternalInput").ap()
    d_wv = nc.dram_tensor("wvT", [DM, KF], F32, kind="ExternalInput").ap()
    d_wo = nc.dram_tensor("woT", [QF, DM], F32, kind="ExternalInput").ap()
    d_wf = nc.dram_tensor("wfT", [DM, DM], F32, kind="ExternalInput").ap()
    d_ct = nc.dram_tensor("ctT", [DM, NCL], F32, kind="ExternalInput").ap()
    d_bias = nc.dram_tensor("fbias", [1, DM], F32, kind="ExternalInput").ap()
    d_qw = nc.dram_tensor("qnw", [1, HD], F32, kind="ExternalInput").ap()
    d_kw = nc.dram_tensor("knw", [1, HD], F32, kind="ExternalInput").ap()
    d_asn = nc.dram_tensor("asn", [1, DM], I32, kind="ExternalInput").ap()
    d_out = nc.dram_tensor("out", [TPC, DM], F32, kind="ExternalOutput").ap()

    d_xnT = nc.dram_tensor("xnT_s", [128, F * TPC], BF16).ap()
    d_qh = nc.dram_tensor("qh_s", [TPC, QF], BF16).ap()
    d_ot = nc.dram_tensor("ot_s", [HC, NT, 128, 128], BF16).ap()
    d_exi = nc.dram_tensor("exi_s", [TPC, 2 * KF], BF16)
    d_exo = nc.dram_tensor("exo_s", [2 * TPC, 2 * KF], BF16)
    d_cli = nc.dram_tensor("cli_s", [NCL, 1], F32)
    d_clo = nc.dram_tensor("clo_s", [NCL, 1], F32)

    ident_bf = nc.inline_tensor(np.eye(128, dtype=np.float32).astype(BF), "c_ident").ap()
    ident_f32 = nc.inline_tensor(np.eye(128, dtype=np.float32), "c_identf").ap()
    dnp = (np.arange(128)[None, :] - np.arange(128)[:, None]).astype(np.float32)
    d_D = nc.inline_tensor(dnp, "c_D").ap()
    ones_bf = nc.inline_tensor(np.ones((1, 128), dtype=np.float32).astype(BF), "c_ones").ap()
    iota_cl = nc.inline_tensor(np.arange(NCL, dtype=np.float32)[:, None], "c_iota").ap()

    A = nc.alloc_sbuf_tensor
    big = A("bigb", [128, BIGC], BF16)
    xa = A("xa", [128, max(DM, NT * DW)], F32)
    xb = A("xb", [128, max(DM, NT * DW)], F32)
    wsl = [A(f"wsl{i}", [128, NSL * 512], BF16) for i in range(2)]
    qs = [A(f"qs{i}", [128, max(QF, DM, TPC)], BF16) for i in range(2)]
    qt = A("qt", [128, max(QF, DM)], BF16)
    tr = A("tr", [128, max(NTK * 128, F * NCL)], F32)
    sc1 = A("sc1", [128, max(DM, 2 * NTK * 128)], BF16)
    sc2 = A("sc2", [128, max(DM, QF) + 512], BF16)
    gg = A("gg", [128, NTK * 128], F32)
    kvst = [A(f"kvst{i}", [128, 2 * KF], BF16) for i in range(2)]
    stg4 = [A(f"stg{i}", [128, 512], BF16) for i in range(4)]
    rsb = A("rsb", [128, DM], BF16)
    CLF = F * NCL
    assert 2 * CLF <= DM
    outst = [A(f"outst{i}", [128, DW], F32) for i in range(2)]
    yf = A("yf", [128, 2 * DW], F32)
    wbck = A("wbck", [128, 128], BF16)
    smc = A("smc", [128, 16], F32)
    qselc = A("qselc", [128, NT], F32)
    invq = A("invq", [128, 4 * H], F32)
    invk = A("invk", [128, 4 * HKV], F32)
    clm = A("clm", [NCL, 1], F32)
    clsel = A("clsel", [NCL, 1], BF16)
    rowsb = A("rowsb", [1, 4 * 512], BF16)
    qwr = A("qwr", [1, 2 * HD], F32)
    qwrb = A("qwrb", [1, 2 * HD], BF16)
    ic = A("ic", [128, 128], BF16)
    icf = A("icf", [128, 128], F32)
    dtt = A("dtt", [128, 128], F32)
    onr = A("onr", [1, 128], BF16)
    iotc = A("iotc", [NCL, 1], F32)
    c2s = A("c2s", [128, NT * (NTK + 1)], F32)
    asn_i = A("asni", [1, 512], I32)

    PP = [nc.alloc_psum_tensor(f"pp{i}", [128, 512], F32) for i in range(8)]

    P = Prog(nc)
    bBA, bBB = Buf(), Buf()
    bXa, bXb = Buf(), Buf()
    bW = [Buf() for _ in wsl]
    bQs = [Buf(), Buf()]
    bQt = Buf()
    bTr = Buf()
    bS1, bS2 = Buf(), Buf()
    bOb = [Buf(), Buf()]
    bOt2 = [Buf(), Buf()]
    bGg = Buf()
    bKv = [Buf(), Buf()]
    bStg = [Buf() for _ in stg4]
    bRsb = Buf()
    bOut = [Buf(), Buf()]
    bYf = Buf()
    bWbk = Buf()
    bSmc = Buf()
    bQsel = Buf()
    bInvq, bInvk = Buf(), Buf()
    bClm, bClsel = Buf(), Buf()
    bRowsb = Buf()
    bQwr, bQwrb = Buf(), Buf()
    bConst = Buf()
    bC2 = Buf()
    bAsn = Buf()
    bP = [Buf() for _ in PP]
    bDxnT, bDqh, bDot, bDexi, bDexo, bDcli, bDclo, bDout = (Buf() for _ in range(8))

    def xn_buf(c):
        return bBA if c < F // 2 else bBB

    def MM(out_ap, lhsT_ap, rhs_ap, start, stop):
        def fn(e, a=out_ap, b=lhsT_ap, c=rhs_ap, s1=start, s2=stop):
            return e.matmul(a, b, c, start=s1, stop=s2)
        return fn

    P.dma("pool", lambda e: e.dma_start(ic[:, :], ident_bf), "cst1", w=[bConst])
    P.dma("pool", lambda e: e.dma_start(icf[:, :], ident_f32), "cst2", w=[bConst])
    P.dma("pool", lambda e: e.dma_start(dtt[:, :], d_D), "cst3", w=[bConst])
    P.dma("pool", lambda e: e.dma_start(onr[:, :], ones_bf), "cst4", w=[bConst])
    P.dma("pool", lambda e: e.dma_start(iotc[:, :], iota_cl), "cst5", w=[bConst])
    P.dma("pool", lambda e: e.dma_start(c2s[:, :], d_c2), "cst6", w=[bC2])
    P.dma("pool", lambda e: e.dma_start(qwr[:, 0:HD], d_qw), "cst8", w=[bQwr])
    P.dma("pool", lambda e: e.dma_start(qwr[:, HD:2 * HD], d_kw), "cst9", w=[bQwr])
    P.dma("pool", lambda e: e.dma_start(
        tr[:, 0:F * NCL].rearrange("p (c n) -> p c n", c=F),
        d_ct.rearrange("(c p) n -> p c n", p=128)), "cst11", w=[bTr])

    P.do("act", lambda e: e.activation(rsb[:, 0:CLF], tr[:, 0:F * NCL], AF.Copy),
         r=[bTr], w=[bRsb])
    P.do("dve", lambda e: e.tensor_tensor(rsb[:, CLF:2 * CLF], tr[:, 0:F * NCL], rsb[:, 0:CLF],
                                          ALU.subtract), r=[bTr, bRsb], w=[bRsb])
    P.do("dve", lambda e: e.tensor_copy(qwrb[:, HD:2 * HD], qwr[:, HD:2 * HD]),
         r=[bQwr], w=[bQwrb])
    P.do("pe", MM(PP[7][:, 0:128], onr[:, :], qwrb[:, HD:2 * HD], True, True),
         r=[bConst, bQwrb], w=[bP[7]])
    P.do("act", lambda e: e.activation(wbck[:, :], PP[7][:, 0:128], AF.Copy), r=[bP[7]], w=[bWbk])

    def _finish0():
        toks = []
        for name, (sem, val) in P.dsem.items():
            if val:
                toks.append((sem, val))
        P.wait_only("sp", toks)
        P.emit()
        return nc

    if stop_phase <= 0:
        return _finish0()
    NT1 = NT if stop_phase > 90 or stop_phase >= 1.5 else NT
    import os
    SUB = int(os.environ.get("KSUB", "99"))
    for m in range(NT if SUB >= 10 else 1):
        P.dma("sp", lambda e, m=m: e.dma_start(xa[:, 0:DM], d_x[m * 128:(m + 1) * 128, :]),
              "x0", w=[bXa])
        P.do("dve", lambda e: e.reduce_sum(smc[:, 0:1], xa[:, 0:DM], AXX), r=[bXa], w=[bSmc])
        P.do("dve", lambda e: e.tensor_scalar(smc[:, 1:2], smc[:, 0:1], 1.0 / DM, None, ALU.mult),
             r=[bSmc], w=[bSmc])
        P.do("act", lambda e: e.activation(sc2[:, 0:DM], xa[:, 0:DM], AF.Square,
                                           bias=smc[:, 1:2], scale=-1.0, accum_out=smc[:, 2:3]),
             r=[bXa, bSmc], w=[bS2, bSmc])
        P.do("dve", lambda e: e.tensor_scalar(smc[:, 3:4], smc[:, 2:3], 1.0 / DM, cfg.EPS_LN,
                                              ALU.mult, ALU.add), r=[bSmc], w=[bSmc])
        P.do("act", lambda e: e.activation(smc[:, 4:5], smc[:, 3:4], AF.Ln), r=[bSmc], w=[bSmc])
        P.do("act", lambda e: e.activation(smc[:, 5:6], smc[:, 4:5], AF.Exp, scale=-0.5),
             r=[bSmc], w=[bSmc])
        P.do("dve", lambda e: e.tensor_scalar(xb[:, 0:DM], xa[:, 0:DM], smc[:, 1:2], smc[:, 5:6],
                                              ALU.subtract, ALU.mult), r=[bXa, bSmc], w=[bXb])
        P.do("act", lambda e: e.activation(sc2[:, 0:DM], xb[:, 0:DM], AF.Copy), r=[bXb], w=[bS2])
        P.do("dve", lambda e: e.tensor_tensor(sc1[:, 0:DM], xb[:, 0:DM], sc2[:, 0:DM],
                                              ALU.subtract), r=[bXb, bS2], w=[bS1])
        if SUB < 2:
            continue
        for cb in range(0, F, 4):
            nb = min(4, F - cb)
            pih = (cb // 4) % 2
            pil = 2 + (cb // 4) % 2

            def fn_hi(e, cb=cb, nb=nb, pih=pih):
                last = None
                for i in range(nb):
                    last = e.matmul(PP[pih][:, i * 128:(i + 1) * 128],
                                    sc2[:, (cb + i) * 128:(cb + i + 1) * 128], ic[:, :],
                                    start=True, stop=True)
                return last
            P.do("pe", fn_hi, r=[bS2, bConst], w=[bP[pih]])

            def fn_lo(e, cb=cb, nb=nb, pil=pil):
                last = None
                for i in range(nb):
                    last = e.matmul(PP[pil][:, i * 128:(i + 1) * 128],
                                    sc1[:, (cb + i) * 128:(cb + i + 1) * 128], ic[:, :],
                                    start=True, stop=True)
                return last
            P.do("pe", fn_lo, r=[bS1, bConst], w=[bP[pil]])
            for i in range(nb):
                c = cb + i
                P.do("act", lambda e, c=c, i=i, pih=pih, m=m:
                     e.activation(big[:, c * TPC + m * 128:c * TPC + (m + 1) * 128],
                                  PP[pih][:, i * 128:(i + 1) * 128], AF.Copy),
                     r=[bP[pih]], w=[xn_buf(c)])
            P.do("act", lambda e, cb=cb, nb=nb, pil=pil:
                 e.activation(qs[0][:, cb * 128:(cb + nb) * 128], PP[pil][:, 0:nb * 128], AF.Copy),
                 r=[bP[pil]], w=[bQs[0]])

        if SUB < 3:
            continue

        def fn_cent(e, m=m):
            last = None
            for c in range(F):
                hiT = big[:, c * TPC + m * 128:c * TPC + (m + 1) * 128]
                loT = qs[0][:, c * 128:(c + 1) * 128]
                last = e.matmul(PP[7][:, 0:NCL], hiT, rsb[:, c * NCL:(c + 1) * NCL],
                                start=(c == 0), stop=False)
                last = e.matmul(PP[7][:, 0:NCL], hiT, rsb[:, CLF + c * NCL:CLF + (c + 1) * NCL],
                                start=False, stop=False)
                last = e.matmul(PP[7][:, 0:NCL], loT, rsb[:, c * NCL:(c + 1) * NCL],
                                start=False, stop=(c == F - 1))
            return last
        P.do("pe", fn_cent, r=[bBA, bBB, bQs[0], bRsb], w=[bP[7]])
        if SUB < 4:
            continue
        SUB2 = int(os.environ.get("KSUB2", "99"))
        P.do("dve", lambda e: e.reduce_max(smc[:, 6:7], PP[7][:, 0:NCL], AXX, negate=True),
             r=[bP[7]], w=[bSmc])
        if SUB2 < 1:
            continue
        P.do("dve", lambda e: e.tensor_scalar(smc[:, 7:8], smc[:, 6:7], 1.0 / cfg.TEMP, None,
                                              ALU.mult), r=[bSmc], w=[bSmc])
        if SUB2 < 2:
            continue
        EXPV = int(os.environ.get("KEXPV", "0"))
        if EXPV == 1:
            P.do("act", lambda e: e.activation(tr[:, 0:NCL], PP[7][:, 0:NCL], AF.Exp,
                                               bias=smc[:, 7:8], scale=1.0 / cfg.TEMP),
                 r=[bP[7], bSmc], w=[bTr, bSmc])
        elif EXPV == 2:
            P.do("act", lambda e: e.activation(tr[:, 0:NCL], PP[7][:, 0:NCL], AF.Exp,
                                               scale=1.0 / cfg.TEMP, accum_out=smc[:, 8:9]),
                 r=[bP[7], bSmc], w=[bTr, bSmc])
        elif EXPV == 3:
            P.do("dve", lambda e: e.tensor_copy(xb[:, 0:NCL], PP[7][:, 0:NCL]), r=[bP[7]], w=[bXb])
            P.do("act", lambda e: e.activation(tr[:, 0:NCL], xb[:, 0:NCL], AF.Exp,
                                               bias=smc[:, 7:8], scale=1.0 / cfg.TEMP,
                                               accum_out=smc[:, 8:9]),
                 r=[bXb, bSmc], w=[bTr, bSmc])
        elif EXPV == 5:
            P.do("act", lambda e: e.activation(smc[:, 8:9], smc[:, 7:8], AF.Copy),
                 r=[bSmc], w=[bSmc])
        elif EXPV == 6:
            P.do("act", lambda e: e.activation(sc1[:, 0:NCL], PP[7][:, 0:NCL], AF.Exp,
                                               bias=smc[:, 7:8], scale=1.0 / cfg.TEMP,
                                               accum_out=smc[:, 8:9]),
                 r=[bP[7], bSmc], w=[bS1, bSmc])
        elif EXPV == 4:
            P.do("act", lambda e: e.activation(tr[:, 0:NCL], PP[7][:, 0:NCL], AF.Copy,
                                               bias=smc[:, 7:8], scale=1.0 / cfg.TEMP,
                                               accum_out=smc[:, 8:9]),
                 r=[bP[7], bSmc], w=[bTr, bSmc])
        else:
            P.do("act", lambda e: e.activation(tr[:, 0:NCL], PP[7][:, 0:NCL], AF.Exp,
                                               bias=smc[:, 7:8], scale=1.0 / cfg.TEMP,
                                               accum_out=smc[:, 8:9]),
                 r=[bP[7], bSmc], w=[bTr, bSmc])
        if SUB2 < 3:
            continue
        P.do("dve", lambda e, m=m: e.tensor_scalar(qselc[:, m:m + 1], smc[:, 8:9],
                                                   1.0 / cfg.THRESH, None, ALU.is_lt),
             r=[bSmc], w=[bQsel])
        if SUB2 < 4:
            continue
        P.do("dve", lambda e: e.reciprocal(smc[:, 9:10], smc[:, 8:9]), r=[bSmc], w=[bSmc])
        if SUB2 < 5:
            continue
        P.do("dve", lambda e: e.tensor_scalar(tr[:, 0:NCL], tr[:, 0:NCL], smc[:, 9:10], None,
                                              ALU.mult), r=[bSmc], w=[bTr])
        if SUB < 5:
            continue
        P.do("pe", lambda e: e.transpose(PP[6][0:NCL, 0:128], tr[:, 0:NCL], icf[:, :]),
             r=[bTr, bConst], w=[bP[6]])
        P.do("dve", lambda e: e.reduce_max(smc[0:NCL, 10:11], PP[6][0:NCL, 0:128], AXX),
             r=[bP[6]], w=[bSmc])
        if m == 0:
            P.do("dve", lambda e: e.tensor_copy(clm[:, :], smc[0:NCL, 10:11]), r=[bSmc], w=[bClm])
        else:
            P.do("dve", lambda e: e.tensor_tensor(clm[:, :], clm[:, :], smc[0:NCL, 10:11],
                                                  ALU.max), r=[bSmc, bClm], w=[bClm])

    for i in range(4):
        Wc = F * TPC // 4
        P.dma("sp", lambda e, i=i, Wc=Wc: e.dma_start(d_xnT[:, i * Wc:(i + 1) * Wc],
                                                      big[:, i * Wc:(i + 1) * Wc]),
              f"xnw{i}", r=[bBA, bBB], wa=[bDxnT])
    P.dma("sp", lambda e: e.dma_start(d_cli.ap(), clm[:, :]), "clw", r=[bClm], wa=[bDcli])

    slab_ctr = [0]

    def load_slab(dram_ap, c0, nchunks, col0, width, stream):
        si = slab_ctr[0] % len(wsl)
        slab_ctr[0] += 1
        t, b = wsl[si], bW[si]
        stream = f"wsl{si}"
        src = dram_ap[c0 * 128:(c0 + nchunks) * 128, col0:col0 + width]
        P.dma("pool", lambda e, t=t, src=src, nchunks=nchunks, width=width:
              e.dma_start(t[:, 0:nchunks * width].rearrange("p (c n) -> p c n", c=nchunks),
                          src.rearrange("(c p) n -> p c n", p=128)),
              stream, w=[b])
        return t, b

    def proj_phase(dram_w, ncg, cw, nchunks_total, out_cb, which):
        for cg in range(ncg):
            for c0 in range(0, F, NSL):
                t, b = load_slab(dram_w, c0, NSL, cg * cw, cw, "w" + which)
                for m in range(NT):
                    def fn(e, c0=c0, t=t, m=m, cw=cw):
                        last = None
                        for i in range(NSL):
                            c = c0 + i
                            last = e.matmul(PP[m % 8][:, 0:cw],
                                            big[:, c * TPC + m * 128:c * TPC + (m + 1) * 128],
                                            t[:, i * cw:(i + 1) * cw],
                                            start=(c == 0), stop=(c == F - 1))
                        return last
                    P.do("pe", fn, r=[bBA, bBB, b], w=[bP[m % 8]])
            for m in range(NT):
                out_cb(m, cg, PP[m % 8], bP[m % 8])

    def _finish():
        toks = []
        for name, (sem, val) in P.dsem.items():
            if val:
                toks.append((sem, val))
        P.wait_only("sp", toks)
        P.emit()
        return nc

    if stop_phase <= 1:
        return _finish()
    def kv_evac(kind):
        def cb(m, cg, psum, pbuf):
            nh = KW // 128
            if kind == "v":
                for hb in range(nh):
                    si = (m * nh + hb) % 4
                    kvh = cg * nh + hb
                    P.do("dve", lambda e, si=si, hb=hb, psum=psum:
                         e.tensor_copy(stg4[si][:, 0:128], psum[:, hb * 128:(hb + 1) * 128]),
                         r=[pbuf], w=[bStg[si]])
                    P.dma("sp", lambda e, si=si, m=m, kvh=kvh:
                          e.dma_start(d_exi.ap()[m * 128:(m + 1) * 128, KF + kvh * 128:KF + (kvh + 1) * 128],
                                      stg4[si][:, 0:128]),
                          f"stg{si}", r=[bStg[si]], wa=[bDexi])
            else:
                for hb in range(nh):
                    kvh = cg * nh + hb
                    P.do("act", lambda e, hb=hb, kvh=kvh, psum=psum:
                         e.activation(sc1[:, 0:128], psum[:, hb * 128:(hb + 1) * 128], AF.Square,
                                      accum_out=invk[:, kvh:kvh + 1]),
                         r=[pbuf], w=[bS1, bInvk])
                h0 = cg * nh
                P.do("dve", lambda e, h0=h0, nh=nh:
                     e.tensor_scalar(invk[:, HKV + h0:HKV + h0 + nh], invk[:, h0:h0 + nh],
                                     1.0 / HD, cfg.EPS_RMS, ALU.mult, ALU.add),
                     r=[bInvk], w=[bInvk])
                P.do("act", lambda e, h0=h0, nh=nh:
                     e.activation(invk[:, 2 * HKV + h0:2 * HKV + h0 + nh],
                                  invk[:, HKV + h0:HKV + h0 + nh], AF.Ln), r=[bInvk], w=[bInvk])
                P.do("act", lambda e, h0=h0, nh=nh:
                     e.activation(invk[:, 3 * HKV + h0:3 * HKV + h0 + nh],
                                  invk[:, 2 * HKV + h0:2 * HKV + h0 + nh], AF.Exp, scale=-0.5),
                     r=[bInvk], w=[bInvk])
                for hb in range(nh):
                    si = (m * nh + hb) % 4
                    kvh = cg * nh + hb
                    P.do("dve", lambda e, si=si, hb=hb, kvh=kvh, psum=psum:
                         e.scalar_tensor_tensor(stg4[si][:, 0:128], psum[:, hb * 128:(hb + 1) * 128],
                                                invk[:, 3 * HKV + kvh:3 * HKV + kvh + 1],
                                                wbck[:, :], ALU.mult, ALU.mult),
                         r=[pbuf, bInvk, bWbk], w=[bStg[si]])
                    P.dma("sp", lambda e, si=si, m=m, kvh=kvh:
                          e.dma_start(d_exi.ap()[m * 128:(m + 1) * 128, kvh * 128:(kvh + 1) * 128],
                                      stg4[si][:, 0:128]),
                          f"stg{si}", r=[bStg[si]], wa=[bDexi])
        return cb

    proj_phase(d_wk, KCG, KW, KC, kv_evac("k"), "k")
    proj_phase(d_wv, KCG, KW, KC, kv_evac("v"), "v")

    if stop_phase <= 2:
        return _finish()
    for h in range(H):
        sc = 1.0 / (math.sqrt(HD) * cfg.slopes[h])
        P.do("dve", lambda e, sc=sc: e.tensor_scalar(qwrb[:, 0:HD], qwr[:, 0:HD], sc, None,
                                                     ALU.mult), r=[bQwr], w=[bQwrb])
        pi = 4 + (h % 2)
        P.do("pe", MM(PP[pi][:, 0:128], onr[:, :], qwrb[:, 0:HD], True, True),
             r=[bConst, bQwrb], w=[bP[pi]])
        P.do("act", lambda e, h=h, pi=pi: e.activation(sc2[:, h * 128:(h + 1) * 128],
                                                       PP[pi][:, 0:128], AF.Copy),
             r=[bP[pi]], w=[bS2])

    def q_evac(m, cg, psum, pbuf):
        nh = QW // 128
        for hb in range(nh):
            h = cg * nh + hb
            P.do("act", lambda e, hb=hb, h=h, psum=psum:
                 e.activation(sc1[:, 0:128], psum[:, hb * 128:(hb + 1) * 128], AF.Square,
                              accum_out=invq[:, h:h + 1]), r=[pbuf], w=[bS1, bInvq])
        h0 = cg * nh
        P.do("dve", lambda e, h0=h0, nh=nh:
             e.tensor_scalar(invq[:, H + h0:H + h0 + nh], invq[:, h0:h0 + nh], 1.0 / HD,
                             cfg.EPS_RMS, ALU.mult, ALU.add), r=[bInvq], w=[bInvq])
        P.do("act", lambda e, h0=h0, nh=nh:
             e.activation(invq[:, 2 * H + h0:2 * H + h0 + nh], invq[:, H + h0:H + h0 + nh], AF.Ln),
             r=[bInvq], w=[bInvq])
        P.do("act", lambda e, h0=h0, nh=nh:
             e.activation(invq[:, 3 * H + h0:3 * H + h0 + nh], invq[:, 2 * H + h0:2 * H + h0 + nh],
                          AF.Exp, scale=-0.5), r=[bInvq], w=[bInvq])
        for hb in range(nh):
            si = (m * nh + hb) % 4
            h = cg * nh + hb
            P.do("dve", lambda e, si=si, hb=hb, h=h, psum=psum:
                 e.scalar_tensor_tensor(stg4[si][:, 0:128], psum[:, hb * 128:(hb + 1) * 128],
                                        invq[:, 3 * H + h:3 * H + h + 1],
                                        sc2[:, h * 128:(h + 1) * 128], ALU.mult, ALU.mult),
                 r=[pbuf, bInvq, bS2], w=[bStg[si]])
            P.dma("sp", lambda e, si=si, m=m, h=h:
                  e.dma_start(d_qh[m * 128:(m + 1) * 128, h * 128:(h + 1) * 128], stg4[si][:, 0:128]),
                  f"stg{si}", r=[bStg[si]], wa=[bDqh])

    proj_phase(d_wq, QCG, QW, HC, q_evac, "q")

    def fn_ar(e):
        return e.collective_compute("AllReduce", ALU.max,
                                    replica_groups=[[0, 1, 2, 3, 4, 5, 6, 7]],
                                    ins=[d_cli.ap().opt()], outs=[d_clo.ap().opt()])
    P.cc(fn_ar, "ccar", r=[bDcli], w=[bDclo])

    def fn_ag(e):
        return e.collective_compute("AllGather", ALU.bypass,
                                    replica_groups=[[0, 1], [2, 3], [4, 5], [6, 7]],
                                    ins=[d_exi.ap().opt()], outs=[d_exo.ap().opt()])
    P.cc(fn_ag, "ccag", r=[bDexi], w=[bDexo])

    if stop_phase <= 3:
        return _finish()
    for t in range(ST):
        rb = (t % 2) * TPC + (t // 2) * 128
        st, sb = kvst[t % 2], bKv[t % 2]
        P.dma("sp", lambda e, st=st, rb=rb: e.dma_start(st[:, 0:KF],
                                                        d_exo.ap()[rb:rb + 128, 0:KF]),
              f"exr{t % 2}", r=[bDexo], w=[sb])
        P.dma("sp", lambda e, t=t, rb=rb: e.dma_start(big[:, BIGA + t * KF:BIGA + (t + 1) * KF],
                                                      d_exo.ap()[rb:rb + 128, KF:2 * KF]),
              "exrv", r=[bDexo], w=[bBB])
        for kb in range(0, KC, 4):
            nb = min(4, KC - kb)
            pi = (kb // 4) % 2

            def fn_tr(e, st=st, kb=kb, nb=nb, pi=pi):
                last = None
                for i in range(nb):
                    last = e.matmul(PP[pi][:, i * 128:(i + 1) * 128],
                                    st[:, (kb + i) * 128:(kb + i + 1) * 128], ic[:, :],
                                    start=True, stop=True)
                return last
            P.do("pe", fn_tr, r=[sb, bConst], w=[bP[pi]])
            for i in range(nb):
                kv = kb + i
                P.do("act", lambda e, kv=kv, t=t, pi=pi, i=i:
                     e.activation(big[:, kv * S + t * 128:kv * S + (t + 1) * 128],
                                  PP[pi][:, i * 128:(i + 1) * 128], AF.Copy),
                     r=[bP[pi]], w=[bBA])

    if stop_phase <= 35:
        return _finish()
    PTO = 0
    PRO = NTK * 128
    for j in range(NT):
        stiles = 2 * j + 2
        span = stiles * 128
        stg, sbq = qs[j % 2], bQs[j % 2]
        P.dma("sp", lambda e, stg=stg, j=j: e.dma_start(stg[:, 0:QF], d_qh[j * 128:(j + 1) * 128, :]),
              f"qhr{j % 2}", r=[bDqh], w=[sbq])
        for cb in range(0, HC, 4):
            nb = min(4, HC - cb)
            pi = 5 + (cb // 4) % 2

            def fn_trq(e, stg=stg, cb=cb, nb=nb, pi=pi):
                last = None
                for i in range(nb):
                    last = e.matmul(PP[pi][:, i * 128:(i + 1) * 128],
                                    stg[:, (cb + i) * 128:(cb + i + 1) * 128], ic[:, :],
                                    start=True, stop=True)
                return last
            P.do("pe", fn_trq, r=[sbq, bConst], w=[bP[pi]])
            P.do("act", lambda e, cb=cb, nb=nb, pi=pi:
                 e.activation(qt[:, cb * 128:(cb + nb) * 128], PP[pi][:, 0:nb * 128], AF.Copy),
                 r=[bP[pi]], w=[bQt])
        for kt in range(stiles):
            cc = j * (NTK + 1) + kt
            P.do("dve", lambda e, kt=kt, cc=cc:
                 e.tensor_scalar(gg[:, kt * 128:(kt + 1) * 128], dtt[:, :], c2s[:, cc:cc + 1],
                                 None, ALU.add), r=[bConst, bC2], w=[bGg])
            P.do("dve", lambda e, kt=kt:
                 e.tensor_scalar(tr[:, 0:128], gg[:, kt * 128:(kt + 1) * 128], 0.0, None,
                                 ALU.is_gt), r=[bGg], w=[bTr])
            P.do("dve", lambda e, kt=kt:
                 e.scalar_tensor_tensor(gg[:, kt * 128:(kt + 1) * 128], tr[:, 0:128], -cfg.NEGBIG,
                                        gg[:, kt * 128:(kt + 1) * 128], ALU.mult, ALU.add),
                 r=[bTr, bGg], w=[bGg])
        ngr = (stiles + 3) // 4
        for h in range(H):
            kv = h // G
            slope = cfg.slopes[h]
            for g in range(ngr):
                wgt = min(512, span - g * 512)
                P.do("pe", MM(PP[g][:, 0:wgt], qt[:, h * 128:(h + 1) * 128],
                              big[:, kv * S + g * 512:kv * S + g * 512 + wgt], True, True),
                     r=[bQt, bBA], w=[bP[g]])
                P.do("dve", lambda e, g=g, wgt=wgt:
                     e.tensor_tensor(tr[:, g * 512:g * 512 + wgt], PP[g][:, 0:wgt],
                                     gg[:, g * 512:g * 512 + wgt], ALU.add),
                     r=[bP[g], bGg], w=[bTr])
            P.do("dve", lambda e, span=span: e.reduce_max(smc[:, 11:12], tr[:, 0:span], AXX,
                                                          negate=True), r=[bTr], w=[bSmc])
            P.do("dve", lambda e, slope=slope: e.tensor_scalar(smc[:, 12:13], smc[:, 11:12],
                                                               slope, None, ALU.mult),
                 r=[bSmc], w=[bSmc])
            P.do("act", lambda e, span=span, slope=slope:
                 e.activation(sc1[:, PRO:PRO + span], tr[:, 0:span], AF.Exp,
                              bias=smc[:, 12:13], scale=slope, accum_out=smc[:, 13:14]),
                 r=[bTr, bSmc], w=[bS1, bSmc])
            P.do("dve", lambda e: e.reciprocal(smc[:, 14:15], smc[:, 13:14]), r=[bSmc], w=[bSmc])
            for g in range(ngr):
                nb = min(4, stiles - g * 4)
                pi = 5 + (g % 2)

                def fn_trp(e, g=g, nb=nb, pi=pi):
                    last = None
                    for i in range(nb):
                        kt = g * 4 + i
                        last = e.matmul(PP[pi][:, i * 128:(i + 1) * 128],
                                        sc1[:, PRO + kt * 128:PRO + (kt + 1) * 128], ic[:, :],
                                        start=True, stop=True)
                    return last
                P.do("pe", fn_trp, r=[bS1, bConst], w=[bP[pi]])
                P.do("dve", lambda e, g=g, nb=nb, pi=pi:
                     e.tensor_copy(sc1[:, PTO + g * 512:PTO + g * 512 + nb * 128],
                                   PP[pi][:, 0:nb * 128]), r=[bP[pi]], w=[bS1])

            def fn_av(e, kv=kv, stiles=stiles):
                last = None
                for kt in range(stiles):
                    last = e.matmul(PP[4][:, 0:128],
                                    sc1[:, PTO + kt * 128:PTO + (kt + 1) * 128],
                                    big[:, BIGA + kt * KF + kv * 128:BIGA + kt * KF + (kv + 1) * 128],
                                    start=(kt == 0), stop=(kt == stiles - 1))
                return last
            P.do("pe", fn_av, r=[bS1, bBB], w=[bP[4]])
            OB = max(DM, QF)
            sl = h % 2
            P.do("dve", lambda e, sl=sl, OB=OB: e.tensor_scalar(
                sc2[:, OB + sl * 128:OB + sl * 128 + 128], PP[4][:, 0:128], smc[:, 14:15],
                None, ALU.mult), r=[bP[4], bSmc], w=[bOb[sl]])
            pi2 = 5 + (h % 2)
            P.do("pe", lambda e, pi2=pi2, sl=sl, OB=OB: e.matmul(
                PP[pi2][:, 0:128], sc2[:, OB + sl * 128:OB + sl * 128 + 128], ic[:, :],
                start=True, stop=True), r=[bOb[sl], bConst], w=[bP[pi2]])
            P.do("act", lambda e, pi2=pi2, sl=sl, OB=OB: e.activation(
                sc2[:, OB + 256 + sl * 128:OB + 256 + sl * 128 + 128], PP[pi2][:, 0:128],
                AF.Copy), r=[bP[pi2]], w=[bOt2[sl]])
            P.dma("sp", lambda e, h=h, j=j, sl=sl, OB=OB: e.dma_start(
                d_ot[h, j, :, :], sc2[:, OB + 256 + sl * 128:OB + 256 + sl * 128 + 128]),
                f"otw{sl}", r=[bOt2[sl]], wa=[bDot])

    if stop_phase <= 4:
        return _finish()
    for i in range(4):
        Wc = F * TPC // 4
        P.dma("sp", lambda e, i=i, Wc=Wc: e.dma_start(big[:, i * Wc:(i + 1) * Wc],
                                                      d_xnT[:, i * Wc:(i + 1) * Wc]),
              f"xnr{i}", r=[bDxnT], w=[bBA] if i < 2 else [bBB])
    P.dma("sp", lambda e: e.dma_start(clm[:, :], d_clo.ap()), "clr", r=[bDclo], w=[bClm])
    P.do("dve", lambda e: e.tensor_scalar(clsel[:, :], clm[:, :], cfg.THRESH, None, ALU.is_gt),
         r=[bClm], w=[bClsel])
    for gc in range(DCG):
        P.dma("pool", lambda e, gc=gc: e.dma_start(asn_i[:, 0:DW], d_asn[:, gc * DW:(gc + 1) * DW]),
              "asnl", r=[bAsn], w=[bAsn])
        P.do("dve", lambda e: e.tensor_copy(yf[0:1, 0:DW], asn_i[:, 0:DW]), r=[bAsn], w=[bYf])
        P.do("dve", lambda e: e.tensor_copy(rowsb[:, 1024:1024 + DW], yf[0:1, 0:DW]),
             r=[bYf], w=[bRowsb])
        P.do("pe", MM(PP[2][0:NCL, 0:DW], onr[:, 0:NCL], rowsb[:, 1024:1024 + DW], True, True),
             r=[bConst, bRowsb], w=[bP[2]])
        P.do("dve", lambda e, gc=gc: e.tensor_scalar(qt[0:NCL, gc * DW:(gc + 1) * DW],
                                                     PP[2][0:NCL, 0:DW], iotc[:, :], None,
                                                     ALU.is_equal),
             r=[bP[2], bConst], w=[bQt])
        P.do("pe", lambda e, gc=gc: e.matmul(PP[3][0:1, 0:DW], clsel[:, :],
                                             qt[0:NCL, gc * DW:(gc + 1) * DW],
                                             start=True, stop=True),
             r=[bClsel, bQt], w=[bP[3]])
        P.do("dve", lambda e: e.tensor_copy(rowsb[:, 1536:1536 + DW], PP[3][0:1, 0:DW]),
             r=[bP[3]], w=[bRowsb])
        P.do("pe", lambda e: e.matmul(PP[2][:, 0:DW], onr[:, :], rowsb[:, 1536:1536 + DW],
                                      start=True, stop=True),
             r=[bConst, bRowsb], w=[bP[2]])
        P.do("dve", lambda e, gc=gc: e.tensor_copy(rsb[:, gc * DW:(gc + 1) * DW], PP[2][:, 0:DW]),
             r=[bP[2]], w=[bRsb])

    for cg in range(DCG):
        P.dma("sp", lambda e, cg=cg: e.dma_start(
            xa[:, 0:NT * DW].rearrange("p (m n) -> p m n", m=NT),
            d_x[:, cg * DW:(cg + 1) * DW].rearrange("(m p) n -> p m n", p=128)),
            "xs", w=[bXa])
        P.dma("pool", lambda e, cg=cg: e.dma_start(yf[0:1, 0:DW], d_bias[:, cg * DW:(cg + 1) * DW]),
              "bil", r=[bYf], w=[bYf])
        P.do("act", lambda e: e.activation(rowsb[:, 0:DW], yf[0:1, 0:DW], AF.Copy),
             r=[bYf], w=[bRowsb])
        P.do("dve", lambda e: e.tensor_tensor(rowsb[:, 512:512 + DW], yf[0:1, 0:DW],
                                              rowsb[:, 0:DW], ALU.subtract),
             r=[bYf, bRowsb], w=[bRowsb])
        allP = [bP[mm % 8] for mm in range(PCNT)]
        for c0 in range(0, HC, NSL):
            ns = min(NSL, HC - c0)
            t, b = load_slab(d_wo, c0, ns, cg * DW, DW, "wo")
            for i in range(ns):
                c = c0 + i
                stg, sbo = qs[c % 2], bQs[c % 2]
                P.dma("sp", lambda e, stg=stg, c=c: e.dma_start(
                    stg[:, 0:TPC].rearrange("p (j q) -> p j q", j=NT),
                    d_ot[c, :, :, :].rearrange("j d q -> d j q")),
                    f"otr{c % 2}", r=[bDot], w=[sbo])

                def fn(e, stg=stg, t=t, i=i, c=c):
                    last = None
                    for m in range(NT):
                        last = e.matmul(PP[m % 8][:, 0:DW], stg[:, m * 128:(m + 1) * 128],
                                        t[:, i * DW:(i + 1) * DW],
                                        start=(c == 0), stop=(c == HC - 1))
                    return last
                P.do("pe", fn, r=[sbo, b], w=allP)
        for m in range(NT):
            P.do("dve", lambda e, m=m: e.tensor_copy(xb[:, m * DW:(m + 1) * DW],
                                                     PP[m % 8][:, 0:DW]),
                 r=[bP[m % 8]], w=[bXb])
        for c0 in range(0, F, NSL):
            t, b = load_slab(d_wf, c0, NSL, cg * DW, DW, "wf")
            for i in range(NSL):
                c = c0 + i

                def fn(e, t=t, i=i, c=c):
                    last = None
                    for m in range(NT):
                        last = e.matmul(PP[m % 8][:, 0:DW],
                                        big[:, c * TPC + m * 128:c * TPC + (m + 1) * 128],
                                        t[:, i * DW:(i + 1) * DW],
                                        start=(c == 0), stop=False)
                    return last
                P.do("pe", fn, r=[bBA, bBB, b], w=allP)

        def fn_bias(e, cg=cg):
            last = None
            for m in range(NT):
                last = e.matmul(PP[m % 8][:, 0:DW], onr[:, :],
                                rowsb[:, 0:DW], start=False, stop=False)
                last = e.matmul(PP[m % 8][:, 0:DW], onr[:, :],
                                rowsb[:, 512:512 + DW], start=False, stop=True)
            return last
        P.do("pe", fn_bias, r=[bConst, bRowsb], w=allP)
        for m in range(NT):
            osl, obf = outst[m % 2], bOut[m % 2]
            P.do("dve", lambda e, m=m, cg=cg:
                 e.scalar_tensor_tensor(yf[:, 0:DW], PP[m % 8][:, 0:DW], qselc[:, m:m + 1],
                                        rsb[:, cg * DW:(cg + 1) * DW], ALU.mult, ALU.mult),
                 r=[bP[m % 8], bQsel, bRsb], w=[bYf])
            P.do("dve", lambda e, m=m:
                 e.tensor_tensor(yf[:, DW:2 * DW], yf[:, 0:DW], xa[:, m * DW:(m + 1) * DW],
                                 ALU.add), r=[bYf, bXa], w=[bYf])
            P.do("dve", lambda e, m=m, osl=osl:
                 e.tensor_tensor(osl[:, :], yf[:, DW:2 * DW], xb[:, m * DW:(m + 1) * DW],
                                 ALU.add), r=[bYf, bXb], w=[obf])
            P.dma("sp", lambda e, m=m, cg=cg, osl=osl:
                  e.dma_start(d_out[m * 128:(m + 1) * 128, cg * DW:(cg + 1) * DW], osl[:, :]),
                  f"outw{m % 2}", r=[obf], wa=[bDout])

    P.wait_only("sp", list(bDout.ws.values()) + ([bDout.w] if bDout.w else []))
    P.emit()
    return nc



def make_inputs(cfg, x, Wq, Wk, Wv, Wo, qw, kw, Wf, fbias, cent, asn):
    B, S, DM, NT = cfg.B, cfg.S, cfg.DM, cfg.NT
    x = np.asarray(x, dtype=np.float32)
    shared = {
        "wqT": np.ascontiguousarray(np.asarray(Wq, dtype=np.float32).T),
        "wkT": np.ascontiguousarray(np.asarray(Wk, dtype=np.float32).T),
        "wvT": np.ascontiguousarray(np.asarray(Wv, dtype=np.float32).T),
        "woT": np.ascontiguousarray(np.asarray(Wo, dtype=np.float32).T),
        "wfT": np.ascontiguousarray(np.asarray(Wf, dtype=np.float32).T),
        "ctT": np.ascontiguousarray(np.asarray(cent, dtype=np.float32).T),
        "fbias": np.asarray(fbias, dtype=np.float32).reshape(1, DM),
        "qnw": np.asarray(qw, dtype=np.float32).reshape(1, cfg.HD),
        "knw": np.asarray(kw, dtype=np.float32).reshape(1, cfg.HD),
        "asn": np.asarray(asn).reshape(1, DM).astype(np.int32),
    }
    xt = x.reshape(B, cfg.ST, 128, DM)
    NTK = 2 * NT
    in_maps = []
    for c in range(8):
        b, par = c // 2, c % 2
        xs = np.ascontiguousarray(xt[b, par::2].reshape(cfg.TPC, DM))
        c2 = np.zeros((128, NT * (NTK + 1)), dtype=np.float32)
        for j in range(NT):
            for kt in range(2 * j + 2):
                c2[:, j * (NTK + 1) + kt] = 128.0 * (kt - (2 * j + par))
        in_maps.append(dict(shared, xt=xs, c2in=c2))
    return in_maps


def assemble(cfg, outs):
    B, S, DM, NT = cfg.B, cfg.S, cfg.DM, cfg.NT
    full = np.zeros((B, S, DM), dtype=np.float32)
    for c in range(8):
        b, par = c // 2, c % 2
        o = np.asarray(outs[c]).reshape(NT, 128, DM)
        for t in range(NT):
            full[b, (2 * t + par) * 128:(2 * t + par + 1) * 128, :] = o[t]
    return full


_FULL = Cfg()


def kernel(L0_attn_x, L0_attn_Wq, L0_attn_Wk, L0_attn_Wv, L0_attn_Wo,
           L0_attn_q_norm_w, L0_attn_k_norm_w,
           L0_ffn_weight, L0_ffn_bias, L0_ffn_centroids, L0_ffn_assignments):
    cfg = _FULL
    nc = build_program(cfg)
    in_maps = make_inputs(cfg, L0_attn_x, L0_attn_Wq, L0_attn_Wk, L0_attn_Wv, L0_attn_Wo,
                          L0_attn_q_norm_w, L0_attn_k_norm_w, L0_ffn_weight, L0_ffn_bias,
                          L0_ffn_centroids, L0_ffn_assignments)
    res = run_bass_kernel_spmd(nc, in_maps, core_ids=list(range(8)))
    return assemble(cfg, [r["out"] for r in res.results])


# version 03
# speedup vs baseline: 1.0401x; 1.0401x over previous
import math
import numpy as np
import ml_dtypes
import concourse.bass as bass
import concourse.mybir as mybir
from concourse.bass_utils import run_bass_kernel_spmd

F32 = mybir.dt.float32
BF16 = mybir.dt.bfloat16
I32 = mybir.dt.int32
AF = mybir.ActivationFunctionType
ALU = mybir.AluOpType
AXX = mybir.AxisListType.X
BF = ml_dtypes.bfloat16


class Cfg:
    def __init__(self, DM=4096, S=2048, H=32, HKV=8, NCL=64):
        self.B = 4
        self.DM = DM
        self.S = S
        self.H = H
        self.HKV = HKV
        self.G = H // HKV
        self.HD = 128
        self.NCL = NCL
        self.F = DM // 128
        self.QF = H * self.HD
        self.KF = HKV * self.HD
        self.NT = S // 256
        self.TPC = self.NT * 128
        self.ST = S // 128
        start = 2 ** (-(2 ** (-(math.log2(H) - 3))))
        self.slopes = [start * start ** i for i in range(H)]
        self.EPS_LN = 1e-5
        self.EPS_RMS = 1e-6
        self.TEMP = 0.1
        self.THRESH = 0.01
        self.NEGBIG = 3.0e32


ENGS = ("pe", "dve", "act", "pool", "sp")


class Buf:
    __slots__ = ("w", "rs", "ws")

    def __init__(self):
        self.w = None
        self.rs = []
        self.ws = {}


class Prog:
    def __init__(self, nc):
        self.nc = nc
        self.ops = {e: [] for e in ENGS}
        self.cnt = {e: 0 for e in ENGS}
        self.esem = {e: nc.alloc_semaphore(f"es_{e}") for e in ENGS}
        self.pe_sem_name = self.esem["pe"].name
        self.wm = {e: {} for e in ENGS}
        self.dsem = {}

    def _waits(self, e, deps):
        best = {}
        for d in deps:
            if d is None:
                continue
            sem, val = d
            k = sem.name
            if e == "pe" and k == self.pe_sem_name:
                continue
            if k not in best or val > best[k][1]:
                best[k] = (sem, val)
        waits = []
        for k, (sem, val) in sorted(best.items()):
            if self.wm[e].get(k, 0) >= val:
                continue
            waits.append((sem, val))
            self.wm[e][k] = val
        return waits

    def _deps(self, r, w, wa, deps):
        out = list(deps)
        for b in r:
            if b.w is not None:
                out.append(b.w)
            out.extend(b.ws.values())
        for b in w:
            if b.w is not None:
                out.append(b.w)
            out.extend(b.ws.values())
            out.extend(b.rs)
        for b in wa:
            pass
        return out

    def _book(self, tok, r, w, wa):
        for b in r:
            b.rs.append(tok)
        for b in w:
            b.w = tok
            b.rs = []
        for b in wa:
            b.ws[tok[0].name] = tok
        return tok

    def do(self, e, fn, r=(), w=(), wa=(), deps=()):
        waits = self._waits(e, self._deps(r, w, wa, deps))
        self.cnt[e] += 1
        tok = (self.esem[e], self.cnt[e])
        self.ops[e].append((waits, fn, self.esem[e], 1))
        return self._book(tok, r, w, wa)

    def dma(self, q, fn, stream, r=(), w=(), wa=(), deps=(), inc=16):
        waits = self._waits(q, self._deps(r, w, wa, deps))
        if stream not in self.dsem:
            self.dsem[stream] = [self.nc.alloc_semaphore("ds_" + stream), 0]
        ent = self.dsem[stream]
        ent[1] += inc
        tok = (ent[0], ent[1])
        self.ops[q].append((waits, fn, ent[0], inc))
        return self._book(tok, r, w, wa)

    def cc(self, fn, stream, r=(), w=(), deps=()):
        return self.dma("pool", fn, stream, r=r, w=w, deps=deps, inc=1)

    def wait_only(self, e, deps):
        waits = self._waits(e, [d for d in deps if d is not None])
        if waits:
            self.ops[e].append((waits, None, None, 0))

    def emit(self):
        nc = self.nc

        def mk(e):
            def body(eng):
                for waits, fn, sem, inc in self.ops[e]:
                    for (s, v) in waits:
                        eng.wait_ge(s, v)
                    if fn is not None:
                        inst = fn(eng)
                        if inst is not None and sem is not None:
                            inst.then_inc(sem, inc)
            return body

        with nc.Block() as block:
            block.sync(mk("sp"))
            block.gpsimd(mk("pool"))
            block.tensor(mk("pe"))
            block.vector(mk("dve"))
            block.scalar(mk("act"))


def build_program(cfg: Cfg, stop_phase=99):
    nc = bass.Bass("TRN2", target_bir_lowering=False)
    DM, S, H, HKV, G, HD = cfg.DM, cfg.S, cfg.H, cfg.HKV, cfg.G, cfg.HD
    F, QF, KF, NT, NCL, ST = cfg.F, cfg.QF, cfg.KF, cfg.NT, cfg.NCL, cfg.ST
    TPC = cfg.TPC
    HC = QF // 128
    KC = KF // 128
    QCG, QW = (QF // 512, 512) if QF >= 512 else (1, QF)
    KCG, KW = (KF // 512, 512) if KF >= 512 else (1, KF)
    DCG, DW = (DM // 512, 512) if DM >= 512 else (1, DM)
    NSL = 8 if F % 8 == 0 else F
    NTK = 2 * NT
    BIGA = HKV * S
    BIGC = max(2 * ST * KF, F * TPC)
    PCNT = min(NT, 8)

    d_x = nc.dram_tensor("xt", [TPC, DM], F32, kind="ExternalInput").ap()
    d_c2 = nc.dram_tensor("c2in", [128, NT * (NTK + 1)], F32, kind="ExternalInput").ap()
    d_wq = nc.dram_tensor("wqT", [DM, QF], F32, kind="ExternalInput").ap()
    d_wk = nc.dram_tensor("wkT", [DM, KF], F32, kind="ExternalInput").ap()
    d_wv = nc.dram_tensor("wvT", [DM, KF], F32, kind="ExternalInput").ap()
    d_wo = nc.dram_tensor("woT", [QF, DM], F32, kind="ExternalInput").ap()
    d_wf = nc.dram_tensor("wfT", [DM, DM], F32, kind="ExternalInput").ap()
    d_ct = nc.dram_tensor("ctT", [DM, NCL], F32, kind="ExternalInput").ap()
    d_bias = nc.dram_tensor("fbias", [1, DM], F32, kind="ExternalInput").ap()
    d_qw = nc.dram_tensor("qnw", [1, HD], F32, kind="ExternalInput").ap()
    d_kw = nc.dram_tensor("knw", [1, HD], F32, kind="ExternalInput").ap()
    d_asn = nc.dram_tensor("asn", [1, DM], I32, kind="ExternalInput").ap()
    d_out = nc.dram_tensor("out", [TPC, DM], F32, kind="ExternalOutput").ap()

    d_xnT = nc.dram_tensor("xnT_s", [128, F * TPC], BF16).ap()
    d_qh = nc.dram_tensor("qh_s", [TPC, QF], BF16).ap()
    d_ot = nc.dram_tensor("ot_s", [HC, 128, NT * 128], BF16).ap()
    d_exi = nc.dram_tensor("exi_s", [TPC, 2 * KF], BF16)
    d_exo = nc.dram_tensor("exo_s", [2 * TPC, 2 * KF], BF16)
    d_cli = nc.dram_tensor("cli_s", [NCL, 1], F32)
    d_clo = nc.dram_tensor("clo_s", [NCL, 1], F32)

    ident_bf = nc.inline_tensor(np.eye(128, dtype=np.float32).astype(BF), "c_ident").ap()
    ident_f32 = nc.inline_tensor(np.eye(128, dtype=np.float32), "c_identf").ap()
    dnp = (np.arange(128)[None, :] - np.arange(128)[:, None]).astype(np.float32)
    d_D = nc.inline_tensor(dnp, "c_D").ap()
    ones_bf = nc.inline_tensor(np.ones((1, 128), dtype=np.float32).astype(BF), "c_ones").ap()
    iota_cl = nc.inline_tensor(np.arange(NCL, dtype=np.float32)[:, None], "c_iota").ap()

    A = nc.alloc_sbuf_tensor
    big = A("bigb", [128, BIGC], BF16)
    xa = A("xa", [128, max(DM, NT * DW)], F32)
    xb = A("xb", [128, max(DM, NT * DW)], F32)
    wsl = [A(f"wsl{i}", [128, NSL * 512], BF16) for i in range(2)]
    qs = [A(f"qs{i}", [128, max(QF, DM, TPC)], BF16) for i in range(2)]
    qt = A("qt", [128, max(QF, DM)], BF16)
    tr = A("tr", [128, max(NTK * 128, F * NCL)], F32)
    sc1 = A("sc1", [128, max(DM, 2 * NTK * 128)], BF16)
    sc2 = A("sc2", [128, max(DM, QF) + 512], BF16)
    gg = A("gg", [128, NTK * 128], F32)
    kvst = [A(f"kvst{i}", [128, max(2 * KF, NTK * 128)], BF16) for i in range(2)]
    stg4 = [A(f"stg{i}", [128, 512], BF16) for i in range(4)]
    rsb = A("rsb", [128, DM], BF16)
    CLF = F * NCL
    assert 2 * CLF <= DM
    outst = [A(f"outst{i}", [128, DW], F32) for i in range(2)]
    yf = A("yf", [128, 2 * DW], F32)
    wbck = A("wbck", [128, 128], BF16)
    smc = A("smc", [128, 16], F32)
    qselc = A("qselc", [128, NT], F32)
    invq = A("invq", [128, 4 * H], F32)
    invk = A("invk", [128, 4 * HKV], F32)
    clm = A("clm", [NCL, 1], F32)
    clsel = A("clsel", [NCL, 1], BF16)
    rowsb = A("rowsb", [1, 4 * 512], BF16)
    qwr = A("qwr", [1, 2 * HD], F32)
    qwrb = A("qwrb", [1, 2 * HD], BF16)
    ic = A("ic", [128, 128], BF16)
    icf = A("icf", [128, 128], F32)
    dtt = A("dtt", [128, 128], F32)
    onr = A("onr", [1, 128], BF16)
    iotc = A("iotc", [NCL, 1], F32)
    c2s = A("c2s", [128, NT * (NTK + 1)], F32)
    asn_i = A("asni", [1, 512], I32)

    PP = [nc.alloc_psum_tensor(f"pp{i}", [128, 512], F32) for i in range(8)]

    P = Prog(nc)
    bBA, bBB = Buf(), Buf()
    bXa, bXb = Buf(), Buf()
    bW = [Buf() for _ in wsl]
    bQs = [Buf(), Buf()]
    bQt = Buf()
    bTr = Buf()
    bS1, bS2 = Buf(), Buf()
    bOb = [Buf(), Buf()]
    bOt2 = [Buf(), Buf()]
    bGg = Buf()
    bKv = [Buf(), Buf()]
    bStg = [Buf() for _ in stg4]
    bRsb = Buf()
    bOut = [Buf(), Buf()]
    bYf = Buf()
    bWbk = Buf()
    bSmc = Buf()
    bQsel = Buf()
    bInvq, bInvk = Buf(), Buf()
    bClm, bClsel = Buf(), Buf()
    bRowsb = Buf()
    bQwr, bQwrb = Buf(), Buf()
    bConst = Buf()
    bC2 = Buf()
    bAsn = Buf()
    bP = [Buf() for _ in PP]
    bDxnT, bDqh, bDot, bDexi, bDexo, bDcli, bDclo, bDout = (Buf() for _ in range(8))

    def xn_buf(c):
        return bBA if c < F // 2 else bBB

    def MM(out_ap, lhsT_ap, rhs_ap, start, stop):
        def fn(e, a=out_ap, b=lhsT_ap, c=rhs_ap, s1=start, s2=stop):
            return e.matmul(a, b, c, start=s1, stop=s2)
        return fn

    P.dma("pool", lambda e: e.dma_start(ic[:, :], ident_bf), "cst1", w=[bConst])
    P.dma("pool", lambda e: e.dma_start(icf[:, :], ident_f32), "cst2", w=[bConst])
    P.dma("pool", lambda e: e.dma_start(dtt[:, :], d_D), "cst3", w=[bConst])
    P.dma("pool", lambda e: e.dma_start(onr[:, :], ones_bf), "cst4", w=[bConst])
    P.dma("pool", lambda e: e.dma_start(iotc[:, :], iota_cl), "cst5", w=[bConst])
    P.dma("pool", lambda e: e.dma_start(c2s[:, :], d_c2), "cst6", w=[bC2])
    P.dma("pool", lambda e: e.dma_start(qwr[:, 0:HD], d_qw), "cst8", w=[bQwr])
    P.dma("pool", lambda e: e.dma_start(qwr[:, HD:2 * HD], d_kw), "cst9", w=[bQwr])
    P.dma("pool", lambda e: e.dma_start(
        tr[:, 0:F * NCL].rearrange("p (c n) -> p c n", c=F),
        d_ct.rearrange("(c p) n -> p c n", p=128)), "cst11", w=[bTr])

    P.do("act", lambda e: e.activation(rsb[:, 0:CLF], tr[:, 0:F * NCL], AF.Copy),
         r=[bTr], w=[bRsb])
    P.do("dve", lambda e: e.tensor_tensor(rsb[:, CLF:2 * CLF], tr[:, 0:F * NCL], rsb[:, 0:CLF],
                                          ALU.subtract), r=[bTr, bRsb], w=[bRsb])
    P.do("dve", lambda e: e.tensor_copy(qwrb[:, HD:2 * HD], qwr[:, HD:2 * HD]),
         r=[bQwr], w=[bQwrb])
    P.do("pe", MM(PP[7][:, 0:128], onr[:, :], qwrb[:, HD:2 * HD], True, True),
         r=[bConst, bQwrb], w=[bP[7]])
    P.do("act", lambda e: e.activation(wbck[:, :], PP[7][:, 0:128], AF.Copy), r=[bP[7]], w=[bWbk])

    def _finish0():
        toks = []
        for name, (sem, val) in P.dsem.items():
            if val:
                toks.append((sem, val))
        P.wait_only("sp", toks)
        P.emit()
        return nc

    if stop_phase <= 0:
        return _finish0()
    NT1 = NT if stop_phase > 90 or stop_phase >= 1.5 else NT
    import os
    SUB = int(os.environ.get("KSUB", "99"))
    for m in range(NT if SUB >= 10 else 1):
        P.dma("sp", lambda e, m=m: e.dma_start(xa[:, 0:DM], d_x[m * 128:(m + 1) * 128, :]),
              "x0", w=[bXa])
        P.do("dve", lambda e: e.reduce_sum(smc[:, 0:1], xa[:, 0:DM], AXX), r=[bXa], w=[bSmc])
        P.do("dve", lambda e: e.tensor_scalar(smc[:, 1:2], smc[:, 0:1], 1.0 / DM, None, ALU.mult),
             r=[bSmc], w=[bSmc])
        P.do("act", lambda e: e.activation(sc2[:, 0:DM], xa[:, 0:DM], AF.Square,
                                           bias=smc[:, 1:2], scale=-1.0, accum_out=smc[:, 2:3]),
             r=[bXa, bSmc], w=[bS2, bSmc])
        P.do("dve", lambda e: e.tensor_scalar(smc[:, 3:4], smc[:, 2:3], 1.0 / DM, cfg.EPS_LN,
                                              ALU.mult, ALU.add), r=[bSmc], w=[bSmc])
        P.do("act", lambda e: e.activation(smc[:, 4:5], smc[:, 3:4], AF.Ln), r=[bSmc], w=[bSmc])
        P.do("act", lambda e: e.activation(smc[:, 5:6], smc[:, 4:5], AF.Exp, scale=-0.5),
             r=[bSmc], w=[bSmc])
        P.do("dve", lambda e: e.tensor_scalar(xb[:, 0:DM], xa[:, 0:DM], smc[:, 1:2], smc[:, 5:6],
                                              ALU.subtract, ALU.mult), r=[bXa, bSmc], w=[bXb])
        P.do("act", lambda e: e.activation(sc2[:, 0:DM], xb[:, 0:DM], AF.Copy), r=[bXb], w=[bS2])
        P.do("dve", lambda e: e.tensor_tensor(sc1[:, 0:DM], xb[:, 0:DM], sc2[:, 0:DM],
                                              ALU.subtract), r=[bXb, bS2], w=[bS1])
        if SUB < 2:
            continue
        for cb in range(0, F, 4):
            nb = min(4, F - cb)
            pih = (cb // 4) % 2
            pil = 2 + (cb // 4) % 2

            def fn_hi(e, cb=cb, nb=nb, pih=pih):
                last = None
                for i in range(nb):
                    last = e.matmul(PP[pih][:, i * 128:(i + 1) * 128],
                                    sc2[:, (cb + i) * 128:(cb + i + 1) * 128], ic[:, :],
                                    start=True, stop=True)
                return last
            P.do("pe", fn_hi, r=[bS2, bConst], w=[bP[pih]])

            def fn_lo(e, cb=cb, nb=nb, pil=pil):
                last = None
                for i in range(nb):
                    last = e.matmul(PP[pil][:, i * 128:(i + 1) * 128],
                                    sc1[:, (cb + i) * 128:(cb + i + 1) * 128], ic[:, :],
                                    start=True, stop=True)
                return last
            P.do("pe", fn_lo, r=[bS1, bConst], w=[bP[pil]])
            for i in range(nb):
                c = cb + i
                P.do("act", lambda e, c=c, i=i, pih=pih, m=m:
                     e.activation(big[:, c * TPC + m * 128:c * TPC + (m + 1) * 128],
                                  PP[pih][:, i * 128:(i + 1) * 128], AF.Copy),
                     r=[bP[pih]], w=[xn_buf(c)])
            P.do("act", lambda e, cb=cb, nb=nb, pil=pil:
                 e.activation(qs[0][:, cb * 128:(cb + nb) * 128], PP[pil][:, 0:nb * 128], AF.Copy),
                 r=[bP[pil]], w=[bQs[0]])

        if SUB < 3:
            continue

        def fn_cent(e, m=m):
            last = None
            for c in range(F):
                hiT = big[:, c * TPC + m * 128:c * TPC + (m + 1) * 128]
                loT = qs[0][:, c * 128:(c + 1) * 128]
                last = e.matmul(PP[7][:, 0:NCL], hiT, rsb[:, c * NCL:(c + 1) * NCL],
                                start=(c == 0), stop=False)
                last = e.matmul(PP[7][:, 0:NCL], hiT, rsb[:, CLF + c * NCL:CLF + (c + 1) * NCL],
                                start=False, stop=False)
                last = e.matmul(PP[7][:, 0:NCL], loT, rsb[:, c * NCL:(c + 1) * NCL],
                                start=False, stop=(c == F - 1))
            return last
        P.do("pe", fn_cent, r=[bBA, bBB, bQs[0], bRsb], w=[bP[7]])
        if SUB < 4:
            continue
        SUB2 = int(os.environ.get("KSUB2", "99"))
        P.do("dve", lambda e: e.reduce_max(smc[:, 6:7], PP[7][:, 0:NCL], AXX, negate=True),
             r=[bP[7]], w=[bSmc])
        if SUB2 < 1:
            continue
        P.do("dve", lambda e: e.tensor_scalar(smc[:, 7:8], smc[:, 6:7], 1.0 / cfg.TEMP, None,
                                              ALU.mult), r=[bSmc], w=[bSmc])
        if SUB2 < 2:
            continue
        EXPV = int(os.environ.get("KEXPV", "0"))
        if EXPV == 1:
            P.do("act", lambda e: e.activation(tr[:, 0:NCL], PP[7][:, 0:NCL], AF.Exp,
                                               bias=smc[:, 7:8], scale=1.0 / cfg.TEMP),
                 r=[bP[7], bSmc], w=[bTr, bSmc])
        elif EXPV == 2:
            P.do("act", lambda e: e.activation(tr[:, 0:NCL], PP[7][:, 0:NCL], AF.Exp,
                                               scale=1.0 / cfg.TEMP, accum_out=smc[:, 8:9]),
                 r=[bP[7], bSmc], w=[bTr, bSmc])
        elif EXPV == 3:
            P.do("dve", lambda e: e.tensor_copy(xb[:, 0:NCL], PP[7][:, 0:NCL]), r=[bP[7]], w=[bXb])
            P.do("act", lambda e: e.activation(tr[:, 0:NCL], xb[:, 0:NCL], AF.Exp,
                                               bias=smc[:, 7:8], scale=1.0 / cfg.TEMP,
                                               accum_out=smc[:, 8:9]),
                 r=[bXb, bSmc], w=[bTr, bSmc])
        elif EXPV == 5:
            P.do("act", lambda e: e.activation(smc[:, 8:9], smc[:, 7:8], AF.Copy),
                 r=[bSmc], w=[bSmc])
        elif EXPV == 6:
            P.do("act", lambda e: e.activation(sc1[:, 0:NCL], PP[7][:, 0:NCL], AF.Exp,
                                               bias=smc[:, 7:8], scale=1.0 / cfg.TEMP,
                                               accum_out=smc[:, 8:9]),
                 r=[bP[7], bSmc], w=[bS1, bSmc])
        elif EXPV == 4:
            P.do("act", lambda e: e.activation(tr[:, 0:NCL], PP[7][:, 0:NCL], AF.Copy,
                                               bias=smc[:, 7:8], scale=1.0 / cfg.TEMP,
                                               accum_out=smc[:, 8:9]),
                 r=[bP[7], bSmc], w=[bTr, bSmc])
        else:
            P.do("act", lambda e: e.activation(tr[:, 0:NCL], PP[7][:, 0:NCL], AF.Exp,
                                               bias=smc[:, 7:8], scale=1.0 / cfg.TEMP,
                                               accum_out=smc[:, 8:9]),
                 r=[bP[7], bSmc], w=[bTr, bSmc])
        if SUB2 < 3:
            continue
        P.do("dve", lambda e, m=m: e.tensor_scalar(qselc[:, m:m + 1], smc[:, 8:9],
                                                   1.0 / cfg.THRESH, None, ALU.is_lt),
             r=[bSmc], w=[bQsel])
        if SUB2 < 4:
            continue
        P.do("dve", lambda e: e.reciprocal(smc[:, 9:10], smc[:, 8:9]), r=[bSmc], w=[bSmc])
        if SUB2 < 5:
            continue
        P.do("dve", lambda e: e.tensor_scalar(tr[:, 0:NCL], tr[:, 0:NCL], smc[:, 9:10], None,
                                              ALU.mult), r=[bSmc], w=[bTr])
        if SUB < 5:
            continue
        P.do("pe", lambda e: e.transpose(PP[6][0:NCL, 0:128], tr[:, 0:NCL], icf[:, :]),
             r=[bTr, bConst], w=[bP[6]])
        P.do("dve", lambda e: e.reduce_max(smc[0:NCL, 10:11], PP[6][0:NCL, 0:128], AXX),
             r=[bP[6]], w=[bSmc])
        if m == 0:
            P.do("dve", lambda e: e.tensor_copy(clm[:, :], smc[0:NCL, 10:11]), r=[bSmc], w=[bClm])
        else:
            P.do("dve", lambda e: e.tensor_tensor(clm[:, :], clm[:, :], smc[0:NCL, 10:11],
                                                  ALU.max), r=[bSmc, bClm], w=[bClm])

    for i in range(4):
        Wc = F * TPC // 4
        P.dma("sp", lambda e, i=i, Wc=Wc: e.dma_start(d_xnT[:, i * Wc:(i + 1) * Wc],
                                                      big[:, i * Wc:(i + 1) * Wc]),
              f"xnw{i}", r=[bBA, bBB], wa=[bDxnT])
    P.dma("sp", lambda e: e.dma_start(d_cli.ap(), clm[:, :]), "clw", r=[bClm], wa=[bDcli])

    slab_ctr = [0]

    def load_slab(dram_ap, c0, nchunks, col0, width, stream):
        si = slab_ctr[0] % len(wsl)
        slab_ctr[0] += 1
        t, b = wsl[si], bW[si]
        stream = f"wsl{si}"
        src = dram_ap[c0 * 128:(c0 + nchunks) * 128, col0:col0 + width]
        P.dma("pool", lambda e, t=t, src=src, nchunks=nchunks, width=width:
              e.dma_start(t[:, 0:nchunks * width].rearrange("p (c n) -> p c n", c=nchunks),
                          src.rearrange("(c p) n -> p c n", p=128)),
              stream, w=[b])
        return t, b

    def proj_phase(dram_w, ncg, cw, nchunks_total, out_cb, which, after_first_slab=None):
        for cg in range(ncg):
            for c0 in range(0, F, NSL):
                t, b = load_slab(dram_w, c0, NSL, cg * cw, cw, "w" + which)
                if after_first_slab is not None:
                    after_first_slab()
                    after_first_slab = None
                for m in range(NT):
                    def fn(e, c0=c0, t=t, m=m, cw=cw):
                        last = None
                        for i in range(NSL):
                            c = c0 + i
                            last = e.matmul(PP[m % 8][:, 0:cw],
                                            big[:, c * TPC + m * 128:c * TPC + (m + 1) * 128],
                                            t[:, i * cw:(i + 1) * cw],
                                            start=(c == 0), stop=(c == F - 1))
                        return last
                    P.do("pe", fn, r=[bBA, bBB, b], w=[bP[m % 8]])
            for m in range(NT):
                out_cb(m, cg, PP[m % 8], bP[m % 8])

    def _finish():
        toks = []
        for name, (sem, val) in P.dsem.items():
            if val:
                toks.append((sem, val))
        P.wait_only("sp", toks)
        P.emit()
        return nc

    if stop_phase <= 1:
        return _finish()
    def kv_evac(kind):
        def cb(m, cg, psum, pbuf):
            nh = KW // 128
            if kind == "v":
                for hb in range(nh):
                    si = (m * nh + hb) % 4
                    kvh = cg * nh + hb
                    P.do("dve", lambda e, si=si, hb=hb, psum=psum:
                         e.tensor_copy(stg4[si][:, 0:128], psum[:, hb * 128:(hb + 1) * 128]),
                         r=[pbuf], w=[bStg[si]])
                    P.dma("sp", lambda e, si=si, m=m, kvh=kvh:
                          e.dma_start(d_exi.ap()[m * 128:(m + 1) * 128, KF + kvh * 128:KF + (kvh + 1) * 128],
                                      stg4[si][:, 0:128]),
                          f"stg{si}", r=[bStg[si]], wa=[bDexi])
            else:
                for hb in range(nh):
                    kvh = cg * nh + hb
                    P.do("act", lambda e, hb=hb, kvh=kvh, psum=psum:
                         e.activation(sc1[:, 0:128], psum[:, hb * 128:(hb + 1) * 128], AF.Square,
                                      accum_out=invk[:, kvh:kvh + 1]),
                         r=[pbuf], w=[bS1, bInvk])
                h0 = cg * nh
                P.do("dve", lambda e, h0=h0, nh=nh:
                     e.tensor_scalar(invk[:, HKV + h0:HKV + h0 + nh], invk[:, h0:h0 + nh],
                                     1.0 / HD, cfg.EPS_RMS, ALU.mult, ALU.add),
                     r=[bInvk], w=[bInvk])
                P.do("act", lambda e, h0=h0, nh=nh:
                     e.activation(invk[:, 2 * HKV + h0:2 * HKV + h0 + nh],
                                  invk[:, HKV + h0:HKV + h0 + nh], AF.Ln), r=[bInvk], w=[bInvk])
                P.do("act", lambda e, h0=h0, nh=nh:
                     e.activation(invk[:, 3 * HKV + h0:3 * HKV + h0 + nh],
                                  invk[:, 2 * HKV + h0:2 * HKV + h0 + nh], AF.Exp, scale=-0.5),
                     r=[bInvk], w=[bInvk])
                for hb in range(nh):
                    si = (m * nh + hb) % 4
                    kvh = cg * nh + hb
                    P.do("dve", lambda e, si=si, hb=hb, kvh=kvh, psum=psum:
                         e.scalar_tensor_tensor(stg4[si][:, 0:128], psum[:, hb * 128:(hb + 1) * 128],
                                                invk[:, 3 * HKV + kvh:3 * HKV + kvh + 1],
                                                wbck[:, :], ALU.mult, ALU.mult),
                         r=[pbuf, bInvk, bWbk], w=[bStg[si]])
                    P.dma("sp", lambda e, si=si, m=m, kvh=kvh:
                          e.dma_start(d_exi.ap()[m * 128:(m + 1) * 128, kvh * 128:(kvh + 1) * 128],
                                      stg4[si][:, 0:128]),
                          f"stg{si}", r=[bStg[si]], wa=[bDexi])
        return cb

    proj_phase(d_wk, KCG, KW, KC, kv_evac("k"), "k")
    proj_phase(d_wv, KCG, KW, KC, kv_evac("v"), "v")

    if stop_phase <= 2:
        return _finish()
    for h in range(H):
        sc = 1.0 / (math.sqrt(HD) * cfg.slopes[h])
        P.do("dve", lambda e, sc=sc: e.tensor_scalar(qwrb[:, 0:HD], qwr[:, 0:HD], sc, None,
                                                     ALU.mult), r=[bQwr], w=[bQwrb])
        pi = 4 + (h % 2)
        P.do("pe", MM(PP[pi][:, 0:128], onr[:, :], qwrb[:, 0:HD], True, True),
             r=[bConst, bQwrb], w=[bP[pi]])
        P.do("act", lambda e, h=h, pi=pi: e.activation(sc2[:, h * 128:(h + 1) * 128],
                                                       PP[pi][:, 0:128], AF.Copy),
             r=[bP[pi]], w=[bS2])

    def q_evac(m, cg, psum, pbuf):
        nh = QW // 128
        for hb in range(nh):
            h = cg * nh + hb
            P.do("act", lambda e, hb=hb, h=h, psum=psum:
                 e.activation(sc1[:, 0:128], psum[:, hb * 128:(hb + 1) * 128], AF.Square,
                              accum_out=invq[:, h:h + 1]), r=[pbuf], w=[bS1, bInvq])
        h0 = cg * nh
        P.do("dve", lambda e, h0=h0, nh=nh:
             e.tensor_scalar(invq[:, H + h0:H + h0 + nh], invq[:, h0:h0 + nh], 1.0 / HD,
                             cfg.EPS_RMS, ALU.mult, ALU.add), r=[bInvq], w=[bInvq])
        P.do("act", lambda e, h0=h0, nh=nh:
             e.activation(invq[:, 2 * H + h0:2 * H + h0 + nh], invq[:, H + h0:H + h0 + nh], AF.Ln),
             r=[bInvq], w=[bInvq])
        P.do("act", lambda e, h0=h0, nh=nh:
             e.activation(invq[:, 3 * H + h0:3 * H + h0 + nh], invq[:, 2 * H + h0:2 * H + h0 + nh],
                          AF.Exp, scale=-0.5), r=[bInvq], w=[bInvq])
        for hb in range(nh):
            si = (m * nh + hb) % 4
            h = cg * nh + hb
            P.do("dve", lambda e, si=si, hb=hb, h=h, psum=psum:
                 e.scalar_tensor_tensor(stg4[si][:, 0:128], psum[:, hb * 128:(hb + 1) * 128],
                                        invq[:, 3 * H + h:3 * H + h + 1],
                                        sc2[:, h * 128:(h + 1) * 128], ALU.mult, ALU.mult),
                 r=[pbuf, bInvq, bS2], w=[bStg[si]])
            P.dma("sp", lambda e, si=si, m=m, h=h:
                  e.dma_start(d_qh[m * 128:(m + 1) * 128, h * 128:(h + 1) * 128], stg4[si][:, 0:128]),
                  f"stg{si}", r=[bStg[si]], wa=[bDqh])

    def fn_ar(e):
        return e.collective_compute("AllReduce", ALU.max,
                                    replica_groups=[[0, 1, 2, 3, 4, 5, 6, 7]],
                                    ins=[d_cli.ap().opt()], outs=[d_clo.ap().opt()])

    def fn_ag(e):
        return e.collective_compute("AllGather", ALU.bypass,
                                    replica_groups=[[0, 1], [2, 3], [4, 5], [6, 7]],
                                    ins=[d_exi.ap().opt()], outs=[d_exo.ap().opt()])

    def emit_ccs():
        P.cc(fn_ag, "ccag", r=[bDexi], w=[bDexo])
        P.cc(fn_ar, "ccar", r=[bDcli], w=[bDclo])

    proj_phase(d_wq, QCG, QW, HC, q_evac, "q", after_first_slab=emit_ccs)

    if stop_phase <= 3:
        return _finish()
    for t in range(ST):
        rb = (t % 2) * TPC + (t // 2) * 128
        st, sb = kvst[t % 2], bKv[t % 2]
        P.dma("sp", lambda e, st=st, rb=rb: e.dma_start(st[:, 0:KF],
                                                        d_exo.ap()[rb:rb + 128, 0:KF]),
              f"exr{t % 2}", r=[bDexo], w=[sb])
        P.dma("sp", lambda e, t=t, rb=rb: e.dma_start(big[:, BIGA + t * KF:BIGA + (t + 1) * KF],
                                                      d_exo.ap()[rb:rb + 128, KF:2 * KF]),
              "exrv", r=[bDexo], w=[bBB])
        for kb in range(0, KC, 4):
            nb = min(4, KC - kb)
            pi = (kb // 4) % 2

            def fn_tr(e, st=st, kb=kb, nb=nb, pi=pi):
                last = None
                for i in range(nb):
                    last = e.matmul(PP[pi][:, i * 128:(i + 1) * 128],
                                    st[:, (kb + i) * 128:(kb + i + 1) * 128], ic[:, :],
                                    start=True, stop=True)
                return last
            P.do("pe", fn_tr, r=[sb, bConst], w=[bP[pi]])
            for i in range(nb):
                kv = kb + i
                P.do("act", lambda e, kv=kv, t=t, pi=pi, i=i:
                     e.activation(big[:, kv * S + t * 128:kv * S + (t + 1) * 128],
                                  PP[pi][:, i * 128:(i + 1) * 128], AF.Copy),
                     r=[bP[pi]], w=[bBA])

    if stop_phase <= 35:
        return _finish()
    PTO = 0
    PRO = NTK * 128
    bSmcB = Buf()
    sctr = [0]
    for j in range(NT):
        stiles = 2 * j + 2
        span = stiles * 128
        stg, sbq = qs[j % 2], bQs[j % 2]
        P.dma("sp", lambda e, stg=stg, j=j: e.dma_start(stg[:, 0:QF], d_qh[j * 128:(j + 1) * 128, :]),
              f"qhr{j % 2}", r=[bDqh], w=[sbq])
        for cb in range(0, HC, 4):
            nb = min(4, HC - cb)
            pi = 5 + (cb // 4) % 2

            def fn_trq(e, stg=stg, cb=cb, nb=nb, pi=pi):
                last = None
                for i in range(nb):
                    last = e.matmul(PP[pi][:, i * 128:(i + 1) * 128],
                                    stg[:, (cb + i) * 128:(cb + i + 1) * 128], ic[:, :],
                                    start=True, stop=True)
                return last
            P.do("pe", fn_trq, r=[sbq, bConst], w=[bP[pi]])
            P.do("act", lambda e, cb=cb, nb=nb, pi=pi:
                 e.activation(qt[:, cb * 128:(cb + nb) * 128], PP[pi][:, 0:nb * 128], AF.Copy),
                 r=[bP[pi]], w=[bQt])
        for kt in range(stiles):
            cc = j * (NTK + 1) + kt
            P.do("dve", lambda e, kt=kt, cc=cc:
                 e.tensor_scalar(gg[:, kt * 128:(kt + 1) * 128], dtt[:, :], c2s[:, cc:cc + 1],
                                 None, ALU.add), r=[bConst, bC2], w=[bGg])
            P.do("dve", lambda e, kt=kt:
                 e.tensor_scalar(tr[:, 0:128], gg[:, kt * 128:(kt + 1) * 128], 0.0, None,
                                 ALU.is_gt), r=[bGg], w=[bTr])
            P.do("dve", lambda e, kt=kt:
                 e.scalar_tensor_tensor(gg[:, kt * 128:(kt + 1) * 128], tr[:, 0:128], -cfg.NEGBIG,
                                        gg[:, kt * 128:(kt + 1) * 128], ALU.mult, ALU.add),
                 r=[bTr, bGg], w=[bGg])
        ngr = (stiles + 3) // 4
        for h in range(H):
            kv = h // G
            slope = cfg.slopes[h]
            sl = h % 2
            if sl == 0:
                trA, bTrX = tr, bTr
                prT, bPrX = sc1, bS1
                ptT, bPtX = sc1, bS1
                PROx, PTOx = PRO, PTO
                C0 = 8
                bSx = bSmc
                o2, bO2 = PP[4], bP[4]
            else:
                trA, bTrX = xb, bXb
                prT, bPrX = kvst[1], bKv[1]
                ptT, bPtX = kvst[0], bKv[0]
                PROx, PTOx = 0, 0
                C0 = 12
                bSx = bSmcB
                o2, bO2 = PP[7], bP[7]
            for g in range(ngr):
                wgt = min(512, span - g * 512)
                sp = sctr[0] % 4
                sctr[0] += 1
                P.do("pe", MM(PP[sp][:, 0:wgt], qt[:, h * 128:(h + 1) * 128],
                              big[:, kv * S + g * 512:kv * S + g * 512 + wgt], True, True),
                     r=[bQt, bBA], w=[bP[sp]])
                P.do("dve", lambda e, g=g, wgt=wgt, sp=sp, trA=trA:
                     e.tensor_tensor(trA[:, g * 512:g * 512 + wgt], PP[sp][:, 0:wgt],
                                     gg[:, g * 512:g * 512 + wgt], ALU.add),
                     r=[bP[sp], bGg], w=[bTrX])
            P.do("dve", lambda e, span=span, trA=trA, C0=C0:
                 e.reduce_max(smc[:, C0:C0 + 1], trA[:, 0:span], AXX, negate=True),
                 r=[bTrX], w=[bSx])
            P.do("dve", lambda e, slope=slope, C0=C0:
                 e.tensor_scalar(smc[:, C0 + 1:C0 + 2], smc[:, C0:C0 + 1], slope, None, ALU.mult),
                 r=[bSx], w=[bSx])
            P.do("act", lambda e, span=span, slope=slope, trA=trA, prT=prT, PROx=PROx, C0=C0:
                 e.activation(prT[:, PROx:PROx + span], trA[:, 0:span], AF.Exp,
                              bias=smc[:, C0 + 1:C0 + 2], scale=slope,
                              accum_out=smc[:, C0 + 2:C0 + 3]),
                 r=[bTrX, bSx], w=[bPrX, bSx])
            P.do("dve", lambda e, C0=C0: e.reciprocal(smc[:, C0 + 3:C0 + 4], smc[:, C0 + 2:C0 + 3]),
                 r=[bSx], w=[bSx])
            for g in range(ngr):
                nb = min(4, stiles - g * 4)
                pi = 5 + (g % 2)

                def fn_trp(e, g=g, nb=nb, pi=pi, prT=prT, PROx=PROx):
                    last = None
                    for i in range(nb):
                        kt = g * 4 + i
                        last = e.matmul(PP[pi][:, i * 128:(i + 1) * 128],
                                        prT[:, PROx + kt * 128:PROx + (kt + 1) * 128], ic[:, :],
                                        start=True, stop=True)
                    return last
                P.do("pe", fn_trp, r=[bPrX, bConst], w=[bP[pi]])
                P.do("act", lambda e, g=g, nb=nb, pi=pi, ptT=ptT, PTOx=PTOx:
                     e.activation(ptT[:, PTOx + g * 512:PTOx + g * 512 + nb * 128],
                                  PP[pi][:, 0:nb * 128], AF.Copy), r=[bP[pi]], w=[bPtX])

            def fn_av(e, kv=kv, stiles=stiles, ptT=ptT, PTOx=PTOx, o2=o2):
                last = None
                for kt in range(stiles):
                    last = e.matmul(o2[:, 0:128],
                                    ptT[:, PTOx + kt * 128:PTOx + (kt + 1) * 128],
                                    big[:, BIGA + kt * KF + kv * 128:BIGA + kt * KF + (kv + 1) * 128],
                                    start=(kt == 0), stop=(kt == stiles - 1))
                return last
            P.do("pe", fn_av, r=[bPtX, bBB], w=[bO2])
            OB = max(DM, QF)
            P.do("dve", lambda e, sl=sl, OB=OB, o2=o2, C0=C0: e.tensor_scalar(
                sc2[:, OB + sl * 128:OB + sl * 128 + 128], o2[:, 0:128], smc[:, C0 + 3:C0 + 4],
                None, ALU.mult), r=[bO2, bSx], w=[bOb[sl]])
            pi2 = 5 + (h % 2)
            P.do("pe", lambda e, pi2=pi2, sl=sl, OB=OB: e.matmul(
                PP[pi2][:, 0:128], sc2[:, OB + sl * 128:OB + sl * 128 + 128], ic[:, :],
                start=True, stop=True), r=[bOb[sl], bConst], w=[bP[pi2]])
            P.do("act", lambda e, pi2=pi2, sl=sl, OB=OB: e.activation(
                sc2[:, OB + 256 + sl * 128:OB + 256 + sl * 128 + 128], PP[pi2][:, 0:128],
                AF.Copy), r=[bP[pi2]], w=[bOt2[sl]])
            P.dma("sp", lambda e, h=h, j=j, sl=sl, OB=OB: e.dma_start(
                d_ot[h, :, j * 128:(j + 1) * 128],
                sc2[:, OB + 256 + sl * 128:OB + 256 + sl * 128 + 128]),
                f"otw{sl}", r=[bOt2[sl]], wa=[bDot])

    if stop_phase <= 4:
        return _finish()
    for i in range(4):
        Wc = F * TPC // 4
        P.dma("sp", lambda e, i=i, Wc=Wc: e.dma_start(big[:, i * Wc:(i + 1) * Wc],
                                                      d_xnT[:, i * Wc:(i + 1) * Wc]),
              f"xnr{i}", r=[bDxnT], w=[bBA] if i < 2 else [bBB])
    P.dma("sp", lambda e: e.dma_start(clm[:, :], d_clo.ap()), "clr", r=[bDclo], w=[bClm])
    P.do("dve", lambda e: e.tensor_scalar(clsel[:, :], clm[:, :], cfg.THRESH, None, ALU.is_gt),
         r=[bClm], w=[bClsel])
    for gc in range(DCG):
        P.dma("pool", lambda e, gc=gc: e.dma_start(asn_i[:, 0:DW], d_asn[:, gc * DW:(gc + 1) * DW]),
              "asnl", r=[bAsn], w=[bAsn])
        P.do("dve", lambda e: e.tensor_copy(yf[0:1, 0:DW], asn_i[:, 0:DW]), r=[bAsn], w=[bYf])
        P.do("dve", lambda e: e.tensor_copy(rowsb[:, 1024:1024 + DW], yf[0:1, 0:DW]),
             r=[bYf], w=[bRowsb])
        P.do("pe", MM(PP[2][0:NCL, 0:DW], onr[:, 0:NCL], rowsb[:, 1024:1024 + DW], True, True),
             r=[bConst, bRowsb], w=[bP[2]])
        P.do("dve", lambda e, gc=gc: e.tensor_scalar(qt[0:NCL, gc * DW:(gc + 1) * DW],
                                                     PP[2][0:NCL, 0:DW], iotc[:, :], None,
                                                     ALU.is_equal),
             r=[bP[2], bConst], w=[bQt])
        P.do("pe", lambda e, gc=gc: e.matmul(PP[3][0:1, 0:DW], clsel[:, :],
                                             qt[0:NCL, gc * DW:(gc + 1) * DW],
                                             start=True, stop=True),
             r=[bClsel, bQt], w=[bP[3]])
        P.do("dve", lambda e: e.tensor_copy(rowsb[:, 1536:1536 + DW], PP[3][0:1, 0:DW]),
             r=[bP[3]], w=[bRowsb])
        P.do("pe", lambda e: e.matmul(PP[2][:, 0:DW], onr[:, :], rowsb[:, 1536:1536 + DW],
                                      start=True, stop=True),
             r=[bConst, bRowsb], w=[bP[2]])
        P.do("dve", lambda e, gc=gc: e.tensor_copy(rsb[:, gc * DW:(gc + 1) * DW], PP[2][:, 0:DW]),
             r=[bP[2]], w=[bRsb])

    for cg in range(DCG):
        P.dma("sp", lambda e, cg=cg: e.dma_start(
            xa[:, 0:NT * DW].rearrange("p (m n) -> p m n", m=NT),
            d_x[:, cg * DW:(cg + 1) * DW].rearrange("(m p) n -> p m n", p=128)),
            "xs", w=[bXa])
        P.dma("pool", lambda e, cg=cg: e.dma_start(yf[0:1, 0:DW], d_bias[:, cg * DW:(cg + 1) * DW]),
              "bil", r=[bYf], w=[bYf])
        P.do("act", lambda e: e.activation(rowsb[:, 0:DW], yf[0:1, 0:DW], AF.Copy),
             r=[bYf], w=[bRowsb])
        P.do("dve", lambda e: e.tensor_tensor(rowsb[:, 512:512 + DW], yf[0:1, 0:DW],
                                              rowsb[:, 0:DW], ALU.subtract),
             r=[bYf, bRowsb], w=[bRowsb])
        allP = [bP[mm % 8] for mm in range(PCNT)]
        for c0 in range(0, HC, NSL):
            ns = min(NSL, HC - c0)
            t, b = load_slab(d_wo, c0, ns, cg * DW, DW, "wo")
            for i in range(ns):
                c = c0 + i
                stg, sbo = qs[c % 2], bQs[c % 2]
                P.dma("sp", lambda e, stg=stg, c=c: e.dma_start(
                    stg[:, 0:TPC], d_ot[c, :, :]),
                    f"otr{c % 2}", r=[bDot], w=[sbo])

                def fn(e, stg=stg, t=t, i=i, c=c):
                    last = None
                    for m in range(NT):
                        last = e.matmul(PP[m % 8][:, 0:DW], stg[:, m * 128:(m + 1) * 128],
                                        t[:, i * DW:(i + 1) * DW],
                                        start=(c == 0), stop=(c == HC - 1))
                    return last
                P.do("pe", fn, r=[sbo, b], w=allP)
        for m in range(NT):
            P.do("dve", lambda e, m=m: e.tensor_copy(xb[:, m * DW:(m + 1) * DW],
                                                     PP[m % 8][:, 0:DW]),
                 r=[bP[m % 8]], w=[bXb])
        for c0 in range(0, F, NSL):
            t, b = load_slab(d_wf, c0, NSL, cg * DW, DW, "wf")
            for i in range(NSL):
                c = c0 + i

                def fn(e, t=t, i=i, c=c):
                    last = None
                    for m in range(NT):
                        last = e.matmul(PP[m % 8][:, 0:DW],
                                        big[:, c * TPC + m * 128:c * TPC + (m + 1) * 128],
                                        t[:, i * DW:(i + 1) * DW],
                                        start=(c == 0), stop=False)
                    return last
                P.do("pe", fn, r=[bBA, bBB, b], w=allP)

        def fn_bias(e, cg=cg):
            last = None
            for m in range(NT):
                last = e.matmul(PP[m % 8][:, 0:DW], onr[:, :],
                                rowsb[:, 0:DW], start=False, stop=False)
                last = e.matmul(PP[m % 8][:, 0:DW], onr[:, :],
                                rowsb[:, 512:512 + DW], start=False, stop=True)
            return last
        P.do("pe", fn_bias, r=[bConst, bRowsb], w=allP)
        for m in range(NT):
            osl, obf = outst[m % 2], bOut[m % 2]
            P.do("dve", lambda e, m=m, cg=cg:
                 e.scalar_tensor_tensor(yf[:, 0:DW], PP[m % 8][:, 0:DW], qselc[:, m:m + 1],
                                        rsb[:, cg * DW:(cg + 1) * DW], ALU.mult, ALU.mult),
                 r=[bP[m % 8], bQsel, bRsb], w=[bYf])
            P.do("dve", lambda e, m=m:
                 e.tensor_tensor(yf[:, DW:2 * DW], yf[:, 0:DW], xa[:, m * DW:(m + 1) * DW],
                                 ALU.add), r=[bYf, bXa], w=[bYf])
            P.do("dve", lambda e, m=m, osl=osl:
                 e.tensor_tensor(osl[:, :], yf[:, DW:2 * DW], xb[:, m * DW:(m + 1) * DW],
                                 ALU.add), r=[bYf, bXb], w=[obf])
            P.dma("sp", lambda e, m=m, cg=cg, osl=osl:
                  e.dma_start(d_out[m * 128:(m + 1) * 128, cg * DW:(cg + 1) * DW], osl[:, :]),
                  f"outw{m % 2}", r=[obf], wa=[bDout])

    P.wait_only("sp", list(bDout.ws.values()) + ([bDout.w] if bDout.w else []))
    P.emit()
    return nc



def make_inputs(cfg, x, Wq, Wk, Wv, Wo, qw, kw, Wf, fbias, cent, asn):
    B, S, DM, NT = cfg.B, cfg.S, cfg.DM, cfg.NT
    x = np.asarray(x, dtype=np.float32)
    shared = {
        "wqT": np.ascontiguousarray(np.asarray(Wq, dtype=np.float32).T),
        "wkT": np.ascontiguousarray(np.asarray(Wk, dtype=np.float32).T),
        "wvT": np.ascontiguousarray(np.asarray(Wv, dtype=np.float32).T),
        "woT": np.ascontiguousarray(np.asarray(Wo, dtype=np.float32).T),
        "wfT": np.ascontiguousarray(np.asarray(Wf, dtype=np.float32).T),
        "ctT": np.ascontiguousarray(np.asarray(cent, dtype=np.float32).T),
        "fbias": np.asarray(fbias, dtype=np.float32).reshape(1, DM),
        "qnw": np.asarray(qw, dtype=np.float32).reshape(1, cfg.HD),
        "knw": np.asarray(kw, dtype=np.float32).reshape(1, cfg.HD),
        "asn": np.asarray(asn).reshape(1, DM).astype(np.int32),
    }
    xt = x.reshape(B, cfg.ST, 128, DM)
    NTK = 2 * NT
    in_maps = []
    for c in range(8):
        b, par = c // 2, c % 2
        xs = np.ascontiguousarray(xt[b, par::2].reshape(cfg.TPC, DM))
        c2 = np.zeros((128, NT * (NTK + 1)), dtype=np.float32)
        for j in range(NT):
            for kt in range(2 * j + 2):
                c2[:, j * (NTK + 1) + kt] = 128.0 * (kt - (2 * j + par))
        in_maps.append(dict(shared, xt=xs, c2in=c2))
    return in_maps


def assemble(cfg, outs):
    B, S, DM, NT = cfg.B, cfg.S, cfg.DM, cfg.NT
    full = np.zeros((B, S, DM), dtype=np.float32)
    for c in range(8):
        b, par = c // 2, c % 2
        o = np.asarray(outs[c]).reshape(NT, 128, DM)
        for t in range(NT):
            full[b, (2 * t + par) * 128:(2 * t + par + 1) * 128, :] = o[t]
    return full


_FULL = Cfg()


def kernel(L0_attn_x, L0_attn_Wq, L0_attn_Wk, L0_attn_Wv, L0_attn_Wo,
           L0_attn_q_norm_w, L0_attn_k_norm_w,
           L0_ffn_weight, L0_ffn_bias, L0_ffn_centroids, L0_ffn_assignments):
    cfg = _FULL
    nc = build_program(cfg)
    in_maps = make_inputs(cfg, L0_attn_x, L0_attn_Wq, L0_attn_Wk, L0_attn_Wv, L0_attn_Wo,
                          L0_attn_q_norm_w, L0_attn_k_norm_w, L0_ffn_weight, L0_ffn_bias,
                          L0_ffn_centroids, L0_ffn_assignments)
    res = run_bass_kernel_spmd(nc, in_maps, core_ids=list(range(8)))
    return assemble(cfg, [r["out"] for r in res.results])


# version 04
# speedup vs baseline: 1.2409x; 1.1931x over previous
import math
import numpy as np
import ml_dtypes
import concourse.bass as bass
import concourse.mybir as mybir
from concourse.bass_utils import run_bass_kernel_spmd

F32 = mybir.dt.float32
BF16 = mybir.dt.bfloat16
I32 = mybir.dt.int32
AF = mybir.ActivationFunctionType
ALU = mybir.AluOpType
AXX = mybir.AxisListType.X
BF = ml_dtypes.bfloat16


class Cfg:
    def __init__(self, DM=4096, S=2048, H=32, HKV=8, NCL=64):
        self.B = 4
        self.DM = DM
        self.S = S
        self.H = H
        self.HKV = HKV
        self.G = H // HKV
        self.HD = 128
        self.NCL = NCL
        self.F = DM // 128
        self.QF = H * self.HD
        self.KF = HKV * self.HD
        self.NT = S // 256
        self.TPC = self.NT * 128
        self.ST = S // 128
        start = 2 ** (-(2 ** (-(math.log2(H) - 3))))
        self.slopes = [start * start ** i for i in range(H)]
        self.EPS_LN = 1e-5
        self.EPS_RMS = 1e-6
        self.TEMP = 0.1
        self.THRESH = 0.01
        self.NEGBIG = 3.0e32


ENGS = ("pe", "dve", "act", "pool", "sp")


class Buf:
    __slots__ = ("w", "rs", "ws")

    def __init__(self):
        self.w = None
        self.rs = []
        self.ws = {}


class Prog:
    def __init__(self, nc):
        self.nc = nc
        self.ops = {e: [] for e in ENGS}
        self.cnt = {e: 0 for e in ENGS}
        self.esem = {e: nc.alloc_semaphore(f"es_{e}") for e in ENGS}
        self.pe_sem_name = self.esem["pe"].name
        self.wm = {e: {} for e in ENGS}
        self.dsem = {}

    def _waits(self, e, deps):
        best = {}
        for d in deps:
            if d is None:
                continue
            sem, val = d
            k = sem.name
            if e == "pe" and k == self.pe_sem_name:
                continue
            if k not in best or val > best[k][1]:
                best[k] = (sem, val)
        waits = []
        for k, (sem, val) in sorted(best.items()):
            if self.wm[e].get(k, 0) >= val:
                continue
            waits.append((sem, val))
            self.wm[e][k] = val
        return waits

    def _deps(self, r, w, wa, deps):
        out = list(deps)
        for b in r:
            if b.w is not None:
                out.append(b.w)
            out.extend(b.ws.values())
        for b in w:
            if b.w is not None:
                out.append(b.w)
            out.extend(b.ws.values())
            out.extend(b.rs)
        for b in wa:
            pass
        return out

    def _book(self, tok, r, w, wa):
        for b in r:
            b.rs.append(tok)
        for b in w:
            b.w = tok
            b.rs = []
        for b in wa:
            b.ws[tok[0].name] = tok
        return tok

    def do(self, e, fn, r=(), w=(), wa=(), deps=()):
        waits = self._waits(e, self._deps(r, w, wa, deps))
        self.cnt[e] += 1
        tok = (self.esem[e], self.cnt[e])
        self.ops[e].append((waits, fn, self.esem[e], 1))
        return self._book(tok, r, w, wa)

    def dma(self, q, fn, stream, r=(), w=(), wa=(), deps=(), inc=16):
        waits = self._waits(q, self._deps(r, w, wa, deps))
        if stream not in self.dsem:
            self.dsem[stream] = [self.nc.alloc_semaphore("ds_" + stream), 0]
        ent = self.dsem[stream]
        ent[1] += inc
        tok = (ent[0], ent[1])
        self.ops[q].append((waits, fn, ent[0], inc))
        return self._book(tok, r, w, wa)

    def cc(self, fn, stream, r=(), w=(), deps=()):
        return self.dma("pool", fn, stream, r=r, w=w, deps=deps, inc=1)

    def wait_only(self, e, deps):
        waits = self._waits(e, [d for d in deps if d is not None])
        if waits:
            self.ops[e].append((waits, None, None, 0))

    def emit(self):
        nc = self.nc

        def mk(e):
            def body(eng):
                for waits, fn, sem, inc in self.ops[e]:
                    for (s, v) in waits:
                        eng.wait_ge(s, v)
                    if fn is not None:
                        inst = fn(eng)
                        if inst is not None and sem is not None:
                            inst.then_inc(sem, inc)
            return body

        with nc.Block() as block:
            block.sync(mk("sp"))
            block.gpsimd(mk("pool"))
            block.tensor(mk("pe"))
            block.vector(mk("dve"))
            block.scalar(mk("act"))


def build_program(cfg: Cfg, stop_phase=99):
    nc = bass.Bass("TRN2", target_bir_lowering=False)
    DM, S, H, HKV, G, HD = cfg.DM, cfg.S, cfg.H, cfg.HKV, cfg.G, cfg.HD
    F, QF, KF, NT, NCL, ST = cfg.F, cfg.QF, cfg.KF, cfg.NT, cfg.NCL, cfg.ST
    TPC = cfg.TPC
    HC = QF // 128
    KC = KF // 128
    QCG, QW = (QF // 512, 512) if QF >= 512 else (1, QF)
    KCG, KW = (KF // 512, 512) if KF >= 512 else (1, KF)
    DCG, DW = (DM // 512, 512) if DM >= 512 else (1, DM)
    NSL = 8 if F % 8 == 0 else F
    NTK = 2 * NT
    BIGA = HKV * S
    BIGC = max(2 * ST * KF, F * TPC)
    PCNT = min(NT, 8)

    d_x = nc.dram_tensor("xt", [TPC, DM], F32, kind="ExternalInput").ap()
    d_c2 = nc.dram_tensor("c2in", [128, NT * (NTK + 1)], F32, kind="ExternalInput").ap()
    d_wq = nc.dram_tensor("wqT", [DM, QF], F32, kind="ExternalInput").ap()
    d_wk = nc.dram_tensor("wkT", [DM, KF], F32, kind="ExternalInput").ap()
    d_wv = nc.dram_tensor("wvT", [DM, KF], F32, kind="ExternalInput").ap()
    d_wo = nc.dram_tensor("woT", [QF, DM], F32, kind="ExternalInput").ap()
    d_wf = nc.dram_tensor("wfT", [DM, DM], F32, kind="ExternalInput").ap()
    d_ct = nc.dram_tensor("ctT", [DM, NCL], F32, kind="ExternalInput").ap()
    d_bias = nc.dram_tensor("fbias", [1, DM], F32, kind="ExternalInput").ap()
    d_qw = nc.dram_tensor("qnw", [1, HD], F32, kind="ExternalInput").ap()
    d_kw = nc.dram_tensor("knw", [1, HD], F32, kind="ExternalInput").ap()
    d_asn = nc.dram_tensor("asn", [1, DM], I32, kind="ExternalInput").ap()
    d_out = nc.dram_tensor("out", [TPC, DM], F32, kind="ExternalOutput").ap()

    d_xnT = nc.dram_tensor("xnT_s", [128, F * TPC], BF16).ap()
    d_qh = nc.dram_tensor("qh_s", [TPC, QF], BF16).ap()
    d_ot = nc.dram_tensor("ot_s", [HC, 128, NT * 128], BF16).ap()
    d_exi = nc.dram_tensor("exi_s", [TPC, 2 * KF], BF16)
    d_exo = nc.dram_tensor("exo_s", [2 * TPC, 2 * KF], BF16)
    d_cli = nc.dram_tensor("cli_s", [NCL, 1], F32)
    d_clo = nc.dram_tensor("clo_s", [NCL, 1], F32)

    ident_bf = nc.inline_tensor(np.eye(128, dtype=np.float32).astype(BF), "c_ident").ap()
    ident_f32 = nc.inline_tensor(np.eye(128, dtype=np.float32), "c_identf").ap()
    dnp = (np.arange(128)[None, :] - np.arange(128)[:, None]).astype(np.float32)
    d_D = nc.inline_tensor(dnp, "c_D").ap()
    ones_bf = nc.inline_tensor(np.ones((1, 128), dtype=np.float32).astype(BF), "c_ones").ap()
    iota_cl = nc.inline_tensor(np.arange(NCL, dtype=np.float32)[:, None], "c_iota").ap()

    A = nc.alloc_sbuf_tensor
    big = A("bigb", [128, BIGC], BF16)
    xa = A("xa", [128, max(DM, NT * DW)], F32)
    xb = A("xb", [128, max(DM, NT * DW)], F32)
    wsl = [A(f"wsl{i}", [128, NSL * 512], BF16) for i in range(2)]
    qs = [A(f"qs{i}", [128, max(QF, DM, TPC)], BF16) for i in range(2)]
    qt = A("qt", [128, max(QF, DM)], BF16)
    tr = A("tr", [128, max(NTK * 128, F * NCL)], F32)
    sc1 = A("sc1", [128, max(DM, 2 * NTK * 128)], BF16)
    sc2 = A("sc2", [128, max(DM, QF) + 512], BF16)
    gg = A("gg", [128, NTK * 128], F32)
    kvst = [A(f"kvst{i}", [128, max(2 * KF, NTK * 128)], BF16) for i in range(2)]
    stg4 = [A(f"stg{i}", [128, 512], BF16) for i in range(4)]
    rsb = A("rsb", [128, DM], BF16)
    CLF = F * NCL
    assert 2 * CLF <= DM
    outst = [A(f"outst{i}", [128, DW], F32) for i in range(2)]
    yf = A("yf", [128, 2 * DW], F32)
    wbck = A("wbck", [128, 128], BF16)
    smc = A("smc", [128, 16], F32)
    qselc = A("qselc", [128, NT], F32)
    invq = A("invq", [128, 4 * H], F32)
    invk = A("invk", [128, 4 * HKV], F32)
    clm = A("clm", [NCL, 1], F32)
    clsel = A("clsel", [NCL, 1], BF16)
    rowsb = A("rowsb", [1, 4 * 512], BF16)
    qwr = A("qwr", [1, 2 * HD], F32)
    qwrb = A("qwrb", [1, 2 * HD], BF16)
    ic = A("ic", [128, 128], BF16)
    icf = A("icf", [128, 128], F32)
    dtt = A("dtt", [128, 128], F32)
    onr = A("onr", [1, 128], BF16)
    iotc = A("iotc", [NCL, 1], F32)
    c2s = A("c2s", [128, NT * (NTK + 1)], F32)
    asn_i = A("asni", [1, 512], I32)

    PP = [nc.alloc_psum_tensor(f"pp{i}", [128, 512], F32) for i in range(8)]

    P = Prog(nc)
    bBA, bBB = Buf(), Buf()
    bXa, bXb = Buf(), Buf()
    bW = [Buf() for _ in wsl]
    bQs = [Buf(), Buf()]
    bQt = Buf()
    bTr = Buf()
    bS1, bS2 = Buf(), Buf()
    bOb = [Buf(), Buf()]
    bOt2 = [Buf(), Buf()]
    bGg = Buf()
    bKv = [Buf(), Buf()]
    bStg = [Buf() for _ in stg4]
    bRsb = Buf()
    bOut = [Buf(), Buf()]
    bYf = Buf()
    bWbk = Buf()
    bSmc = Buf()
    bQsel = Buf()
    bInvq, bInvk = Buf(), Buf()
    bClm, bClsel = Buf(), Buf()
    bRowsb = Buf()
    bQwr, bQwrb = Buf(), Buf()
    bConst = Buf()
    bC2 = Buf()
    bAsn = Buf()
    bP = [Buf() for _ in PP]
    bDxnT, bDqh, bDot, bDexi, bDexo, bDcli, bDclo, bDout = (Buf() for _ in range(8))

    def xn_buf(c):
        return bBA if c < F // 2 else bBB

    def MM(out_ap, lhsT_ap, rhs_ap, start, stop):
        def fn(e, a=out_ap, b=lhsT_ap, c=rhs_ap, s1=start, s2=stop):
            return e.matmul(a, b, c, start=s1, stop=s2)
        return fn

    P.dma("pool", lambda e: e.dma_start(ic[:, :], ident_bf), "cst1", w=[bConst])
    P.dma("pool", lambda e: e.dma_start(icf[:, :], ident_f32), "cst2", w=[bConst])
    P.dma("pool", lambda e: e.dma_start(dtt[:, :], d_D), "cst3", w=[bConst])
    P.dma("pool", lambda e: e.dma_start(onr[:, :], ones_bf), "cst4", w=[bConst])
    P.dma("pool", lambda e: e.dma_start(iotc[:, :], iota_cl), "cst5", w=[bConst])
    P.dma("pool", lambda e: e.dma_start(c2s[:, :], d_c2), "cst6", w=[bC2])
    P.dma("pool", lambda e: e.dma_start(qwr[:, 0:HD], d_qw), "cst8", w=[bQwr])
    P.dma("pool", lambda e: e.dma_start(qwr[:, HD:2 * HD], d_kw), "cst9", w=[bQwr])
    P.dma("pool", lambda e: e.dma_start(
        tr[:, 0:F * NCL].rearrange("p (c n) -> p c n", c=F),
        d_ct.rearrange("(c p) n -> p c n", p=128)), "cst11", w=[bTr])

    P.do("act", lambda e: e.activation(rsb[:, 0:CLF], tr[:, 0:F * NCL], AF.Copy),
         r=[bTr], w=[bRsb])
    P.do("dve", lambda e: e.tensor_tensor(rsb[:, CLF:2 * CLF], tr[:, 0:F * NCL], rsb[:, 0:CLF],
                                          ALU.subtract), r=[bTr, bRsb], w=[bRsb])
    P.do("dve", lambda e: e.tensor_copy(qwrb[:, HD:2 * HD], qwr[:, HD:2 * HD]),
         r=[bQwr], w=[bQwrb])
    P.do("pe", MM(PP[7][:, 0:128], onr[:, :], qwrb[:, HD:2 * HD], True, True),
         r=[bConst, bQwrb], w=[bP[7]])
    P.do("act", lambda e: e.activation(wbck[:, :], PP[7][:, 0:128], AF.Copy), r=[bP[7]], w=[bWbk])

    def _finish0():
        toks = []
        for name, (sem, val) in P.dsem.items():
            if val:
                toks.append((sem, val))
        P.wait_only("sp", toks)
        P.emit()
        return nc

    if stop_phase <= 0:
        return _finish0()
    NT1 = NT if stop_phase > 90 or stop_phase >= 1.5 else NT
    import os
    SUB = int(os.environ.get("KSUB", "99"))
    for m in range(NT if SUB >= 10 else 1):
        P.dma("sp", lambda e, m=m: e.dma_start(xa[:, 0:DM], d_x[m * 128:(m + 1) * 128, :]),
              "x0", w=[bXa])
        P.do("dve", lambda e: e.reduce_sum(smc[:, 0:1], xa[:, 0:DM], AXX), r=[bXa], w=[bSmc])
        P.do("dve", lambda e: e.tensor_scalar(smc[:, 1:2], smc[:, 0:1], 1.0 / DM, None, ALU.mult),
             r=[bSmc], w=[bSmc])
        P.do("act", lambda e: e.activation(sc2[:, 0:DM], xa[:, 0:DM], AF.Square,
                                           bias=smc[:, 1:2], scale=-1.0, accum_out=smc[:, 2:3]),
             r=[bXa, bSmc], w=[bS2, bSmc])
        P.do("dve", lambda e: e.tensor_scalar(smc[:, 3:4], smc[:, 2:3], 1.0 / DM, cfg.EPS_LN,
                                              ALU.mult, ALU.add), r=[bSmc], w=[bSmc])
        P.do("act", lambda e: e.activation(smc[:, 4:5], smc[:, 3:4], AF.Ln), r=[bSmc], w=[bSmc])
        P.do("act", lambda e: e.activation(smc[:, 5:6], smc[:, 4:5], AF.Exp, scale=-0.5),
             r=[bSmc], w=[bSmc])
        P.do("dve", lambda e: e.tensor_scalar(xb[:, 0:DM], xa[:, 0:DM], smc[:, 1:2], smc[:, 5:6],
                                              ALU.subtract, ALU.mult), r=[bXa, bSmc], w=[bXb])
        P.do("act", lambda e: e.activation(sc2[:, 0:DM], xb[:, 0:DM], AF.Copy), r=[bXb], w=[bS2])
        P.do("dve", lambda e: e.tensor_tensor(sc1[:, 0:DM], xb[:, 0:DM], sc2[:, 0:DM],
                                              ALU.subtract), r=[bXb, bS2], w=[bS1])
        if SUB < 2:
            continue
        for cb in range(0, F, 4):
            nb = min(4, F - cb)
            pih = (cb // 4) % 2
            pil = 2 + (cb // 4) % 2

            def fn_hi(e, cb=cb, nb=nb, pih=pih):
                last = None
                for i in range(nb):
                    last = e.matmul(PP[pih][:, i * 128:(i + 1) * 128],
                                    sc2[:, (cb + i) * 128:(cb + i + 1) * 128], ic[:, :],
                                    start=True, stop=True)
                return last
            P.do("pe", fn_hi, r=[bS2, bConst], w=[bP[pih]])

            def fn_lo(e, cb=cb, nb=nb, pil=pil):
                last = None
                for i in range(nb):
                    last = e.matmul(PP[pil][:, i * 128:(i + 1) * 128],
                                    sc1[:, (cb + i) * 128:(cb + i + 1) * 128], ic[:, :],
                                    start=True, stop=True)
                return last
            P.do("pe", fn_lo, r=[bS1, bConst], w=[bP[pil]])
            for i in range(nb):
                c = cb + i
                P.do("act", lambda e, c=c, i=i, pih=pih, m=m:
                     e.activation(big[:, c * TPC + m * 128:c * TPC + (m + 1) * 128],
                                  PP[pih][:, i * 128:(i + 1) * 128], AF.Copy),
                     r=[bP[pih]], w=[xn_buf(c)])
            P.do("act", lambda e, cb=cb, nb=nb, pil=pil:
                 e.activation(qs[0][:, cb * 128:(cb + nb) * 128], PP[pil][:, 0:nb * 128], AF.Copy),
                 r=[bP[pil]], w=[bQs[0]])

        if SUB < 3:
            continue

        def fn_cent(e, m=m):
            last = None
            for c in range(F):
                hiT = big[:, c * TPC + m * 128:c * TPC + (m + 1) * 128]
                loT = qs[0][:, c * 128:(c + 1) * 128]
                last = e.matmul(PP[7][:, 0:NCL], hiT, rsb[:, c * NCL:(c + 1) * NCL],
                                start=(c == 0), stop=False)
                last = e.matmul(PP[7][:, 0:NCL], hiT, rsb[:, CLF + c * NCL:CLF + (c + 1) * NCL],
                                start=False, stop=False)
                last = e.matmul(PP[7][:, 0:NCL], loT, rsb[:, c * NCL:(c + 1) * NCL],
                                start=False, stop=(c == F - 1))
            return last
        P.do("pe", fn_cent, r=[bBA, bBB, bQs[0], bRsb], w=[bP[7]])
        if SUB < 4:
            continue
        SUB2 = int(os.environ.get("KSUB2", "99"))
        P.do("dve", lambda e: e.reduce_max(smc[:, 6:7], PP[7][:, 0:NCL], AXX, negate=True),
             r=[bP[7]], w=[bSmc])
        if SUB2 < 1:
            continue
        P.do("dve", lambda e: e.tensor_scalar(smc[:, 7:8], smc[:, 6:7], 1.0 / cfg.TEMP, None,
                                              ALU.mult), r=[bSmc], w=[bSmc])
        if SUB2 < 2:
            continue
        EXPV = int(os.environ.get("KEXPV", "0"))
        if EXPV == 1:
            P.do("act", lambda e: e.activation(tr[:, 0:NCL], PP[7][:, 0:NCL], AF.Exp,
                                               bias=smc[:, 7:8], scale=1.0 / cfg.TEMP),
                 r=[bP[7], bSmc], w=[bTr, bSmc])
        elif EXPV == 2:
            P.do("act", lambda e: e.activation(tr[:, 0:NCL], PP[7][:, 0:NCL], AF.Exp,
                                               scale=1.0 / cfg.TEMP, accum_out=smc[:, 8:9]),
                 r=[bP[7], bSmc], w=[bTr, bSmc])
        elif EXPV == 3:
            P.do("dve", lambda e: e.tensor_copy(xb[:, 0:NCL], PP[7][:, 0:NCL]), r=[bP[7]], w=[bXb])
            P.do("act", lambda e: e.activation(tr[:, 0:NCL], xb[:, 0:NCL], AF.Exp,
                                               bias=smc[:, 7:8], scale=1.0 / cfg.TEMP,
                                               accum_out=smc[:, 8:9]),
                 r=[bXb, bSmc], w=[bTr, bSmc])
        elif EXPV == 5:
            P.do("act", lambda e: e.activation(smc[:, 8:9], smc[:, 7:8], AF.Copy),
                 r=[bSmc], w=[bSmc])
        elif EXPV == 6:
            P.do("act", lambda e: e.activation(sc1[:, 0:NCL], PP[7][:, 0:NCL], AF.Exp,
                                               bias=smc[:, 7:8], scale=1.0 / cfg.TEMP,
                                               accum_out=smc[:, 8:9]),
                 r=[bP[7], bSmc], w=[bS1, bSmc])
        elif EXPV == 4:
            P.do("act", lambda e: e.activation(tr[:, 0:NCL], PP[7][:, 0:NCL], AF.Copy,
                                               bias=smc[:, 7:8], scale=1.0 / cfg.TEMP,
                                               accum_out=smc[:, 8:9]),
                 r=[bP[7], bSmc], w=[bTr, bSmc])
        else:
            P.do("act", lambda e: e.activation(tr[:, 0:NCL], PP[7][:, 0:NCL], AF.Exp,
                                               bias=smc[:, 7:8], scale=1.0 / cfg.TEMP,
                                               accum_out=smc[:, 8:9]),
                 r=[bP[7], bSmc], w=[bTr, bSmc])
        if SUB2 < 3:
            continue
        P.do("dve", lambda e, m=m: e.tensor_scalar(qselc[:, m:m + 1], smc[:, 8:9],
                                                   1.0 / cfg.THRESH, None, ALU.is_lt),
             r=[bSmc], w=[bQsel])
        if SUB2 < 4:
            continue
        P.do("dve", lambda e: e.reciprocal(smc[:, 9:10], smc[:, 8:9]), r=[bSmc], w=[bSmc])
        if SUB2 < 5:
            continue
        P.do("dve", lambda e: e.tensor_scalar(tr[:, 0:NCL], tr[:, 0:NCL], smc[:, 9:10], None,
                                              ALU.mult), r=[bSmc], w=[bTr])
        if SUB < 5:
            continue
        P.do("pe", lambda e: e.transpose(PP[6][0:NCL, 0:128], tr[:, 0:NCL], icf[:, :]),
             r=[bTr, bConst], w=[bP[6]])
        P.do("dve", lambda e: e.reduce_max(smc[0:NCL, 10:11], PP[6][0:NCL, 0:128], AXX),
             r=[bP[6]], w=[bSmc])
        if m == 0:
            P.do("dve", lambda e: e.tensor_copy(clm[:, :], smc[0:NCL, 10:11]), r=[bSmc], w=[bClm])
        else:
            P.do("dve", lambda e: e.tensor_tensor(clm[:, :], clm[:, :], smc[0:NCL, 10:11],
                                                  ALU.max), r=[bSmc, bClm], w=[bClm])

    for i in range(4):
        Wc = F * TPC // 4
        P.dma("sp", lambda e, i=i, Wc=Wc: e.dma_start(d_xnT[:, i * Wc:(i + 1) * Wc],
                                                      big[:, i * Wc:(i + 1) * Wc]),
              f"xnw{i}", r=[bBA, bBB], wa=[bDxnT])
    P.dma("sp", lambda e: e.dma_start(d_cli.ap(), clm[:, :]), "clw", r=[bClm], wa=[bDcli])

    slab_ctr = [0]

    def load_slab(dram_ap, c0, nchunks, col0, width, stream):
        si = slab_ctr[0] % len(wsl)
        slab_ctr[0] += 1
        t, b = wsl[si], bW[si]
        stream = f"wsl{si}"
        src = dram_ap[c0 * 128:(c0 + nchunks) * 128, col0:col0 + width]
        P.dma("pool", lambda e, t=t, src=src, nchunks=nchunks, width=width:
              e.dma_start(t[:, 0:nchunks * width].rearrange("p (c n) -> p c n", c=nchunks),
                          src.rearrange("(c p) n -> p c n", p=128)),
              stream, w=[b])
        return t, b

    def proj_phase(dram_w, ncg, cw, nchunks_total, out_cb, which, after_first_slab=None):
        for cg in range(ncg):
            for c0 in range(0, F, NSL):
                t, b = load_slab(dram_w, c0, NSL, cg * cw, cw, "w" + which)
                if after_first_slab is not None:
                    after_first_slab()
                    after_first_slab = None
                for m in range(NT):
                    def fn(e, c0=c0, t=t, m=m, cw=cw):
                        last = None
                        for i in range(NSL):
                            c = c0 + i
                            last = e.matmul(PP[m % 8][:, 0:cw],
                                            big[:, c * TPC + m * 128:c * TPC + (m + 1) * 128],
                                            t[:, i * cw:(i + 1) * cw],
                                            start=(c == 0), stop=(c == F - 1))
                        return last
                    P.do("pe", fn, r=[bBA, bBB, b], w=[bP[m % 8]])
            for m in range(NT):
                out_cb(m, cg, PP[m % 8], bP[m % 8])

    def _finish():
        toks = []
        for name, (sem, val) in P.dsem.items():
            if val:
                toks.append((sem, val))
        P.wait_only("sp", toks)
        P.emit()
        return nc

    if stop_phase <= 1:
        return _finish()
    def kv_evac(kind):
        def cb(m, cg, psum, pbuf):
            nh = KW // 128
            if kind == "v":
                for hb in range(nh):
                    si = (m * nh + hb) % 4
                    kvh = cg * nh + hb
                    P.do("dve", lambda e, si=si, hb=hb, psum=psum:
                         e.tensor_copy(stg4[si][:, 0:128], psum[:, hb * 128:(hb + 1) * 128]),
                         r=[pbuf], w=[bStg[si]])
                    P.dma("sp", lambda e, si=si, m=m, kvh=kvh:
                          e.dma_start(d_exi.ap()[m * 128:(m + 1) * 128, KF + kvh * 128:KF + (kvh + 1) * 128],
                                      stg4[si][:, 0:128]),
                          f"stg{si}", r=[bStg[si]], wa=[bDexi])
            else:
                for hb in range(nh):
                    kvh = cg * nh + hb
                    P.do("act", lambda e, hb=hb, kvh=kvh, psum=psum:
                         e.activation(sc1[:, 0:128], psum[:, hb * 128:(hb + 1) * 128], AF.Square,
                                      accum_out=invk[:, kvh:kvh + 1]),
                         r=[pbuf], w=[bS1, bInvk])
                h0 = cg * nh
                P.do("dve", lambda e, h0=h0, nh=nh:
                     e.tensor_scalar(invk[:, HKV + h0:HKV + h0 + nh], invk[:, h0:h0 + nh],
                                     1.0 / HD, cfg.EPS_RMS, ALU.mult, ALU.add),
                     r=[bInvk], w=[bInvk])
                P.do("act", lambda e, h0=h0, nh=nh:
                     e.activation(invk[:, 2 * HKV + h0:2 * HKV + h0 + nh],
                                  invk[:, HKV + h0:HKV + h0 + nh], AF.Ln), r=[bInvk], w=[bInvk])
                P.do("act", lambda e, h0=h0, nh=nh:
                     e.activation(invk[:, 3 * HKV + h0:3 * HKV + h0 + nh],
                                  invk[:, 2 * HKV + h0:2 * HKV + h0 + nh], AF.Exp, scale=-0.5),
                     r=[bInvk], w=[bInvk])
                for hb in range(nh):
                    si = (m * nh + hb) % 4
                    kvh = cg * nh + hb
                    P.do("dve", lambda e, si=si, hb=hb, kvh=kvh, psum=psum:
                         e.scalar_tensor_tensor(stg4[si][:, 0:128], psum[:, hb * 128:(hb + 1) * 128],
                                                invk[:, 3 * HKV + kvh:3 * HKV + kvh + 1],
                                                wbck[:, :], ALU.mult, ALU.mult),
                         r=[pbuf, bInvk, bWbk], w=[bStg[si]])
                    P.dma("sp", lambda e, si=si, m=m, kvh=kvh:
                          e.dma_start(d_exi.ap()[m * 128:(m + 1) * 128, kvh * 128:(kvh + 1) * 128],
                                      stg4[si][:, 0:128]),
                          f"stg{si}", r=[bStg[si]], wa=[bDexi])
        return cb

    proj_phase(d_wk, KCG, KW, KC, kv_evac("k"), "k")
    proj_phase(d_wv, KCG, KW, KC, kv_evac("v"), "v")

    if stop_phase <= 2:
        return _finish()
    for h in range(H):
        sc = 1.0 / (math.sqrt(HD) * cfg.slopes[h])
        P.do("dve", lambda e, sc=sc: e.tensor_scalar(qwrb[:, 0:HD], qwr[:, 0:HD], sc, None,
                                                     ALU.mult), r=[bQwr], w=[bQwrb])
        pi = 4 + (h % 2)
        P.do("pe", MM(PP[pi][:, 0:128], onr[:, :], qwrb[:, 0:HD], True, True),
             r=[bConst, bQwrb], w=[bP[pi]])
        P.do("act", lambda e, h=h, pi=pi: e.activation(sc2[:, h * 128:(h + 1) * 128],
                                                       PP[pi][:, 0:128], AF.Copy),
             r=[bP[pi]], w=[bS2])

    def q_evac(m, cg, psum, pbuf):
        nh = QW // 128
        for hb in range(nh):
            h = cg * nh + hb
            P.do("act", lambda e, hb=hb, h=h, psum=psum:
                 e.activation(sc1[:, 0:128], psum[:, hb * 128:(hb + 1) * 128], AF.Square,
                              accum_out=invq[:, h:h + 1]), r=[pbuf], w=[bS1, bInvq])
        h0 = cg * nh
        P.do("dve", lambda e, h0=h0, nh=nh:
             e.tensor_scalar(invq[:, H + h0:H + h0 + nh], invq[:, h0:h0 + nh], 1.0 / HD,
                             cfg.EPS_RMS, ALU.mult, ALU.add), r=[bInvq], w=[bInvq])
        P.do("act", lambda e, h0=h0, nh=nh:
             e.activation(invq[:, 2 * H + h0:2 * H + h0 + nh], invq[:, H + h0:H + h0 + nh], AF.Ln),
             r=[bInvq], w=[bInvq])
        P.do("act", lambda e, h0=h0, nh=nh:
             e.activation(invq[:, 3 * H + h0:3 * H + h0 + nh], invq[:, 2 * H + h0:2 * H + h0 + nh],
                          AF.Exp, scale=-0.5), r=[bInvq], w=[bInvq])
        for hb in range(nh):
            si = (m * nh + hb) % 4
            h = cg * nh + hb
            P.do("dve", lambda e, si=si, hb=hb, h=h, psum=psum:
                 e.scalar_tensor_tensor(stg4[si][:, 0:128], psum[:, hb * 128:(hb + 1) * 128],
                                        invq[:, 3 * H + h:3 * H + h + 1],
                                        sc2[:, h * 128:(h + 1) * 128], ALU.mult, ALU.mult),
                 r=[pbuf, bInvq, bS2], w=[bStg[si]])
            P.dma("sp", lambda e, si=si, m=m, h=h:
                  e.dma_start(d_qh[m * 128:(m + 1) * 128, h * 128:(h + 1) * 128], stg4[si][:, 0:128]),
                  f"stg{si}", r=[bStg[si]], wa=[bDqh])

    def fn_ar(e):
        return e.collective_compute("AllReduce", ALU.max,
                                    replica_groups=[[0, 1, 2, 3, 4, 5, 6, 7]],
                                    ins=[d_cli.ap().opt()], outs=[d_clo.ap().opt()])

    def fn_ag(e):
        return e.collective_compute("AllGather", ALU.bypass,
                                    replica_groups=[[0, 1], [2, 3], [4, 5], [6, 7]],
                                    ins=[d_exi.ap().opt()], outs=[d_exo.ap().opt()])

    def emit_ccs():
        P.cc(fn_ag, "ccag", r=[bDexi], w=[bDexo])
        P.cc(fn_ar, "ccar", r=[bDcli], w=[bDclo])

    proj_phase(d_wq, QCG, QW, HC, q_evac, "q", after_first_slab=emit_ccs)

    if stop_phase <= 3:
        return _finish()
    for t in range(ST):
        rb = (t % 2) * TPC + (t // 2) * 128
        st, sb = kvst[t % 2], bKv[t % 2]
        P.dma("sp", lambda e, st=st, rb=rb: e.dma_start(st[:, 0:KF],
                                                        d_exo.ap()[rb:rb + 128, 0:KF]),
              f"exr{t % 2}", r=[bDexo], w=[sb])
        P.dma("sp", lambda e, t=t, rb=rb: e.dma_start(big[:, BIGA + t * KF:BIGA + (t + 1) * KF],
                                                      d_exo.ap()[rb:rb + 128, KF:2 * KF]),
              "exrv", r=[bDexo], w=[bBB])
        for kb in range(0, KC, 4):
            nb = min(4, KC - kb)
            pi = (kb // 4) % 2

            def fn_tr(e, st=st, kb=kb, nb=nb, pi=pi):
                last = None
                for i in range(nb):
                    last = e.matmul(PP[pi][:, i * 128:(i + 1) * 128],
                                    st[:, (kb + i) * 128:(kb + i + 1) * 128], ic[:, :],
                                    start=True, stop=True)
                return last
            P.do("pe", fn_tr, r=[sb, bConst], w=[bP[pi]])
            for i in range(nb):
                kv = kb + i
                P.do("act", lambda e, kv=kv, t=t, pi=pi, i=i:
                     e.activation(big[:, kv * S + t * 128:kv * S + (t + 1) * 128],
                                  PP[pi][:, i * 128:(i + 1) * 128], AF.Copy),
                     r=[bP[pi]], w=[bBA])

    if stop_phase <= 35:
        return _finish()
    PTO = 0
    PRO = NTK * 128
    bSmcB = Buf()
    sctr = [0]
    for j in range(NT):
        stiles = 2 * j + 2
        span = stiles * 128
        stg, sbq = qs[j % 2], bQs[j % 2]
        P.dma("sp", lambda e, stg=stg, j=j: e.dma_start(stg[:, 0:QF], d_qh[j * 128:(j + 1) * 128, :]),
              f"qhr{j % 2}", r=[bDqh], w=[sbq])
        for cb in range(0, HC, 4):
            nb = min(4, HC - cb)
            pi = 5 + (cb // 4) % 2

            def fn_trq(e, stg=stg, cb=cb, nb=nb, pi=pi):
                last = None
                for i in range(nb):
                    last = e.matmul(PP[pi][:, i * 128:(i + 1) * 128],
                                    stg[:, (cb + i) * 128:(cb + i + 1) * 128], ic[:, :],
                                    start=True, stop=True)
                return last
            P.do("pe", fn_trq, r=[sbq, bConst], w=[bP[pi]])
            P.do("act", lambda e, cb=cb, nb=nb, pi=pi:
                 e.activation(qt[:, cb * 128:(cb + nb) * 128], PP[pi][:, 0:nb * 128], AF.Copy),
                 r=[bP[pi]], w=[bQt])
        for kt in range(stiles):
            cc = j * (NTK + 1) + kt
            P.do("dve", lambda e, kt=kt, cc=cc:
                 e.tensor_scalar(gg[:, kt * 128:(kt + 1) * 128], dtt[:, :], c2s[:, cc:cc + 1],
                                 None, ALU.add), r=[bConst, bC2], w=[bGg])
            P.do("dve", lambda e, kt=kt:
                 e.tensor_scalar(tr[:, 0:128], gg[:, kt * 128:(kt + 1) * 128], 0.0, None,
                                 ALU.is_gt), r=[bGg], w=[bTr])
            P.do("dve", lambda e, kt=kt:
                 e.scalar_tensor_tensor(gg[:, kt * 128:(kt + 1) * 128], tr[:, 0:128], -cfg.NEGBIG,
                                        gg[:, kt * 128:(kt + 1) * 128], ALU.mult, ALU.add),
                 r=[bTr, bGg], w=[bGg])
        ngr = (stiles + 3) // 4

        def setup(h):
            sl = h % 2
            d = {}
            d["kv"] = h // G
            d["slope"] = cfg.slopes[h]
            d["sl"] = sl
            if sl == 0:
                d["trA"], d["bTrX"] = tr, bTr
                d["prT"], d["bPrX"] = sc1, bS1
                d["ptT"], d["bPtX"] = sc1, bS1
                d["PROx"], d["PTOx"] = PRO, PTO
                d["C0"] = 8
                d["bSx"] = bSmc
                d["o2"], d["bO2"] = PP[4], bP[4]
            else:
                d["trA"], d["bTrX"] = xb, bXb
                d["prT"], d["bPrX"] = kvst[1], bKv[1]
                d["ptT"], d["bPtX"] = kvst[0], bKv[0]
                d["PROx"], d["PTOx"] = 0, 0
                d["C0"] = 12
                d["bSx"] = bSmcB
                d["o2"], d["bO2"] = PP[7], bP[7]
            return d

        def front(h, d):
            kv, slope, trA, bTrX, C0, bSx = d["kv"], d["slope"], d["trA"], d["bTrX"], d["C0"], d["bSx"]
            for g in range(ngr):
                wgt = min(512, span - g * 512)
                sp = sctr[0] % 4
                sctr[0] += 1
                P.do("pe", MM(PP[sp][:, 0:wgt], qt[:, h * 128:(h + 1) * 128],
                              big[:, kv * S + g * 512:kv * S + g * 512 + wgt], True, True),
                     r=[bQt, bBA], w=[bP[sp]])
                P.do("dve", lambda e, g=g, wgt=wgt, sp=sp, trA=trA:
                     e.tensor_tensor(trA[:, g * 512:g * 512 + wgt], PP[sp][:, 0:wgt],
                                     gg[:, g * 512:g * 512 + wgt], ALU.add),
                     r=[bP[sp], bGg], w=[bTrX])
            P.do("dve", lambda e, trA=trA, C0=C0, span=span:
                 e.reduce_max(smc[:, C0:C0 + 1], trA[:, 0:span], AXX, negate=True),
                 r=[bTrX], w=[bSx])
            P.do("dve", lambda e, slope=slope, C0=C0:
                 e.tensor_scalar(smc[:, C0 + 1:C0 + 2], smc[:, C0:C0 + 1], slope, None, ALU.mult),
                 r=[bSx], w=[bSx])

        def fexp(h, d):
            slope, trA, bTrX, prT, bPrX, PROx, C0, bSx = (d["slope"], d["trA"], d["bTrX"],
                                                          d["prT"], d["bPrX"], d["PROx"],
                                                          d["C0"], d["bSx"])
            P.do("act", lambda e, trA=trA, prT=prT, PROx=PROx, C0=C0, slope=slope, span=span:
                 e.activation(prT[:, PROx:PROx + span], trA[:, 0:span], AF.Exp,
                              bias=smc[:, C0 + 1:C0 + 2], scale=slope,
                              accum_out=smc[:, C0 + 2:C0 + 3]),
                 r=[bTrX, bSx], w=[bPrX, bSx])

        def back(h, d, bspan, bstiles, bngr):
            kv, sl, C0, bSx = d["kv"], d["sl"], d["C0"], d["bSx"]
            prT, bPrX, ptT, bPtX = d["prT"], d["bPrX"], d["ptT"], d["bPtX"]
            PROx, PTOx, o2, bO2 = d["PROx"], d["PTOx"], d["o2"], d["bO2"]
            P.do("dve", lambda e, C0=C0: e.reciprocal(smc[:, C0 + 3:C0 + 4],
                                                      smc[:, C0 + 2:C0 + 3]),
                 r=[bSx], w=[bSx])
            for g in range(bngr):
                nb = min(4, bstiles - g * 4)
                pi = 5 + (g % 2)

                def fn_trp(e, g=g, nb=nb, pi=pi, prT=prT, PROx=PROx):
                    last = None
                    for i in range(nb):
                        kt = g * 4 + i
                        last = e.matmul(PP[pi][:, i * 128:(i + 1) * 128],
                                        prT[:, PROx + kt * 128:PROx + (kt + 1) * 128], ic[:, :],
                                        start=True, stop=True)
                    return last
                P.do("pe", fn_trp, r=[bPrX, bConst], w=[bP[pi]])
                P.do("act", lambda e, g=g, nb=nb, pi=pi, ptT=ptT, PTOx=PTOx:
                     e.activation(ptT[:, PTOx + g * 512:PTOx + g * 512 + nb * 128],
                                  PP[pi][:, 0:nb * 128], AF.Copy), r=[bP[pi]], w=[bPtX])

            def fn_av(e, kv=kv, bstiles=bstiles, ptT=ptT, PTOx=PTOx, o2=o2):
                last = None
                for kt in range(bstiles):
                    last = e.matmul(o2[:, 0:128],
                                    ptT[:, PTOx + kt * 128:PTOx + (kt + 1) * 128],
                                    big[:, BIGA + kt * KF + kv * 128:BIGA + kt * KF + (kv + 1) * 128],
                                    start=(kt == 0), stop=(kt == bstiles - 1))
                return last
            P.do("pe", fn_av, r=[bPtX, bBB], w=[bO2])
            OB = max(DM, QF)
            P.do("dve", lambda e, sl=sl, OB=OB, o2=o2, C0=C0: e.tensor_scalar(
                sc2[:, OB + sl * 128:OB + sl * 128 + 128], o2[:, 0:128], smc[:, C0 + 3:C0 + 4],
                None, ALU.mult), r=[bO2, bSx], w=[bOb[sl]])
            pi2 = 5 + (h % 2)
            P.do("pe", lambda e, pi2=pi2, sl=sl, OB=OB: e.matmul(
                PP[pi2][:, 0:128], sc2[:, OB + sl * 128:OB + sl * 128 + 128], ic[:, :],
                start=True, stop=True), r=[bOb[sl], bConst], w=[bP[pi2]])
            P.do("act", lambda e, pi2=pi2, sl=sl, OB=OB: e.activation(
                sc2[:, OB + 256 + sl * 128:OB + 256 + sl * 128 + 128], PP[pi2][:, 0:128],
                AF.Copy), r=[bP[pi2]], w=[bOt2[sl]])
            P.dma("sp", lambda e, h=h, j=j, sl=sl, OB=OB: e.dma_start(
                d_ot[h, :, j * 128:(j + 1) * 128],
                sc2[:, OB + 256 + sl * 128:OB + 256 + sl * 128 + 128]),
                f"otw{sl}", r=[bOt2[sl]], wa=[bDot])

        prev = None
        for h in range(H):
            d = setup(h)
            front(h, d)
            if prev is not None:
                back(prev[0], prev[1], span, stiles, ngr)
            fexp(h, d)
            prev = (h, d)
        back(prev[0], prev[1], span, stiles, ngr)

    if stop_phase <= 4:
        return _finish()
    for i in range(4):
        Wc = F * TPC // 4
        P.dma("sp", lambda e, i=i, Wc=Wc: e.dma_start(big[:, i * Wc:(i + 1) * Wc],
                                                      d_xnT[:, i * Wc:(i + 1) * Wc]),
              f"xnr{i}", r=[bDxnT], w=[bBA] if i < 2 else [bBB])
    P.dma("sp", lambda e: e.dma_start(clm[:, :], d_clo.ap()), "clr", r=[bDclo], w=[bClm])
    P.do("dve", lambda e: e.tensor_scalar(clsel[:, :], clm[:, :], cfg.THRESH, None, ALU.is_gt),
         r=[bClm], w=[bClsel])
    for gc in range(DCG):
        P.dma("pool", lambda e, gc=gc: e.dma_start(asn_i[:, 0:DW], d_asn[:, gc * DW:(gc + 1) * DW]),
              "asnl", r=[bAsn], w=[bAsn])
        P.do("dve", lambda e: e.tensor_copy(yf[0:1, 0:DW], asn_i[:, 0:DW]), r=[bAsn], w=[bYf])
        P.do("dve", lambda e: e.tensor_copy(rowsb[:, 1024:1024 + DW], yf[0:1, 0:DW]),
             r=[bYf], w=[bRowsb])
        P.do("pe", MM(PP[2][0:NCL, 0:DW], onr[:, 0:NCL], rowsb[:, 1024:1024 + DW], True, True),
             r=[bConst, bRowsb], w=[bP[2]])
        P.do("dve", lambda e, gc=gc: e.tensor_scalar(qt[0:NCL, gc * DW:(gc + 1) * DW],
                                                     PP[2][0:NCL, 0:DW], iotc[:, :], None,
                                                     ALU.is_equal),
             r=[bP[2], bConst], w=[bQt])
        P.do("pe", lambda e, gc=gc: e.matmul(PP[3][0:1, 0:DW], clsel[:, :],
                                             qt[0:NCL, gc * DW:(gc + 1) * DW],
                                             start=True, stop=True),
             r=[bClsel, bQt], w=[bP[3]])
        P.do("dve", lambda e: e.tensor_copy(rowsb[:, 1536:1536 + DW], PP[3][0:1, 0:DW]),
             r=[bP[3]], w=[bRowsb])
        P.do("pe", lambda e: e.matmul(PP[2][:, 0:DW], onr[:, :], rowsb[:, 1536:1536 + DW],
                                      start=True, stop=True),
             r=[bConst, bRowsb], w=[bP[2]])
        P.do("dve", lambda e, gc=gc: e.tensor_copy(rsb[:, gc * DW:(gc + 1) * DW], PP[2][:, 0:DW]),
             r=[bP[2]], w=[bRsb])

    for cg in range(DCG):
        P.dma("sp", lambda e, cg=cg: e.dma_start(
            xa[:, 0:NT * DW].rearrange("p (m n) -> p m n", m=NT),
            d_x[:, cg * DW:(cg + 1) * DW].rearrange("(m p) n -> p m n", p=128)),
            "xs", w=[bXa])
        P.dma("pool", lambda e, cg=cg: e.dma_start(yf[0:1, 0:DW], d_bias[:, cg * DW:(cg + 1) * DW]),
              "bil", r=[bYf], w=[bYf])
        P.do("act", lambda e: e.activation(rowsb[:, 0:DW], yf[0:1, 0:DW], AF.Copy),
             r=[bYf], w=[bRowsb])
        P.do("dve", lambda e: e.tensor_tensor(rowsb[:, 512:512 + DW], yf[0:1, 0:DW],
                                              rowsb[:, 0:DW], ALU.subtract),
             r=[bYf, bRowsb], w=[bRowsb])
        allP = [bP[mm % 8] for mm in range(PCNT)]
        for c0 in range(0, HC, NSL):
            ns = min(NSL, HC - c0)
            t, b = load_slab(d_wo, c0, ns, cg * DW, DW, "wo")
            for i in range(ns):
                c = c0 + i
                stg, sbo = qs[c % 2], bQs[c % 2]
                P.dma("sp", lambda e, stg=stg, c=c: e.dma_start(
                    stg[:, 0:TPC], d_ot[c, :, :]),
                    f"otr{c % 2}", r=[bDot], w=[sbo])

                def fn(e, stg=stg, t=t, i=i, c=c):
                    last = None
                    for m in range(NT):
                        last = e.matmul(PP[m % 8][:, 0:DW], stg[:, m * 128:(m + 1) * 128],
                                        t[:, i * DW:(i + 1) * DW],
                                        start=(c == 0), stop=(c == HC - 1))
                    return last
                P.do("pe", fn, r=[sbo, b], w=allP)
        for m in range(NT):
            P.do("dve", lambda e, m=m: e.tensor_copy(xb[:, m * DW:(m + 1) * DW],
                                                     PP[m % 8][:, 0:DW]),
                 r=[bP[m % 8]], w=[bXb])
        for c0 in range(0, F, NSL):
            t, b = load_slab(d_wf, c0, NSL, cg * DW, DW, "wf")
            for i in range(NSL):
                c = c0 + i

                def fn(e, t=t, i=i, c=c):
                    last = None
                    for m in range(NT):
                        last = e.matmul(PP[m % 8][:, 0:DW],
                                        big[:, c * TPC + m * 128:c * TPC + (m + 1) * 128],
                                        t[:, i * DW:(i + 1) * DW],
                                        start=(c == 0), stop=False)
                    return last
                P.do("pe", fn, r=[bBA, bBB, b], w=allP)

        def fn_bias(e, cg=cg):
            last = None
            for m in range(NT):
                last = e.matmul(PP[m % 8][:, 0:DW], onr[:, :],
                                rowsb[:, 0:DW], start=False, stop=False)
                last = e.matmul(PP[m % 8][:, 0:DW], onr[:, :],
                                rowsb[:, 512:512 + DW], start=False, stop=True)
            return last
        P.do("pe", fn_bias, r=[bConst, bRowsb], w=allP)
        for m in range(NT):
            osl, obf = outst[m % 2], bOut[m % 2]
            P.do("dve", lambda e, m=m, cg=cg:
                 e.scalar_tensor_tensor(yf[:, 0:DW], PP[m % 8][:, 0:DW], qselc[:, m:m + 1],
                                        rsb[:, cg * DW:(cg + 1) * DW], ALU.mult, ALU.mult),
                 r=[bP[m % 8], bQsel, bRsb], w=[bYf])
            P.do("dve", lambda e, m=m:
                 e.tensor_tensor(yf[:, DW:2 * DW], yf[:, 0:DW], xa[:, m * DW:(m + 1) * DW],
                                 ALU.add), r=[bYf, bXa], w=[bYf])
            P.do("dve", lambda e, m=m, osl=osl:
                 e.tensor_tensor(osl[:, :], yf[:, DW:2 * DW], xb[:, m * DW:(m + 1) * DW],
                                 ALU.add), r=[bYf, bXb], w=[obf])
            P.dma("sp", lambda e, m=m, cg=cg, osl=osl:
                  e.dma_start(d_out[m * 128:(m + 1) * 128, cg * DW:(cg + 1) * DW], osl[:, :]),
                  f"outw{m % 2}", r=[obf], wa=[bDout])

    P.wait_only("sp", list(bDout.ws.values()) + ([bDout.w] if bDout.w else []))
    P.emit()
    return nc



def make_inputs(cfg, x, Wq, Wk, Wv, Wo, qw, kw, Wf, fbias, cent, asn):
    B, S, DM, NT = cfg.B, cfg.S, cfg.DM, cfg.NT
    x = np.asarray(x, dtype=np.float32)
    shared = {
        "wqT": np.ascontiguousarray(np.asarray(Wq, dtype=np.float32).T),
        "wkT": np.ascontiguousarray(np.asarray(Wk, dtype=np.float32).T),
        "wvT": np.ascontiguousarray(np.asarray(Wv, dtype=np.float32).T),
        "woT": np.ascontiguousarray(np.asarray(Wo, dtype=np.float32).T),
        "wfT": np.ascontiguousarray(np.asarray(Wf, dtype=np.float32).T),
        "ctT": np.ascontiguousarray(np.asarray(cent, dtype=np.float32).T),
        "fbias": np.asarray(fbias, dtype=np.float32).reshape(1, DM),
        "qnw": np.asarray(qw, dtype=np.float32).reshape(1, cfg.HD),
        "knw": np.asarray(kw, dtype=np.float32).reshape(1, cfg.HD),
        "asn": np.asarray(asn).reshape(1, DM).astype(np.int32),
    }
    xt = x.reshape(B, cfg.ST, 128, DM)
    NTK = 2 * NT
    in_maps = []
    for c in range(8):
        b, par = c // 2, c % 2
        xs = np.ascontiguousarray(xt[b, par::2].reshape(cfg.TPC, DM))
        c2 = np.zeros((128, NT * (NTK + 1)), dtype=np.float32)
        for j in range(NT):
            for kt in range(2 * j + 2):
                c2[:, j * (NTK + 1) + kt] = 128.0 * (kt - (2 * j + par))
        in_maps.append(dict(shared, xt=xs, c2in=c2))
    return in_maps


def assemble(cfg, outs):
    B, S, DM, NT = cfg.B, cfg.S, cfg.DM, cfg.NT
    full = np.zeros((B, S, DM), dtype=np.float32)
    for c in range(8):
        b, par = c // 2, c % 2
        o = np.asarray(outs[c]).reshape(NT, 128, DM)
        for t in range(NT):
            full[b, (2 * t + par) * 128:(2 * t + par + 1) * 128, :] = o[t]
    return full


_FULL = Cfg()


def kernel(L0_attn_x, L0_attn_Wq, L0_attn_Wk, L0_attn_Wv, L0_attn_Wo,
           L0_attn_q_norm_w, L0_attn_k_norm_w,
           L0_ffn_weight, L0_ffn_bias, L0_ffn_centroids, L0_ffn_assignments):
    cfg = _FULL
    nc = build_program(cfg)
    in_maps = make_inputs(cfg, L0_attn_x, L0_attn_Wq, L0_attn_Wk, L0_attn_Wv, L0_attn_Wo,
                          L0_attn_q_norm_w, L0_attn_k_norm_w, L0_ffn_weight, L0_ffn_bias,
                          L0_ffn_centroids, L0_ffn_assignments)
    res = run_bass_kernel_spmd(nc, in_maps, core_ids=list(range(8)))
    return assemble(cfg, [r["out"] for r in res.results])


# version 05
# speedup vs baseline: 1.2815x; 1.0327x over previous
import math
import numpy as np
import ml_dtypes
import concourse.bass as bass
import concourse.mybir as mybir
from concourse.bass_utils import run_bass_kernel_spmd

F32 = mybir.dt.float32
BF16 = mybir.dt.bfloat16
I32 = mybir.dt.int32
AF = mybir.ActivationFunctionType
ALU = mybir.AluOpType
AXX = mybir.AxisListType.X
BF = ml_dtypes.bfloat16


class Cfg:
    def __init__(self, DM=4096, S=2048, H=32, HKV=8, NCL=64):
        self.B = 4
        self.DM = DM
        self.S = S
        self.H = H
        self.HKV = HKV
        self.G = H // HKV
        self.HD = 128
        self.NCL = NCL
        self.F = DM // 128
        self.QF = H * self.HD
        self.KF = HKV * self.HD
        self.NT = S // 256
        self.TPC = self.NT * 128
        self.ST = S // 128
        start = 2 ** (-(2 ** (-(math.log2(H) - 3))))
        self.slopes = [start * start ** i for i in range(H)]
        self.EPS_LN = 1e-5
        self.EPS_RMS = 1e-6
        self.TEMP = 0.1
        self.THRESH = 0.01
        self.NEGBIG = 3.0e32


ENGS = ("pe", "dve", "act", "pool", "sp")


class Buf:
    __slots__ = ("w", "rs", "ws")

    def __init__(self):
        self.w = None
        self.rs = []
        self.ws = {}


class Prog:
    def __init__(self, nc):
        self.nc = nc
        self.ops = {e: [] for e in ENGS}
        self.cnt = {e: 0 for e in ENGS}
        self.esem = {e: nc.alloc_semaphore(f"es_{e}") for e in ENGS}
        self.pe_sem_name = self.esem["pe"].name
        self.wm = {e: {} for e in ENGS}
        self.dsem = {}

    def _waits(self, e, deps):
        best = {}
        for d in deps:
            if d is None:
                continue
            sem, val = d
            k = sem.name
            if e == "pe" and k == self.pe_sem_name:
                continue
            if k not in best or val > best[k][1]:
                best[k] = (sem, val)
        waits = []
        for k, (sem, val) in sorted(best.items()):
            if self.wm[e].get(k, 0) >= val:
                continue
            waits.append((sem, val))
            self.wm[e][k] = val
        return waits

    def _deps(self, r, w, wa, deps):
        out = list(deps)
        for b in r:
            if b.w is not None:
                out.append(b.w)
            out.extend(b.ws.values())
        for b in w:
            if b.w is not None:
                out.append(b.w)
            out.extend(b.ws.values())
            out.extend(b.rs)
        for b in wa:
            pass
        return out

    def _book(self, tok, r, w, wa):
        for b in r:
            b.rs.append(tok)
        for b in w:
            b.w = tok
            b.rs = []
        for b in wa:
            b.ws[tok[0].name] = tok
        return tok

    def do(self, e, fn, r=(), w=(), wa=(), deps=()):
        waits = self._waits(e, self._deps(r, w, wa, deps))
        self.cnt[e] += 1
        tok = (self.esem[e], self.cnt[e])
        self.ops[e].append((waits, fn, self.esem[e], 1))
        return self._book(tok, r, w, wa)

    def dma(self, q, fn, stream, r=(), w=(), wa=(), deps=(), inc=16):
        waits = self._waits(q, self._deps(r, w, wa, deps))
        if stream not in self.dsem:
            self.dsem[stream] = [self.nc.alloc_semaphore("ds_" + stream), 0]
        ent = self.dsem[stream]
        ent[1] += inc
        tok = (ent[0], ent[1])
        self.ops[q].append((waits, fn, ent[0], inc))
        return self._book(tok, r, w, wa)

    def cc(self, fn, stream, r=(), w=(), deps=()):
        return self.dma("pool", fn, stream, r=r, w=w, deps=deps, inc=1)

    def wait_only(self, e, deps):
        waits = self._waits(e, [d for d in deps if d is not None])
        if waits:
            self.ops[e].append((waits, None, None, 0))

    def emit(self):
        nc = self.nc

        def mk(e):
            def body(eng):
                for waits, fn, sem, inc in self.ops[e]:
                    for (s, v) in waits:
                        eng.wait_ge(s, v)
                    if fn is not None:
                        inst = fn(eng)
                        if inst is not None and sem is not None:
                            inst.then_inc(sem, inc)
            return body

        with nc.Block() as block:
            block.sync(mk("sp"))
            block.gpsimd(mk("pool"))
            block.tensor(mk("pe"))
            block.vector(mk("dve"))
            block.scalar(mk("act"))


def build_program(cfg: Cfg, stop_phase=99):
    nc = bass.Bass("TRN2", target_bir_lowering=False)
    DM, S, H, HKV, G, HD = cfg.DM, cfg.S, cfg.H, cfg.HKV, cfg.G, cfg.HD
    F, QF, KF, NT, NCL, ST = cfg.F, cfg.QF, cfg.KF, cfg.NT, cfg.NCL, cfg.ST
    TPC = cfg.TPC
    HC = QF // 128
    KC = KF // 128
    QCG, QW = (QF // 512, 512) if QF >= 512 else (1, QF)
    KCG, KW = (KF // 512, 512) if KF >= 512 else (1, KF)
    DCG, DW = (DM // 512, 512) if DM >= 512 else (1, DM)
    NSL = 8 if F % 8 == 0 else F
    NTK = 2 * NT
    BIGA = HKV * S
    BIGC = max(2 * ST * KF, F * TPC)
    PCNT = min(NT, 8)

    d_x = nc.dram_tensor("xt", [TPC, DM], F32, kind="ExternalInput").ap()
    d_c2 = nc.dram_tensor("c2in", [128, NT * (NTK + 1)], F32, kind="ExternalInput").ap()
    d_wq = nc.dram_tensor("wqT", [DM, QF], F32, kind="ExternalInput").ap()
    d_wk = nc.dram_tensor("wkT", [DM, KF], F32, kind="ExternalInput").ap()
    d_wv = nc.dram_tensor("wvT", [DM, KF], F32, kind="ExternalInput").ap()
    d_wo = nc.dram_tensor("woT", [QF, DM], F32, kind="ExternalInput").ap()
    d_wf = nc.dram_tensor("wfT", [DM, DM], F32, kind="ExternalInput").ap()
    d_ct = nc.dram_tensor("ctT", [DM, NCL], F32, kind="ExternalInput").ap()
    d_bias = nc.dram_tensor("fbias", [1, DM], F32, kind="ExternalInput").ap()
    d_qw = nc.dram_tensor("qnw", [1, HD], F32, kind="ExternalInput").ap()
    d_kw = nc.dram_tensor("knw", [1, HD], F32, kind="ExternalInput").ap()
    d_asn = nc.dram_tensor("asn", [1, DM], I32, kind="ExternalInput").ap()
    d_out = nc.dram_tensor("out", [TPC, DM], F32, kind="ExternalOutput").ap()

    d_xnT = nc.dram_tensor("xnT_s", [128, F * TPC], BF16).ap()
    d_qh = nc.dram_tensor("qh_s", [TPC, QF], BF16).ap()
    d_ot = nc.dram_tensor("ot_s", [HC, 128, NT * 128], BF16).ap()
    d_exi = nc.dram_tensor("exi_s", [TPC, 2 * KF], BF16)
    d_exo = nc.dram_tensor("exo_s", [2 * TPC, 2 * KF], BF16)
    d_cli = nc.dram_tensor("cli_s", [NCL, 1], F32)
    d_clo = nc.dram_tensor("clo_s", [NCL, 1], F32)

    ident_bf = nc.inline_tensor(np.eye(128, dtype=np.float32).astype(BF), "c_ident").ap()
    ident_f32 = nc.inline_tensor(np.eye(128, dtype=np.float32), "c_identf").ap()
    dnp = (np.arange(128)[None, :] - np.arange(128)[:, None]).astype(np.float32)
    d_D = nc.inline_tensor(dnp, "c_D").ap()
    ones_bf = nc.inline_tensor(np.ones((1, 128), dtype=np.float32).astype(BF), "c_ones").ap()
    iota_cl = nc.inline_tensor(np.arange(NCL, dtype=np.float32)[:, None], "c_iota").ap()

    A = nc.alloc_sbuf_tensor
    big = A("bigb", [128, BIGC], BF16)
    xa = A("xa", [128, max(DM, NT * DW)], F32)
    xb = A("xb", [128, max(DM, NT * DW)], F32)
    wsl = [A(f"wsl{i}", [128, NSL * 512], BF16) for i in range(2)]
    qs = [A(f"qs{i}", [128, max(QF, DM, TPC)], BF16) for i in range(2)]
    qt = A("qt", [128, max(QF, DM)], BF16)
    tr = A("tr", [128, max(NTK * 128, F * NCL)], F32)
    sc1 = A("sc1", [128, max(DM, 2 * NTK * 128)], BF16)
    sc2 = A("sc2", [128, max(DM, QF) + 512], BF16)
    gg = A("gg", [128, NTK * 128], F32)
    kvst = [A(f"kvst{i}", [128, max(2 * KF, NTK * 128)], BF16) for i in range(2)]
    stg4 = [A(f"stg{i}", [128, 512], BF16) for i in range(4)]
    rsb = A("rsb", [128, DM], BF16)
    CLF = F * NCL
    assert 2 * CLF <= DM
    outst = [A(f"outst{i}", [128, DW], F32) for i in range(2)]
    yf = A("yf", [128, 2 * DW], F32)
    wbck = A("wbck", [128, 128], BF16)
    smc = A("smc", [128, 16], F32)
    qselc = A("qselc", [128, NT], F32)
    invq = A("invq", [128, 4 * H], F32)
    invk = A("invk", [128, 4 * HKV], F32)
    clm = A("clm", [NCL, 1], F32)
    clsel = A("clsel", [NCL, 1], BF16)
    rowsb = A("rowsb", [1, 4 * 512], BF16)
    qwr = A("qwr", [1, 2 * HD], F32)
    qwrb = A("qwrb", [1, 2 * HD], BF16)
    ic = A("ic", [128, 128], BF16)
    icf = A("icf", [128, 128], F32)
    dtt = A("dtt", [128, 128], F32)
    onr = A("onr", [1, 128], BF16)
    iotc = A("iotc", [NCL, 1], F32)
    c2s = A("c2s", [128, NT * (NTK + 1)], F32)
    asn_i = A("asni", [1, 512], I32)

    PP = [nc.alloc_psum_tensor(f"pp{i}", [128, 512], F32) for i in range(8)]

    P = Prog(nc)
    bBA, bBB = Buf(), Buf()
    bXa, bXb = Buf(), Buf()
    bW = [Buf() for _ in wsl]
    bQs = [Buf(), Buf()]
    bQt = Buf()
    bTr = Buf()
    bS1, bS2 = Buf(), Buf()
    bOb = [Buf(), Buf()]
    bOt2 = [Buf(), Buf()]
    bGg = Buf()
    bKv = [Buf(), Buf()]
    bStg = [Buf() for _ in stg4]
    bRsb = Buf()
    bOut = [Buf(), Buf()]
    bYf = Buf()
    bWbk = Buf()
    bSmc = Buf()
    bQsel = Buf()
    bInvq, bInvk = Buf(), Buf()
    bClm, bClsel = Buf(), Buf()
    bRowsb = Buf()
    bQwr, bQwrb = Buf(), Buf()
    bConst = Buf()
    bC2 = Buf()
    bAsn = Buf()
    bP = [Buf() for _ in PP]
    bDxnT, bDqh, bDot, bDexi, bDexo, bDcli, bDclo, bDout = (Buf() for _ in range(8))

    def xn_buf(c):
        return bBA if c < F // 2 else bBB

    def MM(out_ap, lhsT_ap, rhs_ap, start, stop):
        def fn(e, a=out_ap, b=lhsT_ap, c=rhs_ap, s1=start, s2=stop):
            return e.matmul(a, b, c, start=s1, stop=s2)
        return fn

    P.dma("pool", lambda e: e.dma_start(ic[:, :], ident_bf), "cst1", w=[bConst])
    P.dma("pool", lambda e: e.dma_start(icf[:, :], ident_f32), "cst2", w=[bConst])
    P.dma("pool", lambda e: e.dma_start(dtt[:, :], d_D), "cst3", w=[bConst])
    P.dma("pool", lambda e: e.dma_start(onr[:, :], ones_bf), "cst4", w=[bConst])
    P.dma("pool", lambda e: e.dma_start(iotc[:, :], iota_cl), "cst5", w=[bConst])
    P.dma("pool", lambda e: e.dma_start(c2s[:, :], d_c2), "cst6", w=[bC2])
    P.dma("pool", lambda e: e.dma_start(qwr[:, 0:HD], d_qw), "cst8", w=[bQwr])
    P.dma("pool", lambda e: e.dma_start(qwr[:, HD:2 * HD], d_kw), "cst9", w=[bQwr])
    P.dma("pool", lambda e: e.dma_start(
        tr[:, 0:F * NCL].rearrange("p (c n) -> p c n", c=F),
        d_ct.rearrange("(c p) n -> p c n", p=128)), "cst11", w=[bTr])

    P.do("act", lambda e: e.activation(rsb[:, 0:CLF], tr[:, 0:F * NCL], AF.Copy),
         r=[bTr], w=[bRsb])
    P.do("dve", lambda e: e.tensor_tensor(rsb[:, CLF:2 * CLF], tr[:, 0:F * NCL], rsb[:, 0:CLF],
                                          ALU.subtract), r=[bTr, bRsb], w=[bRsb])
    P.do("dve", lambda e: e.tensor_copy(qwrb[:, HD:2 * HD], qwr[:, HD:2 * HD]),
         r=[bQwr], w=[bQwrb])
    P.do("pe", MM(PP[7][:, 0:128], onr[:, :], qwrb[:, HD:2 * HD], True, True),
         r=[bConst, bQwrb], w=[bP[7]])
    P.do("act", lambda e: e.activation(wbck[:, :], PP[7][:, 0:128], AF.Copy), r=[bP[7]], w=[bWbk])

    def _finish0():
        toks = []
        for name, (sem, val) in P.dsem.items():
            if val:
                toks.append((sem, val))
        P.wait_only("sp", toks)
        P.emit()
        return nc

    if stop_phase <= 0:
        return _finish0()
    NT1 = NT if stop_phase > 90 or stop_phase >= 1.5 else NT
    import os
    SUB = int(os.environ.get("KSUB", "99"))
    for m in range(NT if SUB >= 10 else 1):
        P.dma("sp", lambda e, m=m: e.dma_start(xa[:, 0:DM], d_x[m * 128:(m + 1) * 128, :]),
              "x0", w=[bXa])
        P.do("dve", lambda e: e.reduce_sum(smc[:, 0:1], xa[:, 0:DM], AXX), r=[bXa], w=[bSmc])
        P.do("dve", lambda e: e.tensor_scalar(smc[:, 1:2], smc[:, 0:1], 1.0 / DM, None, ALU.mult),
             r=[bSmc], w=[bSmc])
        P.do("act", lambda e: e.activation(sc2[:, 0:DM], xa[:, 0:DM], AF.Square,
                                           bias=smc[:, 1:2], scale=-1.0, accum_out=smc[:, 2:3]),
             r=[bXa, bSmc], w=[bS2, bSmc])
        P.do("dve", lambda e: e.tensor_scalar(smc[:, 3:4], smc[:, 2:3], 1.0 / DM, cfg.EPS_LN,
                                              ALU.mult, ALU.add), r=[bSmc], w=[bSmc])
        P.do("act", lambda e: e.activation(smc[:, 4:5], smc[:, 3:4], AF.Ln), r=[bSmc], w=[bSmc])
        P.do("act", lambda e: e.activation(smc[:, 5:6], smc[:, 4:5], AF.Exp, scale=-0.5),
             r=[bSmc], w=[bSmc])
        P.do("dve", lambda e: e.tensor_scalar(xb[:, 0:DM], xa[:, 0:DM], smc[:, 1:2], smc[:, 5:6],
                                              ALU.subtract, ALU.mult), r=[bXa, bSmc], w=[bXb])
        P.do("act", lambda e: e.activation(sc2[:, 0:DM], xb[:, 0:DM], AF.Copy), r=[bXb], w=[bS2])
        P.do("dve", lambda e: e.tensor_tensor(sc1[:, 0:DM], xb[:, 0:DM], sc2[:, 0:DM],
                                              ALU.subtract), r=[bXb, bS2], w=[bS1])
        if SUB < 2:
            continue
        for cb in range(0, F, 4):
            nb = min(4, F - cb)
            pih = (cb // 4) % 2
            pil = 2 + (cb // 4) % 2

            def fn_hi(e, cb=cb, nb=nb, pih=pih):
                last = None
                for i in range(nb):
                    last = e.matmul(PP[pih][:, i * 128:(i + 1) * 128],
                                    sc2[:, (cb + i) * 128:(cb + i + 1) * 128], ic[:, :],
                                    start=True, stop=True)
                return last
            P.do("pe", fn_hi, r=[bS2, bConst], w=[bP[pih]])

            def fn_lo(e, cb=cb, nb=nb, pil=pil):
                last = None
                for i in range(nb):
                    last = e.matmul(PP[pil][:, i * 128:(i + 1) * 128],
                                    sc1[:, (cb + i) * 128:(cb + i + 1) * 128], ic[:, :],
                                    start=True, stop=True)
                return last
            P.do("pe", fn_lo, r=[bS1, bConst], w=[bP[pil]])
            for i in range(nb):
                c = cb + i
                P.do("act", lambda e, c=c, i=i, pih=pih, m=m:
                     e.activation(big[:, c * TPC + m * 128:c * TPC + (m + 1) * 128],
                                  PP[pih][:, i * 128:(i + 1) * 128], AF.Copy),
                     r=[bP[pih]], w=[xn_buf(c)])
            P.do("act", lambda e, cb=cb, nb=nb, pil=pil:
                 e.activation(qs[0][:, cb * 128:(cb + nb) * 128], PP[pil][:, 0:nb * 128], AF.Copy),
                 r=[bP[pil]], w=[bQs[0]])

        if SUB < 3:
            continue

        def fn_cent(e, m=m):
            last = None
            for c in range(F):
                hiT = big[:, c * TPC + m * 128:c * TPC + (m + 1) * 128]
                loT = qs[0][:, c * 128:(c + 1) * 128]
                last = e.matmul(PP[7][:, 0:NCL], hiT, rsb[:, c * NCL:(c + 1) * NCL],
                                start=(c == 0), stop=False)
                last = e.matmul(PP[7][:, 0:NCL], hiT, rsb[:, CLF + c * NCL:CLF + (c + 1) * NCL],
                                start=False, stop=False)
                last = e.matmul(PP[7][:, 0:NCL], loT, rsb[:, c * NCL:(c + 1) * NCL],
                                start=False, stop=(c == F - 1))
            return last
        P.do("pe", fn_cent, r=[bBA, bBB, bQs[0], bRsb], w=[bP[7]])
        if SUB < 4:
            continue
        SUB2 = int(os.environ.get("KSUB2", "99"))
        P.do("dve", lambda e: e.reduce_max(smc[:, 6:7], PP[7][:, 0:NCL], AXX, negate=True),
             r=[bP[7]], w=[bSmc])
        if SUB2 < 1:
            continue
        P.do("dve", lambda e: e.tensor_scalar(smc[:, 7:8], smc[:, 6:7], 1.0 / cfg.TEMP, None,
                                              ALU.mult), r=[bSmc], w=[bSmc])
        if SUB2 < 2:
            continue
        EXPV = int(os.environ.get("KEXPV", "0"))
        if EXPV == 1:
            P.do("act", lambda e: e.activation(tr[:, 0:NCL], PP[7][:, 0:NCL], AF.Exp,
                                               bias=smc[:, 7:8], scale=1.0 / cfg.TEMP),
                 r=[bP[7], bSmc], w=[bTr, bSmc])
        elif EXPV == 2:
            P.do("act", lambda e: e.activation(tr[:, 0:NCL], PP[7][:, 0:NCL], AF.Exp,
                                               scale=1.0 / cfg.TEMP, accum_out=smc[:, 8:9]),
                 r=[bP[7], bSmc], w=[bTr, bSmc])
        elif EXPV == 3:
            P.do("dve", lambda e: e.tensor_copy(xb[:, 0:NCL], PP[7][:, 0:NCL]), r=[bP[7]], w=[bXb])
            P.do("act", lambda e: e.activation(tr[:, 0:NCL], xb[:, 0:NCL], AF.Exp,
                                               bias=smc[:, 7:8], scale=1.0 / cfg.TEMP,
                                               accum_out=smc[:, 8:9]),
                 r=[bXb, bSmc], w=[bTr, bSmc])
        elif EXPV == 5:
            P.do("act", lambda e: e.activation(smc[:, 8:9], smc[:, 7:8], AF.Copy),
                 r=[bSmc], w=[bSmc])
        elif EXPV == 6:
            P.do("act", lambda e: e.activation(sc1[:, 0:NCL], PP[7][:, 0:NCL], AF.Exp,
                                               bias=smc[:, 7:8], scale=1.0 / cfg.TEMP,
                                               accum_out=smc[:, 8:9]),
                 r=[bP[7], bSmc], w=[bS1, bSmc])
        elif EXPV == 4:
            P.do("act", lambda e: e.activation(tr[:, 0:NCL], PP[7][:, 0:NCL], AF.Copy,
                                               bias=smc[:, 7:8], scale=1.0 / cfg.TEMP,
                                               accum_out=smc[:, 8:9]),
                 r=[bP[7], bSmc], w=[bTr, bSmc])
        else:
            P.do("act", lambda e: e.activation(tr[:, 0:NCL], PP[7][:, 0:NCL], AF.Exp,
                                               bias=smc[:, 7:8], scale=1.0 / cfg.TEMP,
                                               accum_out=smc[:, 8:9]),
                 r=[bP[7], bSmc], w=[bTr, bSmc])
        if SUB2 < 3:
            continue
        P.do("dve", lambda e, m=m: e.tensor_scalar(qselc[:, m:m + 1], smc[:, 8:9],
                                                   1.0 / cfg.THRESH, None, ALU.is_lt),
             r=[bSmc], w=[bQsel])
        if SUB2 < 4:
            continue
        P.do("dve", lambda e: e.reciprocal(smc[:, 9:10], smc[:, 8:9]), r=[bSmc], w=[bSmc])
        if SUB2 < 5:
            continue
        P.do("dve", lambda e: e.tensor_scalar(tr[:, 0:NCL], tr[:, 0:NCL], smc[:, 9:10], None,
                                              ALU.mult), r=[bSmc], w=[bTr])
        if SUB < 5:
            continue
        P.do("pe", lambda e: e.transpose(PP[6][0:NCL, 0:128], tr[:, 0:NCL], icf[:, :]),
             r=[bTr, bConst], w=[bP[6]])
        P.do("dve", lambda e: e.reduce_max(smc[0:NCL, 10:11], PP[6][0:NCL, 0:128], AXX),
             r=[bP[6]], w=[bSmc])
        if m == 0:
            P.do("dve", lambda e: e.tensor_copy(clm[:, :], smc[0:NCL, 10:11]), r=[bSmc], w=[bClm])
        else:
            P.do("dve", lambda e: e.tensor_tensor(clm[:, :], clm[:, :], smc[0:NCL, 10:11],
                                                  ALU.max), r=[bSmc, bClm], w=[bClm])

    for i in range(4):
        Wc = F * TPC // 4
        P.dma("sp", lambda e, i=i, Wc=Wc: e.dma_start(d_xnT[:, i * Wc:(i + 1) * Wc],
                                                      big[:, i * Wc:(i + 1) * Wc]),
              f"xnw{i}", r=[bBA, bBB], wa=[bDxnT])
    P.dma("sp", lambda e: e.dma_start(d_cli.ap(), clm[:, :]), "clw", r=[bClm], wa=[bDcli])

    slab_ctr = [0]

    def load_slab(dram_ap, c0, nchunks, col0, width, stream):
        si = slab_ctr[0] % len(wsl)
        slab_ctr[0] += 1
        t, b = wsl[si], bW[si]
        stream = f"wsl{si}"
        src = dram_ap[c0 * 128:(c0 + nchunks) * 128, col0:col0 + width]
        P.dma("pool", lambda e, t=t, src=src, nchunks=nchunks, width=width:
              e.dma_start(t[:, 0:nchunks * width].rearrange("p (c n) -> p c n", c=nchunks),
                          src.rearrange("(c p) n -> p c n", p=128)),
              stream, w=[b])
        return t, b

    def proj_phase(dram_w, ncg, cw, nchunks_total, out_cb, which, after_first_slab=None):
        for cg in range(ncg):
            for c0 in range(0, F, NSL):
                t, b = load_slab(dram_w, c0, NSL, cg * cw, cw, "w" + which)
                if after_first_slab is not None:
                    after_first_slab()
                    after_first_slab = None
                for m in range(NT):
                    def fn(e, c0=c0, t=t, m=m, cw=cw):
                        last = None
                        for i in range(NSL):
                            c = c0 + i
                            last = e.matmul(PP[m % 8][:, 0:cw],
                                            big[:, c * TPC + m * 128:c * TPC + (m + 1) * 128],
                                            t[:, i * cw:(i + 1) * cw],
                                            start=(c == 0), stop=(c == F - 1))
                        return last
                    P.do("pe", fn, r=[bBA, bBB, b], w=[bP[m % 8]])
            for m in range(NT):
                out_cb(m, cg, PP[m % 8], bP[m % 8])

    def _finish():
        toks = []
        for name, (sem, val) in P.dsem.items():
            if val:
                toks.append((sem, val))
        P.wait_only("sp", toks)
        P.emit()
        return nc

    if stop_phase <= 1:
        return _finish()
    def kv_evac(kind):
        def cb(m, cg, psum, pbuf):
            nh = KW // 128
            if kind == "v":
                for hb in range(nh):
                    si = (m * nh + hb) % 4
                    kvh = cg * nh + hb
                    P.do("dve", lambda e, si=si, hb=hb, psum=psum:
                         e.tensor_copy(stg4[si][:, 0:128], psum[:, hb * 128:(hb + 1) * 128]),
                         r=[pbuf], w=[bStg[si]])
                    P.dma("sp", lambda e, si=si, m=m, kvh=kvh:
                          e.dma_start(d_exi.ap()[m * 128:(m + 1) * 128, KF + kvh * 128:KF + (kvh + 1) * 128],
                                      stg4[si][:, 0:128]),
                          f"stg{si}", r=[bStg[si]], wa=[bDexi])
            else:
                for hb in range(nh):
                    kvh = cg * nh + hb
                    P.do("act", lambda e, hb=hb, kvh=kvh, psum=psum:
                         e.activation(sc1[:, 0:128], psum[:, hb * 128:(hb + 1) * 128], AF.Square,
                                      accum_out=invk[:, kvh:kvh + 1]),
                         r=[pbuf], w=[bS1, bInvk])
                h0 = cg * nh
                P.do("dve", lambda e, h0=h0, nh=nh:
                     e.tensor_scalar(invk[:, HKV + h0:HKV + h0 + nh], invk[:, h0:h0 + nh],
                                     1.0 / HD, cfg.EPS_RMS, ALU.mult, ALU.add),
                     r=[bInvk], w=[bInvk])
                P.do("act", lambda e, h0=h0, nh=nh:
                     e.activation(invk[:, 2 * HKV + h0:2 * HKV + h0 + nh],
                                  invk[:, HKV + h0:HKV + h0 + nh], AF.Ln), r=[bInvk], w=[bInvk])
                P.do("act", lambda e, h0=h0, nh=nh:
                     e.activation(invk[:, 3 * HKV + h0:3 * HKV + h0 + nh],
                                  invk[:, 2 * HKV + h0:2 * HKV + h0 + nh], AF.Exp, scale=-0.5),
                     r=[bInvk], w=[bInvk])
                for hb in range(nh):
                    si = (m * nh + hb) % 4
                    kvh = cg * nh + hb
                    P.do("dve", lambda e, si=si, hb=hb, kvh=kvh, psum=psum:
                         e.scalar_tensor_tensor(stg4[si][:, 0:128], psum[:, hb * 128:(hb + 1) * 128],
                                                invk[:, 3 * HKV + kvh:3 * HKV + kvh + 1],
                                                wbck[:, :], ALU.mult, ALU.mult),
                         r=[pbuf, bInvk, bWbk], w=[bStg[si]])
                    P.dma("sp", lambda e, si=si, m=m, kvh=kvh:
                          e.dma_start(d_exi.ap()[m * 128:(m + 1) * 128, kvh * 128:(kvh + 1) * 128],
                                      stg4[si][:, 0:128]),
                          f"stg{si}", r=[bStg[si]], wa=[bDexi])
        return cb

    proj_phase(d_wk, KCG, KW, KC, kv_evac("k"), "k")
    proj_phase(d_wv, KCG, KW, KC, kv_evac("v"), "v")

    if stop_phase <= 2:
        return _finish()
    for h in range(H):
        sc = 1.0 / (math.sqrt(HD) * cfg.slopes[h])
        P.do("dve", lambda e, sc=sc: e.tensor_scalar(qwrb[:, 0:HD], qwr[:, 0:HD], sc, None,
                                                     ALU.mult), r=[bQwr], w=[bQwrb])
        pi = 4 + (h % 2)
        P.do("pe", MM(PP[pi][:, 0:128], onr[:, :], qwrb[:, 0:HD], True, True),
             r=[bConst, bQwrb], w=[bP[pi]])
        P.do("act", lambda e, h=h, pi=pi: e.activation(sc2[:, h * 128:(h + 1) * 128],
                                                       PP[pi][:, 0:128], AF.Copy),
             r=[bP[pi]], w=[bS2])

    def q_evac(m, cg, psum, pbuf):
        nh = QW // 128
        for hb in range(nh):
            h = cg * nh + hb
            P.do("act", lambda e, hb=hb, h=h, psum=psum:
                 e.activation(sc1[:, 0:128], psum[:, hb * 128:(hb + 1) * 128], AF.Square,
                              accum_out=invq[:, h:h + 1]), r=[pbuf], w=[bS1, bInvq])
        h0 = cg * nh
        P.do("dve", lambda e, h0=h0, nh=nh:
             e.tensor_scalar(invq[:, H + h0:H + h0 + nh], invq[:, h0:h0 + nh], 1.0 / HD,
                             cfg.EPS_RMS, ALU.mult, ALU.add), r=[bInvq], w=[bInvq])
        P.do("act", lambda e, h0=h0, nh=nh:
             e.activation(invq[:, 2 * H + h0:2 * H + h0 + nh], invq[:, H + h0:H + h0 + nh], AF.Ln),
             r=[bInvq], w=[bInvq])
        P.do("act", lambda e, h0=h0, nh=nh:
             e.activation(invq[:, 3 * H + h0:3 * H + h0 + nh], invq[:, 2 * H + h0:2 * H + h0 + nh],
                          AF.Exp, scale=-0.5), r=[bInvq], w=[bInvq])
        for hb in range(nh):
            si = (m * nh + hb) % 4
            h = cg * nh + hb
            P.do("dve", lambda e, si=si, hb=hb, h=h, psum=psum:
                 e.scalar_tensor_tensor(stg4[si][:, 0:128], psum[:, hb * 128:(hb + 1) * 128],
                                        invq[:, 3 * H + h:3 * H + h + 1],
                                        sc2[:, h * 128:(h + 1) * 128], ALU.mult, ALU.mult),
                 r=[pbuf, bInvq, bS2], w=[bStg[si]])
            P.dma("sp", lambda e, si=si, m=m, h=h:
                  e.dma_start(d_qh[m * 128:(m + 1) * 128, h * 128:(h + 1) * 128], stg4[si][:, 0:128]),
                  f"stg{si}", r=[bStg[si]], wa=[bDqh])

    def fn_ar(e):
        return e.collective_compute("AllReduce", ALU.max,
                                    replica_groups=[[0, 1, 2, 3, 4, 5, 6, 7]],
                                    ins=[d_cli.ap().opt()], outs=[d_clo.ap().opt()])

    def fn_ag(e):
        return e.collective_compute("AllGather", ALU.bypass,
                                    replica_groups=[[0, 1], [2, 3], [4, 5], [6, 7]],
                                    ins=[d_exi.ap().opt()], outs=[d_exo.ap().opt()])

    def emit_ccs():
        P.cc(fn_ag, "ccag", r=[bDexi], w=[bDexo])
        P.cc(fn_ar, "ccar", r=[bDcli], w=[bDclo])

    proj_phase(d_wq, QCG, QW, HC, q_evac, "q", after_first_slab=emit_ccs)

    if stop_phase <= 3:
        return _finish()
    for t in range(ST):
        rb = (t % 2) * TPC + (t // 2) * 128
        st, sb = kvst[t % 2], bKv[t % 2]
        P.dma("sp", lambda e, st=st, rb=rb: e.dma_start(st[:, 0:KF],
                                                        d_exo.ap()[rb:rb + 128, 0:KF]),
              f"exr{t % 2}", r=[bDexo], w=[sb])
        P.dma("sp", lambda e, t=t, rb=rb: e.dma_start(big[:, BIGA + t * KF:BIGA + (t + 1) * KF],
                                                      d_exo.ap()[rb:rb + 128, KF:2 * KF]),
              "exrv", r=[bDexo], w=[bBB])
        for kb in range(0, KC, 4):
            nb = min(4, KC - kb)
            pi = (kb // 4) % 2

            def fn_tr(e, st=st, kb=kb, nb=nb, pi=pi):
                last = None
                for i in range(nb):
                    last = e.matmul(PP[pi][:, i * 128:(i + 1) * 128],
                                    st[:, (kb + i) * 128:(kb + i + 1) * 128], ic[:, :],
                                    start=True, stop=True)
                return last
            P.do("pe", fn_tr, r=[sb, bConst], w=[bP[pi]])
            for i in range(nb):
                kv = kb + i
                P.do("act", lambda e, kv=kv, t=t, pi=pi, i=i:
                     e.activation(big[:, kv * S + t * 128:kv * S + (t + 1) * 128],
                                  PP[pi][:, i * 128:(i + 1) * 128], AF.Copy),
                     r=[bP[pi]], w=[bBA])

    if stop_phase <= 35:
        return _finish()
    PTO = 0
    PRO = NTK * 128
    bSmcB = Buf()
    sctr = [0]
    for j in range(NT):
        stiles = 2 * j + 2
        span = stiles * 128
        stg, sbq = qs[j % 2], bQs[j % 2]
        P.dma("sp", lambda e, stg=stg, j=j: e.dma_start(stg[:, 0:QF], d_qh[j * 128:(j + 1) * 128, :]),
              f"qhr{j % 2}", r=[bDqh], w=[sbq])
        for cb in range(0, HC, 4):
            nb = min(4, HC - cb)
            pi = 5 + (cb // 4) % 2

            def fn_trq(e, stg=stg, cb=cb, nb=nb, pi=pi):
                last = None
                for i in range(nb):
                    last = e.matmul(PP[pi][:, i * 128:(i + 1) * 128],
                                    stg[:, (cb + i) * 128:(cb + i + 1) * 128], ic[:, :],
                                    start=True, stop=True)
                return last
            P.do("pe", fn_trq, r=[sbq, bConst], w=[bP[pi]])
            P.do("act", lambda e, cb=cb, nb=nb, pi=pi:
                 e.activation(qt[:, cb * 128:(cb + nb) * 128], PP[pi][:, 0:nb * 128], AF.Copy),
                 r=[bP[pi]], w=[bQt])
        for kt in range(stiles):
            cc = j * (NTK + 1) + kt
            P.do("dve", lambda e, kt=kt, cc=cc:
                 e.tensor_scalar(gg[:, kt * 128:(kt + 1) * 128], dtt[:, :], c2s[:, cc:cc + 1],
                                 None, ALU.add), r=[bConst, bC2], w=[bGg])
            P.do("dve", lambda e, kt=kt:
                 e.tensor_scalar(tr[:, 0:128], gg[:, kt * 128:(kt + 1) * 128], 0.0, None,
                                 ALU.is_gt), r=[bGg], w=[bTr])
            P.do("dve", lambda e, kt=kt:
                 e.scalar_tensor_tensor(gg[:, kt * 128:(kt + 1) * 128], tr[:, 0:128], -cfg.NEGBIG,
                                        gg[:, kt * 128:(kt + 1) * 128], ALU.mult, ALU.add),
                 r=[bTr, bGg], w=[bGg])
        ngr = (stiles + 3) // 4

        def setup(h):
            sl = h % 2
            d = {}
            d["kv"] = h // G
            d["slope"] = cfg.slopes[h]
            d["sl"] = sl
            if sl == 0:
                d["trA"], d["bTrX"] = tr, bTr
                d["prT"], d["bPrX"] = sc1, bS1
                d["ptT"], d["bPtX"] = sc1, bS1
                d["PROx"], d["PTOx"] = PRO, PTO
                d["C0"] = 8
                d["bSx"] = bSmc
                d["o2"], d["bO2"] = PP[4], bP[4]
            else:
                d["trA"], d["bTrX"] = xb, bXb
                d["prT"], d["bPrX"] = kvst[1], bKv[1]
                d["ptT"], d["bPtX"] = kvst[0], bKv[0]
                d["PROx"], d["PTOx"] = 0, 0
                d["C0"] = 12
                d["bSx"] = bSmcB
                d["o2"], d["bO2"] = PP[7], bP[7]
            return d

        def front(h, d):
            kv, slope, trA, bTrX, C0, bSx = d["kv"], d["slope"], d["trA"], d["bTrX"], d["C0"], d["bSx"]
            for g in range(ngr):
                wgt = min(512, span - g * 512)
                sp = sctr[0] % 4
                sctr[0] += 1
                P.do("pe", MM(PP[sp][:, 0:wgt], qt[:, h * 128:(h + 1) * 128],
                              big[:, kv * S + g * 512:kv * S + g * 512 + wgt], True, True),
                     r=[bQt, bBA], w=[bP[sp]])
                P.do("dve", lambda e, g=g, wgt=wgt, sp=sp, trA=trA:
                     e.tensor_tensor(trA[:, g * 512:g * 512 + wgt], PP[sp][:, 0:wgt],
                                     gg[:, g * 512:g * 512 + wgt], ALU.add),
                     r=[bP[sp], bGg], w=[bTrX])
            P.do("dve", lambda e, trA=trA, C0=C0, span=span:
                 e.reduce_max(smc[:, C0:C0 + 1], trA[:, 0:span], AXX, negate=True),
                 r=[bTrX], w=[bSx])
            P.do("dve", lambda e, slope=slope, C0=C0:
                 e.tensor_scalar(smc[:, C0 + 1:C0 + 2], smc[:, C0:C0 + 1], slope, None, ALU.mult),
                 r=[bSx], w=[bSx])

        def fexp(h, d):
            slope, trA, bTrX, prT, bPrX, PROx, C0, bSx = (d["slope"], d["trA"], d["bTrX"],
                                                          d["prT"], d["bPrX"], d["PROx"],
                                                          d["C0"], d["bSx"])
            P.do("act", lambda e, trA=trA, prT=prT, PROx=PROx, C0=C0, slope=slope, span=span:
                 e.activation(prT[:, PROx:PROx + span], trA[:, 0:span], AF.Exp,
                              bias=smc[:, C0 + 1:C0 + 2], scale=slope,
                              accum_out=smc[:, C0 + 2:C0 + 3]),
                 r=[bTrX, bSx], w=[bPrX, bSx])

        def back(h, d, bspan, bstiles, bngr):
            kv, sl, C0, bSx = d["kv"], d["sl"], d["C0"], d["bSx"]
            prT, bPrX, ptT, bPtX = d["prT"], d["bPrX"], d["ptT"], d["bPtX"]
            PROx, PTOx, o2, bO2 = d["PROx"], d["PTOx"], d["o2"], d["bO2"]
            P.do("dve", lambda e, C0=C0: e.reciprocal(smc[:, C0 + 3:C0 + 4],
                                                      smc[:, C0 + 2:C0 + 3]),
                 r=[bSx], w=[bSx])
            for g in range(bngr):
                nb = min(4, bstiles - g * 4)
                pi = 5 + (g % 2)

                def fn_trp(e, g=g, nb=nb, pi=pi, prT=prT, PROx=PROx):
                    last = None
                    for i in range(nb):
                        kt = g * 4 + i
                        last = e.matmul(PP[pi][:, i * 128:(i + 1) * 128],
                                        prT[:, PROx + kt * 128:PROx + (kt + 1) * 128], ic[:, :],
                                        start=True, stop=True)
                    return last
                P.do("pe", fn_trp, r=[bPrX, bConst], w=[bP[pi]])
                P.do("act", lambda e, g=g, nb=nb, pi=pi, ptT=ptT, PTOx=PTOx:
                     e.activation(ptT[:, PTOx + g * 512:PTOx + g * 512 + nb * 128],
                                  PP[pi][:, 0:nb * 128], AF.Copy), r=[bP[pi]], w=[bPtX])

            def fn_av(e, kv=kv, bstiles=bstiles, ptT=ptT, PTOx=PTOx, o2=o2):
                last = None
                for kt in range(bstiles):
                    last = e.matmul(o2[:, 0:128],
                                    ptT[:, PTOx + kt * 128:PTOx + (kt + 1) * 128],
                                    big[:, BIGA + kt * KF + kv * 128:BIGA + kt * KF + (kv + 1) * 128],
                                    start=(kt == 0), stop=(kt == bstiles - 1))
                return last
            P.do("pe", fn_av, r=[bPtX, bBB], w=[bO2])
            OB = max(DM, QF)
            P.do("dve", lambda e, sl=sl, OB=OB, o2=o2, C0=C0: e.tensor_scalar(
                sc2[:, OB + sl * 128:OB + sl * 128 + 128], o2[:, 0:128], smc[:, C0 + 3:C0 + 4],
                None, ALU.mult), r=[bO2, bSx], w=[bOb[sl]])
            pi2 = 5 + (h % 2)
            P.do("pe", lambda e, pi2=pi2, sl=sl, OB=OB: e.matmul(
                PP[pi2][:, 0:128], sc2[:, OB + sl * 128:OB + sl * 128 + 128], ic[:, :],
                start=True, stop=True), r=[bOb[sl], bConst], w=[bP[pi2]])
            P.do("act", lambda e, pi2=pi2, sl=sl, OB=OB: e.activation(
                sc2[:, OB + 256 + sl * 128:OB + 256 + sl * 128 + 128], PP[pi2][:, 0:128],
                AF.Copy), r=[bP[pi2]], w=[bOt2[sl]])
            P.dma("sp", lambda e, h=h, j=j, sl=sl, OB=OB: e.dma_start(
                d_ot[h, :, j * 128:(j + 1) * 128],
                sc2[:, OB + 256 + sl * 128:OB + 256 + sl * 128 + 128]),
                f"otw{sl}", r=[bOt2[sl]], wa=[bDot])

        pq = []
        for h in range(H):
            d = setup(h)
            front(h, d)
            if len(pq) >= 2:
                ph, pd = pq.pop(0)
                back(ph, pd, span, stiles, ngr)
            fexp(h, d)
            pq.append((h, d))
        for (ph, pd) in pq:
            back(ph, pd, span, stiles, ngr)

    if stop_phase <= 4:
        return _finish()
    for i in range(4):
        Wc = F * TPC // 4
        P.dma("sp", lambda e, i=i, Wc=Wc: e.dma_start(big[:, i * Wc:(i + 1) * Wc],
                                                      d_xnT[:, i * Wc:(i + 1) * Wc]),
              f"xnr{i}", r=[bDxnT], w=[bBA] if i < 2 else [bBB])
    P.dma("sp", lambda e: e.dma_start(clm[:, :], d_clo.ap()), "clr", r=[bDclo], w=[bClm])
    P.do("dve", lambda e: e.tensor_scalar(clsel[:, :], clm[:, :], cfg.THRESH, None, ALU.is_gt),
         r=[bClm], w=[bClsel])
    for gc in range(DCG):
        P.dma("pool", lambda e, gc=gc: e.dma_start(asn_i[:, 0:DW], d_asn[:, gc * DW:(gc + 1) * DW]),
              "asnl", r=[bAsn], w=[bAsn])
        P.do("dve", lambda e: e.tensor_copy(yf[0:1, 0:DW], asn_i[:, 0:DW]), r=[bAsn], w=[bYf])
        P.do("dve", lambda e: e.tensor_copy(rowsb[:, 1024:1024 + DW], yf[0:1, 0:DW]),
             r=[bYf], w=[bRowsb])
        P.do("pe", MM(PP[2][0:NCL, 0:DW], onr[:, 0:NCL], rowsb[:, 1024:1024 + DW], True, True),
             r=[bConst, bRowsb], w=[bP[2]])
        P.do("dve", lambda e, gc=gc: e.tensor_scalar(qt[0:NCL, gc * DW:(gc + 1) * DW],
                                                     PP[2][0:NCL, 0:DW], iotc[:, :], None,
                                                     ALU.is_equal),
             r=[bP[2], bConst], w=[bQt])
        P.do("pe", lambda e, gc=gc: e.matmul(PP[3][0:1, 0:DW], clsel[:, :],
                                             qt[0:NCL, gc * DW:(gc + 1) * DW],
                                             start=True, stop=True),
             r=[bClsel, bQt], w=[bP[3]])
        P.do("dve", lambda e: e.tensor_copy(rowsb[:, 1536:1536 + DW], PP[3][0:1, 0:DW]),
             r=[bP[3]], w=[bRowsb])
        P.do("pe", lambda e: e.matmul(PP[2][:, 0:DW], onr[:, :], rowsb[:, 1536:1536 + DW],
                                      start=True, stop=True),
             r=[bConst, bRowsb], w=[bP[2]])
        P.do("dve", lambda e, gc=gc: e.tensor_copy(rsb[:, gc * DW:(gc + 1) * DW], PP[2][:, 0:DW]),
             r=[bP[2]], w=[bRsb])

    for cg in range(DCG):
        P.dma("sp", lambda e, cg=cg: e.dma_start(
            xa[:, 0:NT * DW].rearrange("p (m n) -> p m n", m=NT),
            d_x[:, cg * DW:(cg + 1) * DW].rearrange("(m p) n -> p m n", p=128)),
            "xs", w=[bXa])
        P.dma("pool", lambda e, cg=cg: e.dma_start(yf[0:1, 0:DW], d_bias[:, cg * DW:(cg + 1) * DW]),
              "bil", r=[bYf], w=[bYf])
        P.do("act", lambda e: e.activation(rowsb[:, 0:DW], yf[0:1, 0:DW], AF.Copy),
             r=[bYf], w=[bRowsb])
        P.do("dve", lambda e: e.tensor_tensor(rowsb[:, 512:512 + DW], yf[0:1, 0:DW],
                                              rowsb[:, 0:DW], ALU.subtract),
             r=[bYf, bRowsb], w=[bRowsb])
        allP = [bP[mm % 8] for mm in range(PCNT)]
        for c0 in range(0, HC, NSL):
            ns = min(NSL, HC - c0)
            t, b = load_slab(d_wo, c0, ns, cg * DW, DW, "wo")
            for i in range(ns):
                c = c0 + i
                stg, sbo = qs[c % 2], bQs[c % 2]
                P.dma("sp", lambda e, stg=stg, c=c: e.dma_start(
                    stg[:, 0:TPC], d_ot[c, :, :]),
                    f"otr{c % 2}", r=[bDot], w=[sbo])

                def fn(e, stg=stg, t=t, i=i, c=c):
                    last = None
                    for m in range(NT):
                        last = e.matmul(PP[m % 8][:, 0:DW], stg[:, m * 128:(m + 1) * 128],
                                        t[:, i * DW:(i + 1) * DW],
                                        start=(c == 0), stop=(c == HC - 1))
                    return last
                P.do("pe", fn, r=[sbo, b], w=allP)
        for m in range(NT):
            P.do("dve", lambda e, m=m: e.tensor_copy(xb[:, m * DW:(m + 1) * DW],
                                                     PP[m % 8][:, 0:DW]),
                 r=[bP[m % 8]], w=[bXb])
        for c0 in range(0, F, NSL):
            t, b = load_slab(d_wf, c0, NSL, cg * DW, DW, "wf")
            for i in range(NSL):
                c = c0 + i

                def fn(e, t=t, i=i, c=c):
                    last = None
                    for m in range(NT):
                        last = e.matmul(PP[m % 8][:, 0:DW],
                                        big[:, c * TPC + m * 128:c * TPC + (m + 1) * 128],
                                        t[:, i * DW:(i + 1) * DW],
                                        start=(c == 0), stop=False)
                    return last
                P.do("pe", fn, r=[bBA, bBB, b], w=allP)

        def fn_bias(e, cg=cg):
            last = None
            for m in range(NT):
                last = e.matmul(PP[m % 8][:, 0:DW], onr[:, :],
                                rowsb[:, 0:DW], start=False, stop=False)
                last = e.matmul(PP[m % 8][:, 0:DW], onr[:, :],
                                rowsb[:, 512:512 + DW], start=False, stop=True)
            return last
        P.do("pe", fn_bias, r=[bConst, bRowsb], w=allP)
        for m in range(NT):
            osl, obf = outst[m % 2], bOut[m % 2]
            P.do("dve", lambda e, m=m, cg=cg:
                 e.scalar_tensor_tensor(yf[:, 0:DW], PP[m % 8][:, 0:DW], qselc[:, m:m + 1],
                                        rsb[:, cg * DW:(cg + 1) * DW], ALU.mult, ALU.mult),
                 r=[bP[m % 8], bQsel, bRsb], w=[bYf])
            P.do("dve", lambda e, m=m:
                 e.tensor_tensor(yf[:, DW:2 * DW], yf[:, 0:DW], xa[:, m * DW:(m + 1) * DW],
                                 ALU.add), r=[bYf, bXa], w=[bYf])
            P.do("dve", lambda e, m=m, osl=osl:
                 e.tensor_tensor(osl[:, :], yf[:, DW:2 * DW], xb[:, m * DW:(m + 1) * DW],
                                 ALU.add), r=[bYf, bXb], w=[obf])
            P.dma("sp", lambda e, m=m, cg=cg, osl=osl:
                  e.dma_start(d_out[m * 128:(m + 1) * 128, cg * DW:(cg + 1) * DW], osl[:, :]),
                  f"outw{m % 2}", r=[obf], wa=[bDout])

    P.wait_only("sp", list(bDout.ws.values()) + ([bDout.w] if bDout.w else []))
    P.emit()
    return nc



def make_inputs(cfg, x, Wq, Wk, Wv, Wo, qw, kw, Wf, fbias, cent, asn):
    B, S, DM, NT = cfg.B, cfg.S, cfg.DM, cfg.NT
    x = np.asarray(x, dtype=np.float32)
    shared = {
        "wqT": np.ascontiguousarray(np.asarray(Wq, dtype=np.float32).T),
        "wkT": np.ascontiguousarray(np.asarray(Wk, dtype=np.float32).T),
        "wvT": np.ascontiguousarray(np.asarray(Wv, dtype=np.float32).T),
        "woT": np.ascontiguousarray(np.asarray(Wo, dtype=np.float32).T),
        "wfT": np.ascontiguousarray(np.asarray(Wf, dtype=np.float32).T),
        "ctT": np.ascontiguousarray(np.asarray(cent, dtype=np.float32).T),
        "fbias": np.asarray(fbias, dtype=np.float32).reshape(1, DM),
        "qnw": np.asarray(qw, dtype=np.float32).reshape(1, cfg.HD),
        "knw": np.asarray(kw, dtype=np.float32).reshape(1, cfg.HD),
        "asn": np.asarray(asn).reshape(1, DM).astype(np.int32),
    }
    xt = x.reshape(B, cfg.ST, 128, DM)
    NTK = 2 * NT
    in_maps = []
    for c in range(8):
        b, par = c // 2, c % 2
        xs = np.ascontiguousarray(xt[b, par::2].reshape(cfg.TPC, DM))
        c2 = np.zeros((128, NT * (NTK + 1)), dtype=np.float32)
        for j in range(NT):
            for kt in range(2 * j + 2):
                c2[:, j * (NTK + 1) + kt] = 128.0 * (kt - (2 * j + par))
        in_maps.append(dict(shared, xt=xs, c2in=c2))
    return in_maps


def assemble(cfg, outs):
    B, S, DM, NT = cfg.B, cfg.S, cfg.DM, cfg.NT
    full = np.zeros((B, S, DM), dtype=np.float32)
    for c in range(8):
        b, par = c // 2, c % 2
        o = np.asarray(outs[c]).reshape(NT, 128, DM)
        for t in range(NT):
            full[b, (2 * t + par) * 128:(2 * t + par + 1) * 128, :] = o[t]
    return full


_FULL = Cfg()


def kernel(L0_attn_x, L0_attn_Wq, L0_attn_Wk, L0_attn_Wv, L0_attn_Wo,
           L0_attn_q_norm_w, L0_attn_k_norm_w,
           L0_ffn_weight, L0_ffn_bias, L0_ffn_centroids, L0_ffn_assignments):
    cfg = _FULL
    nc = build_program(cfg)
    in_maps = make_inputs(cfg, L0_attn_x, L0_attn_Wq, L0_attn_Wk, L0_attn_Wv, L0_attn_Wo,
                          L0_attn_q_norm_w, L0_attn_k_norm_w, L0_ffn_weight, L0_ffn_bias,
                          L0_ffn_centroids, L0_ffn_assignments)
    res = run_bass_kernel_spmd(nc, in_maps, core_ids=list(range(8)))
    return assemble(cfg, [r["out"] for r in res.results])
